# version 10
# baseline (speedup 1.0000x reference)
import math
import numpy as np
import ml_dtypes
from contextlib import ExitStack

import concourse.bass as bass
import concourse.mybir as mybir
from concourse.bass_utils import run_bass_kernel_spmd

F32 = mybir.dt.float32
BF16 = mybir.dt.bfloat16
F16 = mybir.dt.float16
ALU = mybir.AluOpType
AF = mybir.ActivationFunctionType

BF = ml_dtypes.bfloat16

T = 2048
D = 1024
DI = 2048
RNE_C = 12582912.0

def _pwl_constants():
    xs = [0.25 * k for k in range(17)]
    ev = [0.5 * x * (1.0 + math.erf(x / math.sqrt(2.0))) - x / 2.0 for x in xs]
    s = [(ev[k + 1] - ev[k]) / 0.25 for k in range(16)]
    knots = []
    for k in range(1, 16):
        knots.append((64.0 * k, s[k] - s[k - 1]))
    knots.append((1024.0, 0.5 - s[15]))
    j256 = 256.0 * (4.0 - (0.5 * 4.0 * (1.0 + math.erf(4.0 / math.sqrt(2.0)))))
    return s[0], knots, j256


PWL_S0, PWL_KNOTS, PWL_J256 = _pwl_constants()


class Ctx:
    ENGINES = ("sync", "vector", "scalar", "tensor", "gpsimd")

    def __init__(self, nc, stack):
        self.nc = nc
        self.stack = stack
        self.ops = {e: [] for e in self.ENGINES}
        self.sems = {}
        self.cnt = {}
        self.bump = 16512
        self.phase_tag = "g"
        self.psA = None
        self.psB = None

    def _bind(self, name, srckey):
        if name in self.sems:
            if srckey is not None:
                prev = self.srck.get(name)
                if prev is None:
                    self.srck[name] = srckey
                else:
                    assert prev == srckey or name == "q", (
                        f"sem {name} fired from {prev} and {srckey}")
            return self.sems[name]
        if not hasattr(self, "pools"):
            self.pools, self.htot, self.hbase, self.srck = {}, {}, {}, {}
            self.suid = 0
        pool = self.pools.setdefault(srckey, []) if srckey is not None else None
        if pool and name != "q":
            h = pool.pop()
        else:
            self.suid += 1
            h = self.stack.enter_context(
                self.nc.semaphore(f"s{self.suid:03d}_{name}"))
            self.htot[id(h)] = 0
        self.sems[name] = h
        self.hbase[name] = self.htot[id(h)]
        self.srck[name] = srckey
        return h

    def fire(self, name, amount):
        self.cnt[name] = self.cnt.get(name, 0) + amount
        return (name, amount)

    def val(self, name):
        return self.cnt.get(name, 0)

    def emit(self, engine, fn, waits=(), fire=None):
        wl = []
        for (n, v) in waits:
            if v <= 0:
                continue
            assert self.cnt.get(n, 0) >= v or n == "q", (
                f"forward wait {n} >= {v} (cnt {self.cnt.get(n, 0)})")
            h = self._bind(n, None)
            wl.append((h, self.hbase[n] + v))
        fr = None
        if fire is not None:
            name, amount = fire
            srckey = ("dma", engine) if amount >= 16 else ("eng", engine)
            h = self._bind(name, srckey)
            self.htot[id(h)] += amount
            fr = (h, amount)

        def runner(eng):
            for h, v in wl:
                eng.wait_ge(h, v)
            ins = fn(eng)
            if fr is not None:
                ins.then_inc(fr[0], fr[1])
        self.ops[engine].append(runner)

    def dma(self, queue, out, in_, waits=(), sem="dma", transpose=False):
        f = self.fire(sem, 16)
        if transpose:
            self.emit(queue, lambda e: e.dma_start_transpose(out=out, in_=in_),
                      waits, f)
        else:
            self.emit(queue, lambda e: e.dma_start(out=out, in_=in_), waits, f)
        return self.cnt[sem]

    SBUF_BASE = 16512
    SBUF_LIMIT = 229344

    def phase(self, tag):
        self.phase_tag = tag
        self.bump = self.SBUF_BASE
        for n in [n for n in self.sems if n != "q"]:
            h = self.sems.pop(n)
            self.cnt.pop(n, None)
            sk = self.srck.pop(n, None)
            self.hbase.pop(n, None)
            if sk is not None:
                self.pools.setdefault(sk, []).append(h)

    def sbuf(self, name, shape, dt):
        per = 1
        for s in shape[1:]:
            per *= s
        per *= {F32: 4, BF16: 2, F16: 2}[dt]
        off = (self.bump + 31) & ~31
        self.bump = off + per
        assert self.bump <= self.SBUF_LIMIT, (
            f"sbuf overflow in phase {self.phase_tag}: {self.bump}")
        h = self.nc.alloc_sbuf_tensor_at(
            f"{self.phase_tag}_{name}", list(shape), dt, offset=off)
        return h

    def psum(self, name, shape, dt=F32):
        return self.stack.enter_context(self.nc.psum_tensor(name, shape, dt))

    def finalize(self, final_waits):
        ops = self.ops
        wl = [(self.sems[n], self.hbase.get(n, 0) + v) for (n, v) in final_waits]
        with self.nc.Block() as block:
            @block.sync
            def _(eng):
                for f in ops["sync"]:
                    f(eng)
                for h, v in wl:
                    eng.wait_ge(h, v)

            @block.vector
            def _(eng):
                for f in ops["vector"]:
                    f(eng)

            @block.scalar
            def _(eng):
                for f in ops["scalar"]:
                    f(eng)

            @block.tensor
            def _(eng):
                for f in ops["tensor"]:
                    f(eng)

            @block.gpsimd
            def _(eng):
                for f in ops["gpsimd"]:
                    f(eng)


def new_prog():
    nc = bass.Bass("TRN2", target_bir_lowering=False)
    stack = ExitStack()
    ctx = Ctx(nc, stack)
    return nc, stack, ctx


def din(nc, name, shape, dt=F32):
    return nc.dram_tensor(name, list(shape), dt, kind="ExternalInput").ap()


def dout(nc, name, shape, dt=F32):
    return nc.dram_tensor(name, list(shape), dt, kind="ExternalOutput").ap()



def _seq(ctx, engine, fn):
    v = ctx.val("q")
    ctx.emit(engine, fn, waits=(("q", v),), fire=ctx.fire("q", 1))


def _seqdma(ctx, queue, out, in_, transpose=False, slow=False):
    v = ctx.val("q")
    f = ctx.fire("q", 16)
    if transpose:
        ctx.emit(queue, lambda e: e.dma_start_transpose(out=out, in_=in_),
                 waits=(("q", v),), fire=f)
    elif slow:
        ctx.emit(queue, lambda e: e.dma_start(out=out, in_=in_,
                                              allow_slow_non_contiguous=True),
                 waits=(("q", v),), fire=f)
    else:
        ctx.emit(queue, lambda e: e.dma_start(out=out, in_=in_),
                 waits=(("q", v),), fire=f)


class Seq:

    def __init__(self, ctx):
        self.ctx = ctx

    def op(self, engine, fn):
        _seq(self.ctx, engine, fn)

    def dma(self, out, in_, queue="sync", transpose=False, slow=False):
        _seqdma(self.ctx, queue, out, in_, transpose, slow)

    def opw(self, engine, fn, extra):
        ctx = self.ctx
        v = ctx.val("q")
        ctx.emit(engine, fn, waits=tuple([("q", v)] + list(extra)),
                 fire=ctx.fire("q", 1))

    def done(self):
        self.ctx.finalize([("q", self.ctx.val("q"))])


class ConstCols:

    def __init__(self, S, values):
        self.tile = S.ctx.sbuf("cstc", [128, max(1, len(values))], F32)
        self.idx = {}
        for i, v in enumerate(values):
            self.idx[v] = i
            S.op("vector", lambda e, i=i, v=v: e.memset(self.tile[:, i:i + 1], v))

    def ap(self, v):
        i = self.idx[v]
        return self.tile[:, i:i + 1]


CASTS = [
    ("win", "win_bf", D, 4096),
    ("wout", "wout_bf", DI, D),
    ("w1", "w1_bf", D, 4096),
    ("w2", "w2_bf", 4096, D),
    ("wxp", "wxp_bf", DI, 192),
    ("f1t", "f1t_bf", D, 4096),
    ("f2t", "f2t_bf", D, 4096),
    ("h1t", "h1t_bf", 4096, 4096),
    ("h2t", "h2t_bf", 4096, 4096),
    ("w3t", "w3t_bf", 4096, D),
]


def prep_io(nc, mk):
    io = {}
    for (iname, oname, rows, cols) in CASTS:
        io[iname] = mk(iname, (rows, cols), F32, 0)
        io[oname] = mk(oname, (rows, cols), BF16, 1)
    return io


def emit_prep(nc, ctx, S, io):
    tin = ctx.sbuf("tin", [128, 2, 4096], F32)
    tout = ctx.sbuf("tout", [128, 2, 4096], BF16)
    dsl = DStream(ctx, "pr_L")
    dss = DStream(ctx, "pr_S")
    V = "pr_v"
    tiles = []
    for (iname, oname, rows, cols) in CASTS:
        for i in range(rows // 128):
            tiles.append((io[iname], io[oname], i, cols))
    q0 = ctx.val("q")
    for t, (srcp, dstp, i, cols) in enumerate(tiles):
        p = t % 2
        lt = dsl.emit("sync", p, (lambda srcp=srcp, i=i, cols=cols, p=p:
                                  lambda e: e.dma_start(
            out=tin[:, p, :cols], in_=srcp[i * 128:(i + 1) * 128, :]))(),
            waits=tuple([("q", q0)] if t < 2 else [(V, t - 1)]))
        ctx.emit("vector", (lambda cols=cols, p=p: lambda e: e.tensor_copy(
            out=tout[:, p, :cols], in_=tin[:, p, :cols]))(),
            waits=tuple([lt] + ([dss.tok(p)] if t >= 2 else [])),
            fire=ctx.fire(V, 1))
        dss.emit("gpsimd", p, (lambda dstp=dstp, i=i, cols=cols, p=p:
                               lambda e: e.dma_start(
            out=dstp[i * 128:(i + 1) * 128, :], in_=tout[:, p, :cols]))(),
            waits=((V, t + 1),))
    ctx.emit("vector", lambda e: e.engine_nop(),
             waits=(dss.tok(0), dss.tok(1)), fire=ctx.fire("q", 1))


def build_prep():
    nc, stack, ctx = new_prog()
    with stack:
        S = Seq(ctx)
        io = prep_io(nc, lambda n, s, d, o: (dout if o else din)(nc, n, s, d))
        ctx.phase("prep")
        emit_prep(nc, ctx, S, io)
        S.done()
    return nc



LXA_IO = ["x", "win_bf", "convw", "convb", "wxp_bf", "wdt", "bdt",
          "xn_bf", "xi_bf", "z_bf", "u_f", "u_bf", "B0", "C0", "delta", "du"]


def lxa_io(nc, layer, mk):
    NST = 64 if layer == 0 else 16
    MX = 192 if layer == 0 else 96
    fl = F32 if layer == 0 else BF16
    io = {}
    io["x"] = mk("x", (D, T), F32, 0)
    io["win_bf"] = mk("win_bf", (D, 4096), BF16, 0)
    io["convw"] = mk("convw", (DI, 4), fl, 0)
    io["convb"] = mk("convb", (DI,), fl, 0)
    io["wxp_bf"] = mk("wxp_bf", (DI, MX), BF16, 0)
    io["wdt"] = mk("wdt", (64, DI), fl, 0)
    io["bdt"] = mk("bdt", (DI,), fl, 0)
    if layer == 0:
        io["lnw"] = mk("lnw", (D,), F32, 0)
        io["lnb"] = mk("lnb", (D,), F32, 0)
        io["xn_f"] = mk("xn_f", (D, T), F32, 1)
    io["xn_bf"] = mk("xn_bf", (D, T), BF16, 1)
    io["xi_bf"] = mk("xi_bf", (DI, T), BF16, 1)
    io["z_bf"] = mk("z_bf", (DI, T), BF16, 1)
    io["u_f"] = mk("u_f", (DI, T), F32, 1)
    io["u_bf"] = mk("u_bf", (DI, T), BF16, 1)
    io["B0"] = mk("B0", (NST, T), F32, 1)
    io["C0"] = mk("C0", (NST, T), F32, 1)
    io["delta"] = mk("delta", (DI, T), F32, 1)
    io["du"] = mk("du", (DI, T), F32, 1)
    io["du_f16"] = mk("du_f16", (DI, T), F16, 1)
    return io


def emit_lxa(nc, ctx, S, io, layer):
    KD = D // 128
    H = T // 2
    NST = 64 if layer == 0 else 16
    MX = 192 if layer == 0 else 96

    x = io["x"]
    win = io["win_bf"]
    convw = io["convw"]
    convb = io["convb"]
    wxp = io["wxp_bf"]
    wdt = io["wdt"]
    bdt = io["bdt"]
    if layer == 0:
        lnw = io["lnw"]
        lnb = io["lnb"]
        xn_f = io["xn_f"]
    xn_bf = io["xn_bf"]
    xi_bf = io["xi_bf"]
    z_bf = io["z_bf"]
    u_f = io["u_f"]
    u_bf = io["u_bf"]
    B0 = io["B0"]
    C0 = io["C0"]
    delta = io["delta"]
    du = io["du"]
    du_f16 = io["du_f16"]

    if True:
        xt = ctx.sbuf("xt", [128, H], F32)
        sq = ctx.sbuf("sq", [128, H], F32)
        ones_col = ctx.sbuf("ones_col", [128, 1], F32)
        ones_row = ctx.sbuf("ones_row", [1, 128], F32)
        rows = ctx.sbuf("rows", [1, 4, T], F32)
        rep_a = ctx.sbuf("rep_a", [128, T], F32)
        rep_b = ctx.sbuf("rep_b", [128, T], F32)
        xbf = ctx.sbuf("xbf", [128, H], BF16)
        xn2 = ctx.sbuf("xn2", [128, KD, T], BF16)
        wm = ctx.sbuf("wm", [128, KD, 128], BF16)
        ebf = ctx.sbuf("ebf", [128, H], BF16)
        ct = ctx.sbuf("ct", [128, T + 3], BF16)
        cwr = ctx.sbuf("cwr", [128, 16, 4], F32 if layer == 0 else BF16)
        cbr = ctx.sbuf("cbr", [128, 16], F32 if layer == 0 else BF16)
        cw = ctx.sbuf("cw", [128, 16, 4], F32)
        cb = ctx.sbuf("cb", [128, 16], F32)
        ua = ctx.sbuf("ua", [128, T], F32)
        ub = ctx.sbuf("ub", [128, T], BF16)
        wxpt = ctx.sbuf("wxpt", [128, 16, MX], BF16)
        uh = ctx.sbuf("uh", [128, H], BF16)
        dtr = ctx.sbuf("dtr", [64, T], BF16)
        wdtf = ctx.sbuf("wdtf", [64, DI], F32 if layer == 0 else BF16)
        wdtb = ctx.sbuf("wdtb", [64, DI], BF16)
        bdtr = ctx.sbuf("bdtr", [128, 16], F32 if layer == 0 else BF16)
        bdtc = ctx.sbuf("bdtc", [128, 16], F32)
        tb = ctx.sbuf("tb", [128, 4, H], F32)
        duh = ctx.sbuf("duh", [128, H], F16)
        lncol = ctx.sbuf("lncol", [128, 2, KD], F32)
        ps_big = ctx.psA
        ps_sm = ctx.psB[0:64]

        MU, SQS, RSTD, SCR = range(4)
        psb2 = ps_big[:, 0:2, :].rearrange("p a f -> p (a f)")

        S.op("vector", lambda e: e.memset(ones_col[:], 1.0))
        S.op("vector", lambda e: e.memset(ones_row[:], 1.0))
        cst = ConstCols(S, [1e-5])

        def stats_pass(src_ap, xn_mode):
            for th in range(2):
                for k in range(KD):
                    S.dma(xt[:], src_ap[k * 128:(k + 1) * 128, th * H:(th + 1) * H])
                    if xn_mode:
                        S.op("vector", lambda e, th=th: e.tensor_sub(
                            out=xt[:], in0=xt[:], in1=rep_a[:, th * H:(th + 1) * H]))
                        S.op("vector", lambda e, th=th: e.tensor_mul(
                            out=xt[:], in0=xt[:], in1=rep_b[:, th * H:(th + 1) * H]))
                        if layer == 0:
                            S.dma(xn_f[k * 128:(k + 1) * 128, th * H:(th + 1) * H],
                                  xt[:], queue="gpsimd")
                        S.op("vector", lambda e: e.tensor_copy(out=xbf[:], in_=xt[:]))
                        S.dma(xn_bf[k * 128:(k + 1) * 128, th * H:(th + 1) * H],
                              xbf[:], queue="gpsimd")
                    if (not xn_mode) or layer == 0:
                        S.op("scalar", lambda e: e.activation(
                            out=sq[:], in_=xt[:], func=AF.Square))
                        for which in range(2):
                            st = xt if which == 0 else sq
                            for fc in range(2):
                                S.op("tensor", lambda e, w=which, fc=fc, k=k, st=st:
                                     e.matmul(out=ps_sm[0:1, w * 2 + fc, :],
                                              lhsT=ones_col[:],
                                              rhs=st[:, fc * 512:(fc + 1) * 512],
                                              start=(k == 0), stop=(k == KD - 1)))
                if (not xn_mode) or layer == 0:
                    for which, slot in ((0, MU), (1, SQS)):
                        for fc in range(2):
                            S.op("scalar", lambda e, w=which, s=slot, fc=fc, th=th:
                                 e.activation(
                                     out=rows[:, s, th * H + fc * 512:
                                              th * H + (fc + 1) * 512],
                                     in_=ps_sm[0:1, w * 2 + fc, :],
                                     func=AF.Identity, scale=1.0 / D))

        def row_math():
            S.op("vector", lambda e: e.tensor_mul(
                out=rows[:, SCR, :], in0=rows[:, MU, :], in1=rows[:, MU, :]))
            S.op("vector", lambda e: e.tensor_sub(
                out=rows[:, SCR, :], in0=rows[:, SQS, :], in1=rows[:, SCR, :]))
            S.op("scalar", lambda e: e.activation(
                out=rows[:, SCR, :], in_=rows[:, SCR, :], func=AF.Sqrt,
                bias=cst.ap(1e-5)[0:1, :]))
            S.op("vector", lambda e: e.reciprocal(
                out=rows[:, RSTD, :], in_=rows[:, SCR, :]))

        def rep_pass():
            for slot, dst in ((MU, rep_a), (RSTD, rep_b)):
                for fc in range(4):
                    S.op("tensor", lambda e, s=slot, fc=fc: e.matmul(
                        out=ps_big[:, fc, :], lhsT=ones_row[:],
                        rhs=rows[:, s, fc * 512:(fc + 1) * 512],
                        start=True, stop=True))
                S.op("vector", lambda e, d=dst: e.tensor_copy(
                    out=d[:], in_=ps_big[:].rearrange("p a f -> p (a f)")))

        stats_pass(x, False)
        row_math()
        rep_pass()
        stats_pass(x, True)
        if layer == 0:
            S.dma(lncol[:, 0, :], lnw.rearrange("(k p) -> p k", p=128), slow=True)
            S.dma(lncol[:, 1, :], lnb.rearrange("(k p) -> p k", p=128), slow=True)
            row_math()
            rep_pass()
            for th in range(2):
                for k in range(KD):
                    S.dma(xt[:], xn_f[k * 128:(k + 1) * 128, th * H:(th + 1) * H])
                    S.op("vector", lambda e, th=th: e.tensor_sub(
                        out=xt[:], in0=xt[:], in1=rep_a[:, th * H:(th + 1) * H]))
                    S.op("vector", lambda e, th=th: e.tensor_mul(
                        out=xt[:], in0=xt[:], in1=rep_b[:, th * H:(th + 1) * H]))
                    S.op("scalar", lambda e, k=k, th=th: e.activation(
                        out=xn2[:, k, th * H:(th + 1) * H], in_=xt[:],
                        func=AF.Identity, scale=lncol[:, 0, k:k + 1],
                        bias=lncol[:, 1, k:k + 1]))
        else:
            for k in range(KD):
                S.dma(xn2[:, k, :], xn_bf[k * 128:(k + 1) * 128, :])

        for m in range(32):
            S.dma(wm[:], win.rearrange("(k p) m -> p k m", p=128)[:, :, m * 128:(m + 1) * 128])
            for th in range(2):
                for k in range(KD):
                    for j in range(2):
                        S.op("tensor", lambda e, k=k, th=th, j=j: e.matmul(
                            out=ps_big[:, j, :], lhsT=wm[:, k, :],
                            rhs=xn2[:, k, th * H + j * 512:th * H + (j + 1) * 512],
                            start=(k == 0), stop=(k == KD - 1)))
                S.op("vector", lambda e: e.tensor_copy(out=ebf[:], in_=psb2))
                dst = xi_bf if m < 16 else z_bf
                mm2 = m if m < 16 else m - 16
                S.dma(dst[mm2 * 128:(mm2 + 1) * 128, th * H:(th + 1) * H],
                      ebf[:], queue="gpsimd")

        S.dma(cwr[:], convw.rearrange("(g p) k -> p g k", p=128), slow=True)
        S.dma(cbr[:], convb.rearrange("(g p) -> p g", p=128), slow=True)
        S.op("vector", lambda e: e.tensor_copy(out=cw[:], in_=cwr[:]))
        S.op("vector", lambda e: e.tensor_copy(out=cb[:], in_=cbr[:]))
        S.op("vector", lambda e: e.memset(ct[:, 0:3], 0.0))
        for g in range(16):
            S.dma(ct[:, 3:T + 3], xi_bf[g * 128:(g + 1) * 128, :])
            S.op("vector", lambda e, g=g: e.tensor_scalar(
                out=ua[:], in0=ct[:, 0:T], scalar1=cw[:, g, 0:1],
                scalar2=None, op0=ALU.mult))
            for kk in range(1, 4):
                S.op("vector", lambda e, g=g, kk=kk: e.scalar_tensor_tensor(
                    out=ua[:], in0=ct[:, kk:T + kk], scalar=cw[:, g, kk:kk + 1],
                    in1=ua[:], op0=ALU.mult, op1=ALU.add))
            S.op("scalar", lambda e, g=g: e.activation(
                out=ua[:], in_=ua[:], func=AF.Silu, bias=cb[:, g:g + 1]))
            S.op("vector", lambda e: e.tensor_copy(out=ub[:], in_=ua[:]))
            if layer == 1:
                S.op("vector", lambda e: e.tensor_copy(out=ua[:], in_=ub[:]))
            S.dma(u_f[g * 128:(g + 1) * 128, :], ua[:], queue="gpsimd")
            S.dma(u_bf[g * 128:(g + 1) * 128, :], ub[:], queue="gpsimd")

        S.dma(wxpt[:], wxp.rearrange("(k p) m -> p k m", p=128))
        psc2 = ps_sm[:, 0:2, :].rearrange("p a f -> p (a f)")
        psc3 = ps_sm[:, 2:4, :].rearrange("p a f -> p (a f)")
        for th in range(2):
            for k in range(16):
                S.dma(uh[:], u_bf[k * 128:(k + 1) * 128, th * H:(th + 1) * H])
                for j in range(2):
                    uhj = uh[:, j * 512:(j + 1) * 512]
                    if MX == 192:
                        S.op("tensor", lambda e, k=k, j=j, u2=uhj: e.matmul(
                            out=ps_big[:, j, :], lhsT=wxpt[:, k, 0:128],
                            rhs=u2, start=(k == 0), stop=(k == 15)))
                        S.op("tensor", lambda e, k=k, j=j, u2=uhj: e.matmul(
                            out=ps_sm[:, j, :], lhsT=wxpt[:, k, 128:192], rhs=u2,
                            start=(k == 0), stop=(k == 15)))
                    else:
                        S.op("tensor", lambda e, k=k, j=j, u2=uhj: e.matmul(
                            out=ps_big[0:64, j, :], lhsT=wxpt[:, k, 0:64],
                            rhs=u2, start=(k == 0), stop=(k == 15)))
                        S.op("tensor", lambda e, k=k, j=j, u2=uhj: e.matmul(
                            out=ps_sm[0:16, j, :], lhsT=wxpt[:, k, 64:80], rhs=u2,
                            start=(k == 0), stop=(k == 15)))
                        S.op("tensor", lambda e, k=k, j=j, u2=uhj: e.matmul(
                            out=ps_sm[0:16, 2 + j, :], lhsT=wxpt[:, k, 80:96], rhs=u2,
                            start=(k == 0), stop=(k == 15)))
            S.op("vector", lambda e, th=th: e.tensor_copy(
                out=dtr[:, th * H:(th + 1) * H],
                in_=ps_big[0:64, 0:2, :].rearrange("p a f -> p (a f)")))
            if MX == 192:
                S.op("vector", lambda e: e.tensor_copy(
                    out=tb[64:128, 0, :], in_=psb2[64:128, :]))
                S.op("vector", lambda e: e.tensor_copy(
                    out=tb[0:64, 1, :], in_=psc2[0:64, :]))
                S.dma(B0[:, th * H:(th + 1) * H], tb[64:128, 0, :], queue="gpsimd")
                S.dma(C0[:, th * H:(th + 1) * H], tb[0:64, 1, :], queue="gpsimd")
            else:
                S.op("vector", lambda e: e.tensor_copy(
                    out=tb[0:16, 0, :], in_=psc2[0:16, :]))
                S.op("vector", lambda e: e.tensor_copy(
                    out=tb[0:16, 1, :], in_=psc3[0:16, :]))
                S.dma(B0[:, th * H:(th + 1) * H], tb[0:16, 0, :], queue="gpsimd")
                S.dma(C0[:, th * H:(th + 1) * H], tb[0:16, 1, :], queue="gpsimd")

        S.dma(wdtf[:], wdt)
        S.op("vector", lambda e: e.tensor_copy(out=wdtb[:], in_=wdtf[:]))
        S.dma(bdtr[:], bdt.rearrange("(g p) -> p g", p=128), slow=True)
        S.op("vector", lambda e: e.tensor_copy(out=bdtc[:], in_=bdtr[:]))
        for m in range(16):
            for th in range(2):
                for j in range(2):
                    S.op("tensor", lambda e, m=m, th=th, j=j: e.matmul(
                        out=ps_big[:, j, :], lhsT=wdtb[:, m * 128:(m + 1) * 128],
                        rhs=dtr[:, th * H + j * 512:th * H + (j + 1) * 512],
                        start=True, stop=True))
                S.op("vector", lambda e, m=m: e.tensor_scalar(
                    out=tb[:, 0, :], in0=psb2, scalar1=bdtc[:, m:m + 1],
                    scalar2=0.0, op0=ALU.add, op1=ALU.max))
                S.op("scalar", lambda e, m=m: e.activation(
                    out=tb[:, 1, :], in_=psb2, func=AF.Abs, bias=bdtc[:, m:m + 1]))
                S.op("scalar", lambda e: e.activation(
                    out=tb[:, 2, :], in_=tb[:, 1, :], func=AF.Exp, scale=-1.0))
                S.op("scalar", lambda e: e.activation(
                    out=tb[:, 1, :], in_=tb[:, 2, :], func=AF.Ln, bias=1.0))
                S.op("vector", lambda e: e.tensor_add(
                    out=tb[:, 2, :], in0=tb[:, 0, :], in1=tb[:, 1, :]))
                S.dma(tb[:, 3, :], u_f[m * 128:(m + 1) * 128, th * H:(th + 1) * H])
                S.op("vector", lambda e: e.tensor_mul(
                    out=tb[:, 0, :], in0=tb[:, 2, :], in1=tb[:, 3, :]))
                S.op("vector", lambda e: e.tensor_copy(out=duh[:], in_=tb[:, 0, :]))
                S.dma(delta[m * 128:(m + 1) * 128, th * H:(th + 1) * H],
                      tb[:, 2, :], queue="gpsimd")
                S.dma(du[m * 128:(m + 1) * 128, th * H:(th + 1) * H],
                      tb[:, 0, :], queue="gpsimd")
                S.dma(du_f16[m * 128:(m + 1) * 128, th * H:(th + 1) * H],
                      duh[:], queue="gpsimd")


def build_lxa(layer, stop=None):
    nc, stack, ctx = new_prog()
    S = Seq(ctx)
    with stack:
        io = lxa_io(nc, layer, lambda n, s, d, o: (dout if o else din)(nc, n, s, d))
        ctx.psA = ctx.psum("psA", [128, 4, 512], F32)
        ctx.psB = ctx.psum("psB", [128, 4, 512], F32)
        ctx.phase("lxa")
        emit_lxa2(nc, ctx, S, io, layer, stop=stop)
        S.done()
    return nc



def scan_io(nc, layer, mk):
    NST = 64 if layer == 0 else 16
    a_dt = F32 if layer == 0 else BF16
    io = {}
    io["delta"] = mk("delta", (DI, T), F32, 0)
    io["du"] = mk("du", (DI, T), F32, 0)
    io["du_f16"] = mk("du_f16", (DI, T), F16, 0)
    io["Bm"] = mk("Bm", (NST, T), F32, 0)
    io["Cm"] = mk("Cm", (NST, T), F32, 0)
    io["alog"] = mk("alog", (DI, NST), a_dt, 0)
    io["y"] = mk("y", (DI, T), F32, 1)
    return io


def emit_scan(nc, ctx, S, io, layer, dg0=0, dg1=16):
    NST = 64 if layer == 0 else 16
    a_dt = F32 if layer == 0 else BF16

    delta = io["delta"]
    du = io["du"]
    Bm = io["Bm"]
    Cm = io["Cm"]
    alog = io["alog"]
    y = io["y"]

    if True:
        dl = ctx.sbuf("dl", [128, T], F32)
        dul = ctx.sbuf("dul", [128, T], F32)
        yac = ctx.sbuf("yac", [128, T], F32)
        dat = ctx.sbuf("dat", [128, T], F32)
        dbut = ctx.sbuf("dbut", [128, T], F32)
        ht = ctx.sbuf("ht", [128, T], F32)
        hct = ctx.sbuf("hct", [128, T], F32)
        brow = ctx.sbuf("brow", [1, T], F32)
        crow = ctx.sbuf("crow", [1, T], F32)
        alr = ctx.sbuf("alr", [128, NST], a_dt)
        asb = ctx.sbuf("asb", [128, NST], F32)
        ones_row = ctx.sbuf("ones_row", [1, 128], F32)
        ps_b = ctx.psA
        ps_c = ctx.psB
        psbv = ps_b[:].rearrange("p a f -> p (a f)")
        pscv = ps_c[:].rearrange("p a f -> p (a f)")

        S.op("vector", lambda e: e.memset(ones_row[:], 1.0))
        for dg in range(dg0, dg1):
            S.dma(dl[:], delta[dg * 128:(dg + 1) * 128, :])
            S.dma(dul[:], du[dg * 128:(dg + 1) * 128, :])
            S.dma(alr[:], alog[dg * 128:(dg + 1) * 128, :])
            S.op("scalar", lambda e: e.activation(out=asb[:], in_=alr[:], func=AF.Exp))
            S.op("vector", lambda e: e.tensor_single_scalar(
                out=asb[:], in_=asb[:], scalar=-1.0, op=ALU.mult))
            S.op("vector", lambda e: e.memset(yac[:], 0.0))
            for n in range(NST):
                S.dma(brow[:], Bm[n:n + 1, :])
                S.dma(crow[:], Cm[n:n + 1, :])
                for fc in range(4):
                    S.op("tensor", lambda e, fc=fc: e.matmul(
                        out=ps_b[:, fc, :], lhsT=ones_row[:],
                        rhs=brow[:, fc * 512:(fc + 1) * 512], start=True, stop=True))
                for fc in range(4):
                    S.op("tensor", lambda e, fc=fc: e.matmul(
                        out=ps_c[:, fc, :], lhsT=ones_row[:],
                        rhs=crow[:, fc * 512:(fc + 1) * 512], start=True, stop=True))
                S.op("scalar", lambda e, n=n: e.activation(
                    out=dat[:], in_=dl[:], func=AF.Exp, scale=asb[:, n:n + 1]))
                S.op("vector", lambda e: e.tensor_mul(out=dbut[:], in0=dul[:], in1=psbv))
                S.op("vector", lambda e: e.tensor_tensor_scan(
                    out=ht[:], data0=dat[:], data1=dbut[:], initial=0.0,
                    op0=ALU.mult, op1=ALU.add))
                S.op("vector", lambda e: e.tensor_mul(out=hct[:], in0=ht[:], in1=pscv))
                S.op("vector", lambda e: e.tensor_add(out=yac[:], in0=yac[:], in1=hct[:]))
            S.dma(y[dg * 128:(dg + 1) * 128, :], yac[:], queue="gpsimd")


def emit_scan3(nc, ctx, S, io, layer):
    NST = 64 if layer == 0 else 16
    a_dt = F32 if layer == 0 else BF16
    G = 4
    NG = 16 // G
    tg = "sn_"
    Wb, Wc = tg + "wb", tg + "wc"
    P, AE, AX, V, Y, PP = (tg + "pe", tg + "ae", tg + "ax", tg + "dve",
                           tg + "yad", tg + "pp")

    delta = io["delta"]
    duf16 = io["du_f16"]
    Bm = io["Bm"]
    Cm = io["Cm"]
    alog = io["alog"]
    y = io["y"]

    def vb(e):
        return (e // 2) * 5 + (3 if e % 2 else 0)

    dl = ctx.sbuf("dl", [128, G, T], F32)
    duf = ctx.sbuf("duf", [128, G, T], F16)
    alr = ctx.sbuf("alr", [128, 16, NST], a_dt)
    asb = ctx.sbuf("asb", [128, 16, NST], F32)
    brow = ctx.sbuf("brow", [1, 2, T], F32)
    crow = ctx.sbuf("crow", [1, 2, T], F32)
    dat = ctx.sbuf("dat", [128, 2, T], F16)
    besb = ctx.sbuf("besb", [128, 2, T], F16)
    cesb = ctx.sbuf("cesb", [128, 2, T], F16)
    dbut = ctx.sbuf("dbut", [128, 2, T], F16)
    ht = ctx.sbuf("ht", [128, 2, T], F16)
    hct = ctx.sbuf("hct", [128, 2, T], F16)
    yac = ctx.sbuf("yac", [128, G, T], F32)
    ones_row = ctx.sbuf("ones_row", [1, 128], F32)
    psB, psC = ctx.psA, ctx.psB

    S.op("vector", lambda e: e.memset(ones_row[:], 1.0))
    S.dma(alr[:], alog.rearrange("(g p) n -> p g n", p=128))
    S.op("scalar", lambda e: e.activation(
        out=asb[:].rearrange("p g n -> p (g n)"),
        in_=alr[:].rearrange("p g n -> p (g n)"), func=AF.Exp))
    S.op("vector", lambda e: e.tensor_single_scalar(
        out=asb[:].rearrange("p g n -> p (g n)"),
        in_=asb[:].rearrange("p g n -> p (g n)"), scalar=-1.0, op=ALU.mult))
    q0 = ctx.val("q")

    for gg in range(NG):
        for dgl in range(G):
            dg = gg * G + dgl
            w = [("q", q0)] if gg == 0 else [(AX, G * NST * gg)]
            ctx.emit("sync", (lambda dg=dg, dgl=dgl: lambda e: e.dma_start(
                out=dl[:, dgl, :], in_=delta[dg * 128:(dg + 1) * 128, :]))(),
                waits=tuple(w), fire=ctx.fire(tg + "dl" + str(dgl), 16))
            ctx.emit("sync", (lambda dg=dg, dgl=dgl: lambda e: e.dma_start(
                out=duf[:, dgl, :], in_=duf16[dg * 128:(dg + 1) * 128, :]))(),
                waits=tuple(([("q", q0)] if gg == 0 else
                             [(V, vb(G * NST * gg - G + dgl) + 1)])),
                fire=ctx.fire(tg + "df" + str(dgl), 16))
            ctx.emit("gpsimd", (lambda dgl=dgl: lambda e: e.memset(
                yac[:, dgl, :], 0.0))(),
                waits=tuple(([("q", q0)] if gg == 0 else
                             [(tg + "ys" + str(dgl), 16 * gg)])),
                fire=ctx.fire(PP, 1))
        for n in range(NST):
            i = gg * NST + n
            p = i % 2
            rw = [("q", q0)] if i == 0 else ([(P, i - 1)] if i >= 2 else [])
            ctx.emit("sync", (lambda n=n, p=p: lambda e: e.dma_start(
                out=brow[:, p, :], in_=Bm[n:n + 1, :]))(),
                waits=tuple(rw), fire=ctx.fire(Wb + str(p), 16))
            ctx.emit("sync", (lambda n=n, p=p: lambda e: e.dma_start(
                out=crow[:, p, :], in_=Cm[n:n + 1, :]))(),
                waits=tuple(rw), fire=ctx.fire(Wc + str(p), 16))
            for fc in range(4):
                wts = []
                if fc == 0:
                    wts = [(Wb + str(p), 16 * (i // 2 + 1)),
                           (Wc + str(p), 16 * (i // 2 + 1)), (AE, 2 * i)]
                    if i == 0:
                        wts.append(("q", q0))
                ctx.emit("tensor", (lambda p=p, fc=fc: lambda e: e.matmul(
                    out=psB[:, fc, :], lhsT=ones_row[:],
                    rhs=brow[:, p, fc * 512:(fc + 1) * 512],
                    start=True, stop=True))(),
                    waits=tuple(wts))
            for fc in range(4):
                ctx.emit("tensor", (lambda p=p, fc=fc: lambda e: e.matmul(
                    out=psC[:, fc, :], lhsT=ones_row[:],
                    rhs=crow[:, p, fc * 512:(fc + 1) * 512],
                    start=True, stop=True))(),
                    fire=ctx.fire(P, 1) if fc == 3 else None)
            ctx.emit("scalar", (lambda p=p: lambda e: e.activation(
                out=besb[:, p, :],
                in_=psB[:].rearrange("p a f -> p (a f)"),
                func=AF.Identity))(),
                waits=tuple([(P, i + 1)]
                            + ([(V, vb(G * (i - 1) - 1) + 1)] if i >= 2 else [])),
                fire=ctx.fire(AE, 1))
            ctx.emit("scalar", (lambda p=p: lambda e: e.activation(
                out=cesb[:, p, :],
                in_=psC[:].rearrange("p a f -> p (a f)"),
                func=AF.Identity))(),
                waits=tuple([(Y, G * (i - 1))] if i >= 2 else []),
                fire=ctx.fire(AE, 1))
            for dgl in range(G):
                dg = gg * G + dgl
                e_ = i * G + dgl
                pe = e_ % 2
                ctx.emit("scalar", (lambda dgl=dgl, dg=dg, n=n, pe=pe, gg=gg:
                                    lambda e: e.activation(
                    out=dat[:, pe, :], in_=dl[:, dgl, :], func=AF.Exp,
                    scale=asb[:, dg, n:n + 1]))(),
                    waits=tuple([(tg + "dl" + str(dgl), 16 * (gg + 1))]
                                + ([(V, vb(e_ - 2) + 2)] if e_ >= 2 else
                                   [("q", q0)])),
                    fire=ctx.fire(AX, 1))
                ctx.emit("vector", (lambda dgl=dgl, p=p, pe=pe, gg=gg:
                                    lambda e: e.tensor_mul(
                    out=dbut[:, pe, :], in0=duf[:, dgl, :],
                    in1=besb[:, p, :]))(),
                    waits=tuple([(AE, 2 * i + 1),
                                 (tg + "df" + str(dgl), 16 * (gg + 1))]
                                + ([(V, vb(e_ - 2) + 2)] if e_ >= 2 else [])),
                    fire=ctx.fire(V, 1))
                ctx.emit("vector", (lambda pe=pe, e_=e_, i=i, dgl=dgl:
                                    lambda e: e.tensor_tensor_scan(
                    out=ht[:, pe, :], data0=dat[:, pe, :],
                    data1=dbut[:, pe, :],
                    initial=0.0, op0=ALU.mult, op1=ALU.add))(),
                    waits=tuple([(AX, G * i + dgl + 1), (V, vb(e_) + 1)]
                                + ([(tg + "hp", (e_ - 2) // 2 + 1)]
                                   if (e_ % 2 == 1 and e_ >= 2) else [])),
                    fire=ctx.fire(V, 1))
                if e_ % 2 == 0:
                    ctx.emit("vector", (lambda pe=pe, p=p, e_=e_: lambda e: e.tensor_mul(
                        out=hct[:, pe, :], in0=ht[:, pe, :], in1=cesb[:, p, :]))(),
                        waits=tuple([(V, vb(e_) + 2), (AE, 2 * i + 2)]
                                    + ([(Y, e_ - 1)] if e_ >= 1 else [])),
                        fire=ctx.fire(V, 1))
                    ctx.emit("gpsimd", (lambda dgl=dgl, pe=pe:
                                        lambda e: e.tensor_add(
                        out=yac[:, dgl, :], in0=yac[:, dgl, :],
                        in1=hct[:, pe, :]))(),
                        waits=((V, vb(e_) + 3), (PP, gg * G + dgl + 1)),
                        fire=ctx.fire(Y, 1))
                else:
                    ctx.emit("gpsimd", (lambda pe=pe, p=p: lambda e: e.tensor_mul(
                        out=hct[:, pe, :], in0=ht[:, pe, :], in1=cesb[:, p, :]))(),
                        waits=tuple([(V, vb(e_) + 2), (AE, 2 * i + 2)]
                                    + ([(Y, e_ - 1)] if e_ >= 1 else [])),
                        fire=ctx.fire(tg + "hp", 1))
                    ctx.emit("gpsimd", (lambda dgl=dgl, pe=pe:
                                        lambda e: e.tensor_add(
                        out=yac[:, dgl, :], in0=yac[:, dgl, :],
                        in1=hct[:, pe, :]))(),
                        waits=((tg + "hp", e_ // 2 + 1),
                               (PP, gg * G + dgl + 1)),
                        fire=ctx.fire(Y, 1))
        for dgl in range(G):
            dg = gg * G + dgl
            ctx.emit("gpsimd", (lambda dg=dg, dgl=dgl: lambda e: e.dma_start(
                out=y[dg * 128:(dg + 1) * 128, :], in_=yac[:, dgl, :]))(),
                waits=((Y, (gg + 1) * NST * G),),
                fire=ctx.fire(tg + "ys" + str(dgl), 16))
    ctx.emit("vector", lambda e: e.engine_nop(),
             waits=tuple((tg + "ys" + str(d), 16 * NG) for d in range(G)),
             fire=ctx.fire("q", 1))


def build_scan(layer, dg0=0, dg1=16):
    nc, stack, ctx = new_prog()
    S = Seq(ctx)
    with stack:
        io = scan_io(nc, layer, lambda n, s, d, o: (dout if o else din)(nc, n, s, d))
        ctx.psA = ctx.psum("psA", [128, 4, 512], F32)
        ctx.psB = ctx.psum("psB", [128, 4, 512], F32)
        ctx.phase("scan")
        emit_scan3(nc, ctx, S, io, layer)
        S.done()
    return nc



def l0out_io(nc, mk):
    io = {}
    io["y"] = mk("y", (DI, T), F32, 0)
    io["u_f"] = mk("u_f", (DI, T), F32, 0)
    io["z_bf"] = mk("z_bf", (DI, T), BF16, 0)
    io["d0"] = mk("d0", (DI,), F32, 0)
    io["wout_bf"] = mk("wout_bf", (DI, D), BF16, 0)
    io["xn_f"] = mk("xn_f", (D, T), F32, 0)
    io["xn_bf"] = mk("xn_bf", (D, T), BF16, 0)
    io["w1_bf"] = mk("w1_bf", (D, 4096), BF16, 0)
    io["b1"] = mk("b1", (4096,), F32, 0)
    io["w2_bf"] = mk("w2_bf", (4096, D), BF16, 0)
    io["b2"] = mk("b2", (D,), F32, 0)
    io["x"] = mk("x", (D, T), F32, 0)
    io["attn0"] = mk("attn0", (D, T), F32, 1)
    io["h_bf"] = mk("h_bf", (4096, T), BF16, 1)
    io["x1"] = mk("x1", (D, T), F32, 1)
    return io


def emit_l0out(nc, ctx, S, io):
    H = T // 2

    y = io["y"]
    u_f = io["u_f"]
    z_bf = io["z_bf"]
    d0 = io["d0"]
    wout = io["wout_bf"]
    xn_f = io["xn_f"]
    xn_bf = io["xn_bf"]
    w1 = io["w1_bf"]
    b1 = io["b1"]
    w2 = io["w2_bf"]
    b2 = io["b2"]
    x = io["x"]
    attn0 = io["attn0"]
    h_bf = io["h_bf"]
    x1 = io["x1"]

    KD = D // 128
    if True:
        gbuf = ctx.sbuf("gbuf", [128, 16, T], BF16)
        xnb = ctx.sbuf("xnb", [128, KD, T], BF16)
        t0 = ctx.sbuf("t0", [128, T], F32)
        t1 = ctx.sbuf("t1", [128, T], F32)
        dcol = ctx.sbuf("dcol", [128, 16], F32)
        zt = ctx.sbuf("zt", [128, T], BF16)
        wom = ctx.sbuf("wom", [128, 16, 128], BF16)
        w1m = ctx.sbuf("w1m", [128, KD, 128], BF16)
        b1c = ctx.sbuf("b1c", [128, 32], F32)
        b2c = ctx.sbuf("b2c", [128, 8], F32)
        tb = ctx.sbuf("tb", [128, 4, H], F32)
        hb = ctx.sbuf("hb", [128, H], BF16)
        hq = ctx.sbuf("hq", [128, 32, 512], BF16)
        w2m = ctx.sbuf("w2m", [128, 32, 128], BF16)
        knot = ctx.sbuf("knot", [128, 18], F32)
        ps_big = ctx.psA
        psb2 = ps_big[:, 0:2, :].rearrange("p a f -> p (a f)")
        psq = ps_big[:, 0:1, :].rearrange("p a f -> p (a f)")

        sgn = []
        for i, (vk, ck) in enumerate(PWL_KNOTS):
            S.op("vector", lambda e, i=i, vk=vk, ck=ck: e.memset(
                knot[:, i:i + 1], -abs(ck) * vk))
            sgn.append(1.0 if ck >= 0 else -1.0)
        S.op("vector", lambda e: e.memset(knot[:, 16:17], -1024.0))
        S.dma(dcol[:], d0.rearrange("(g p) -> p g", p=128), slow=True)
        S.dma(b1c[:], b1.rearrange("(g p) -> p g", p=128), slow=True)
        S.dma(b2c[:], b2.rearrange("(g p) -> p g", p=128), slow=True)
        S.op("vector", lambda e: e.tensor_single_scalar(
            out=b2c[:], in_=b2c[:], scalar=256.0, op=ALU.mult))

        for g in range(16):
            S.dma(t0[:], y[g * 128:(g + 1) * 128, :])
            S.dma(t1[:], u_f[g * 128:(g + 1) * 128, :])
            S.dma(zt[:], z_bf[g * 128:(g + 1) * 128, :])
            S.op("vector", lambda e, g=g: e.scalar_tensor_tensor(
                out=t0[:], in0=t1[:], scalar=dcol[:, g:g + 1], in1=t0[:],
                op0=ALU.mult, op1=ALU.add))
            S.op("scalar", lambda e: e.activation(out=t1[:], in_=zt[:], func=AF.Silu))
            S.op("vector", lambda e, g=g: e.tensor_mul(
                out=gbuf[:, g, :], in0=t0[:], in1=t1[:]))

        for m in range(KD):
            S.dma(wom[:], wout.rearrange("(k p) m -> p k m", p=128)[:, :, m * 128:(m + 1) * 128])
            for th in range(2):
                for k in range(16):
                    for j in range(2):
                        S.op("tensor", lambda e, k=k, th=th, j=j: e.matmul(
                            out=ps_big[:, j, :], lhsT=wom[:, k, :],
                            rhs=gbuf[:, k, th * H + j * 512:th * H + (j + 1) * 512],
                            start=(k == 0), stop=(k == 15)))
                S.dma(t0[:, 0:H], xn_f[m * 128:(m + 1) * 128, th * H:(th + 1) * H])
                S.op("vector", lambda e: e.tensor_add(
                    out=t1[:, 0:H], in0=t0[:, 0:H], in1=psb2))
                S.dma(attn0[m * 128:(m + 1) * 128, th * H:(th + 1) * H],
                      t1[:, 0:H], queue="gpsimd")

        for k in range(KD):
            S.dma(xnb[:, k, :], xn_bf[k * 128:(k + 1) * 128, :])
        for m in range(32):
            S.dma(w1m[:], w1.rearrange("(k p) m -> p k m", p=128)[:, :, m * 128:(m + 1) * 128])
            for th in range(2):
                for k in range(KD):
                    for j in range(2):
                        S.op("tensor", lambda e, k=k, th=th, j=j: e.matmul(
                            out=ps_big[:, j, :], lhsT=w1m[:, k, :],
                            rhs=xnb[:, k, th * H + j * 512:th * H + (j + 1) * 512],
                            start=(k == 0), stop=(k == KD - 1)))
                S.op("vector", lambda e, m=m: e.tensor_scalar(
                    out=tb[:, 0, :], in0=psb2, scalar1=b1c[:, m:m + 1],
                    scalar2=256.0, op0=ALU.add, op1=ALU.mult))
                S.op("vector", lambda e: e.tensor_scalar(
                    out=tb[:, 1, :], in0=tb[:, 0, :], scalar1=RNE_C,
                    scalar2=RNE_C, op0=ALU.add, op1=ALU.subtract))
                S.op("scalar", lambda e: e.activation(
                    out=tb[:, 2, :], in_=tb[:, 1, :], func=AF.Abs))
                S.op("vector", lambda e: e.tensor_single_scalar(
                    out=tb[:, 3, :], in_=tb[:, 2, :], scalar=PWL_S0, op=ALU.mult))
                for i, (vk, ck) in enumerate(PWL_KNOTS):
                    S.op("scalar", lambda e, i=i, ck=ck: e.activation(
                        out=tb[:, 0, :], in_=tb[:, 2, :], func=AF.Relu,
                        scale=abs(ck), bias=knot[:, i:i + 1]))
                    S.op("vector", lambda e, i=i: e.scalar_tensor_tensor(
                        out=tb[:, 3, :], in0=tb[:, 0, :], scalar=sgn[i],
                        in1=tb[:, 3, :], op0=ALU.mult, op1=ALU.add))
                S.op("scalar", lambda e: e.activation(
                    out=tb[:, 0, :], in_=tb[:, 2, :], func=AF.Relu,
                    bias=knot[:, 16:17]))
                S.op("vector", lambda e: e.tensor_scalar(
                    out=tb[:, 0, :], in0=tb[:, 0, :], scalar1=1.0,
                    scalar2=PWL_J256, op0=ALU.min, op1=ALU.mult))
                S.op("vector", lambda e: e.tensor_add(
                    out=tb[:, 3, :], in0=tb[:, 3, :], in1=tb[:, 0, :]))
                S.op("vector", lambda e: e.scalar_tensor_tensor(
                    out=tb[:, 0, :], in0=tb[:, 1, :], scalar=0.5,
                    in1=tb[:, 3, :], op0=ALU.mult, op1=ALU.add))
                S.op("vector", lambda e: e.tensor_scalar(
                    out=tb[:, 1, :], in0=tb[:, 0, :], scalar1=RNE_C,
                    scalar2=RNE_C, op0=ALU.add, op1=ALU.subtract))
                S.op("vector", lambda e: e.tensor_copy(out=hb[:], in_=tb[:, 1, :]))
                S.dma(h_bf[m * 128:(m + 1) * 128, th * H:(th + 1) * H],
                      hb[:], queue="gpsimd")

        for f4 in range(4):
            for k in range(32):
                S.dma(hq[:, k, :], h_bf[k * 128:(k + 1) * 128, f4 * 512:(f4 + 1) * 512])
            for m in range(KD):
                S.dma(w2m[:], w2.rearrange("(k p) m -> p k m", p=128)[:, :, m * 128:(m + 1) * 128])
                for k in range(32):
                    S.op("tensor", lambda e, k=k: e.matmul(
                        out=psq, lhsT=w2m[:, k, :], rhs=hq[:, k, :],
                        start=(k == 0), stop=(k == 31)))
                S.op("vector", lambda e, m=m: e.tensor_scalar(
                    out=tb[:, 0, 0:512], in0=psq, scalar1=b2c[:, m:m + 1],
                    scalar2=RNE_C, op0=ALU.add, op1=ALU.add))
                S.op("vector", lambda e: e.tensor_scalar(
                    out=tb[:, 1, 0:512], in0=tb[:, 0, 0:512], scalar1=RNE_C,
                    scalar2=1.0 / 256.0, op0=ALU.subtract, op1=ALU.mult))
                S.dma(tb[:, 2, 0:512], x[m * 128:(m + 1) * 128, f4 * 512:(f4 + 1) * 512])
                S.dma(tb[:, 3, 0:512], attn0[m * 128:(m + 1) * 128, f4 * 512:(f4 + 1) * 512])
                S.op("vector", lambda e: e.tensor_add(
                    out=tb[:, 0, 0:512], in0=tb[:, 2, 0:512], in1=tb[:, 3, 0:512]))
                S.op("vector", lambda e: e.tensor_add(
                    out=tb[:, 1, 0:512], in0=tb[:, 0, 0:512], in1=tb[:, 1, 0:512]))
                S.dma(x1[m * 128:(m + 1) * 128, f4 * 512:(f4 + 1) * 512],
                      tb[:, 1, 0:512], queue="gpsimd")


def build_l0out():
    nc, stack, ctx = new_prog()
    S = Seq(ctx)
    with stack:
        io = l0out_io(nc, lambda n, s, d, o: (dout if o else din)(nc, n, s, d))
        ctx.psA = ctx.psum("psA", [128, 4, 512], F32)
        ctx.psB = ctx.psum("psB", [128, 4, 512], F32)
        ctx.phase("l0out")
        emit_l0out2(nc, ctx, S, io)
        S.done()
    return nc



def l1out_io(nc, mk):
    io = {}
    io["y"] = mk("y", (DI, T), F32, 0)
    io["u_f"] = mk("u_f", (DI, T), F32, 0)
    io["z_bf"] = mk("z_bf", (DI, T), BF16, 0)
    io["dskip"] = mk("dskip", (DI,), BF16, 0)
    io["opT"] = mk("opT", (DI, D), BF16, 0)
    io["xn_bf"] = mk("xn_bf", (D, T), BF16, 0)
    io["f1t_bf"] = mk("f1t_bf", (D, 4096), BF16, 0)
    io["w1s"] = mk("w1s", (4096,), F32, 0)
    io["f2t_bf"] = mk("f2t_bf", (D, 4096), BF16, 0)
    io["w2s"] = mk("w2s", (4096,), F32, 0)
    io["h1t_bf"] = mk("h1t_bf", (4096, 4096), BF16, 0)
    io["h1s"] = mk("h1s", (4096,), F32, 0)
    io["h2t_bf"] = mk("h2t_bf", (4096, 4096), BF16, 0)
    io["h2s"] = mk("h2s", (4096,), F32, 0)
    io["w3t_bf"] = mk("w3t_bf", (4096, D), BF16, 0)
    io["w3s"] = mk("w3s", (D,), F32, 0)
    io["x1"] = mk("x1", (D, T), F32, 0)
    io["attn1"] = mk("attn1", (D, T), BF16, 1)
    io["gf"] = mk("gf", (4096, T), BF16, 1)
    io["a1"] = mk("a1", (4096, T), BF16, 1)
    io["a2"] = mk("a2", (4096, T), BF16, 1)
    io["out"] = mk("out", (D, T), F32, 1)
    return io


def emit_l1out(nc, ctx, S, io):
    H = T // 2
    KD = D // 128

    y1 = io["y"]
    u1 = io["u_f"]
    z1 = io["z_bf"]
    dskip = io["dskip"]
    opt = io["opT"]
    xn1 = io["xn_bf"]
    f1t = io["f1t_bf"]
    w1s = io["w1s"]
    f2t = io["f2t_bf"]
    w2s = io["w2s"]
    h1t = io["h1t_bf"]
    h1s = io["h1s"]
    h2t = io["h2t_bf"]
    h2s = io["h2s"]
    w3t = io["w3t_bf"]
    w3s = io["w3s"]
    x1 = io["x1"]
    attn1 = io["attn1"]
    gf = io["gf"]
    a1 = io["a1"]
    a2 = io["a2"]
    out = io["out"]

    if True:
        gbuf = ctx.sbuf("gbuf", [128, 16, T], BF16)
        t0 = ctx.sbuf("t0", [128, T], F32)
        t1 = ctx.sbuf("t1", [128, T], F32)
        zt = ctx.sbuf("zt", [128, T], BF16)
        dcr = ctx.sbuf("dcr", [128, 16], BF16)
        dcol = ctx.sbuf("dcol", [128, 16], F32)
        wm = ctx.sbuf("wm", [128, 16, 128], BF16)
        wm2 = ctx.sbuf("wm2", [128, 16, 128], BF16)
        ab = ctx.sbuf("ab", [128, H], BF16)
        s1c = ctx.sbuf("s1c", [128, 32], F32)
        s2c = ctx.sbuf("s2c", [128, 32], F32)
        sh1c = ctx.sbuf("sh1c", [128, 32], F32)
        sh2c = ctx.sbuf("sh2c", [128, 32], F32)
        s3c = ctx.sbuf("s3c", [128, 8], F32)
        rq = ctx.sbuf("rq", [128, 32, 512], BF16)
        w32 = ctx.sbuf("w32", [128, 32, 128], BF16)
        ps_big = ctx.psA
        psb2 = ps_big[:, 0:2, :].rearrange("p a f -> p (a f)")
        psc2 = ps_big[:, 2:4, :].rearrange("p a f -> p (a f)")
        psq = ps_big[:, 0:1, :].rearrange("p a f -> p (a f)")

        S.dma(dcr[:], dskip.rearrange("(g p) -> p g", p=128), slow=True)
        S.op("vector", lambda e: e.tensor_copy(out=dcol[:], in_=dcr[:]))
        S.dma(s1c[:], w1s.rearrange("(g p) -> p g", p=128), slow=True)
        S.dma(s2c[:], w2s.rearrange("(g p) -> p g", p=128), slow=True)
        S.dma(sh1c[:], h1s.rearrange("(g p) -> p g", p=128), slow=True)
        S.dma(sh2c[:], h2s.rearrange("(g p) -> p g", p=128), slow=True)
        S.dma(s3c[:], w3s.rearrange("(g p) -> p g", p=128), slow=True)

        for g in range(16):
            S.dma(t0[:], y1[g * 128:(g + 1) * 128, :])
            S.dma(t1[:], u1[g * 128:(g + 1) * 128, :])
            S.dma(zt[:], z1[g * 128:(g + 1) * 128, :])
            S.op("vector", lambda e, g=g: e.scalar_tensor_tensor(
                out=t0[:], in0=t1[:], scalar=dcol[:, g:g + 1], in1=t0[:],
                op0=ALU.mult, op1=ALU.add))
            S.op("scalar", lambda e: e.activation(out=t1[:], in_=zt[:], func=AF.Silu))
            S.op("vector", lambda e, g=g: e.tensor_mul(
                out=gbuf[:, g, :], in0=t0[:], in1=t1[:]))

        for m in range(KD):
            S.dma(wm[:], opt.rearrange("(k p) m -> p k m", p=128)[:, :, m * 128:(m + 1) * 128])
            for th in range(2):
                for k in range(16):
                    for j in range(2):
                        S.op("tensor", lambda e, k=k, th=th, j=j: e.matmul(
                            out=ps_big[:, j, :], lhsT=wm[:, k, :],
                            rhs=gbuf[:, k, th * H + j * 512:th * H + (j + 1) * 512],
                            start=(k == 0), stop=(k == 15)))
                S.op("vector", lambda e: e.tensor_copy(out=ab[:], in_=psb2))
                S.dma(attn1[m * 128:(m + 1) * 128, th * H:(th + 1) * H],
                      ab[:], queue="gpsimd")

        for k in range(KD):
            S.dma(gbuf[:, k, :], xn1[k * 128:(k + 1) * 128, :])
        for m in range(32):
            S.dma(wm[:, 0:KD, :], f1t.rearrange("(k p) m -> p k m", p=128)[:, :, m * 128:(m + 1) * 128])
            S.dma(wm2[:, 0:KD, :], f2t.rearrange("(k p) m -> p k m", p=128)[:, :, m * 128:(m + 1) * 128])
            for th in range(2):
                for k in range(KD):
                    for j in range(2):
                        S.op("tensor", lambda e, k=k, th=th, j=j: e.matmul(
                            out=ps_big[:, j, :], lhsT=wm[:, k, :],
                            rhs=gbuf[:, k, th * H + j * 512:th * H + (j + 1) * 512],
                            start=(k == 0), stop=(k == KD - 1)))
                for k in range(KD):
                    for j in range(2):
                        S.op("tensor", lambda e, k=k, th=th, j=j: e.matmul(
                            out=ps_big[:, 2 + j, :], lhsT=wm2[:, k, :],
                            rhs=gbuf[:, k, th * H + j * 512:th * H + (j + 1) * 512],
                            start=(k == 0), stop=(k == KD - 1)))
                S.op("scalar", lambda e, m=m: e.activation(
                    out=t0[:, 0:H], in_=psb2, func=AF.Silu, scale=s1c[:, m:m + 1]))
                S.op("scalar", lambda e, m=m: e.activation(
                    out=t1[:, 0:H], in_=psc2, func=AF.Identity, scale=s2c[:, m:m + 1]))
                S.op("vector", lambda e: e.tensor_mul(
                    out=ab[:], in0=t0[:, 0:H], in1=t1[:, 0:H]))
                S.dma(gf[m * 128:(m + 1) * 128, th * H:(th + 1) * H],
                      ab[:], queue="gpsimd")

        for (src, dst, wmat, scol) in ((gf, a1, h1t, sh1c), (a1, a2, h2t, sh2c)):
            for f4 in range(4):
                for k in range(32):
                    S.dma(rq[:, k, :], src[k * 128:(k + 1) * 128,
                                           f4 * 512:(f4 + 1) * 512])
                for m in range(32):
                    S.dma(w32[:], wmat.rearrange("(k p) m -> p k m", p=128)[:, :, m * 128:(m + 1) * 128])
                    for k in range(32):
                        S.op("tensor", lambda e, k=k: e.matmul(
                            out=psq, lhsT=w32[:, k, :], rhs=rq[:, k, :],
                            start=(k == 0), stop=(k == 31)))
                    S.op("scalar", lambda e, m=m, scol=scol: e.activation(
                        out=ab[:, 0:512], in_=psq, func=AF.Gelu,
                        scale=scol[:, m:m + 1]))
                    S.dma(dst[m * 128:(m + 1) * 128, f4 * 512:(f4 + 1) * 512],
                          ab[:, 0:512], queue="gpsimd")

        for f4 in range(4):
            for k in range(32):
                S.dma(rq[:, k, :], a2[k * 128:(k + 1) * 128, f4 * 512:(f4 + 1) * 512])
            for m in range(KD):
                S.dma(w32[:], w3t.rearrange("(k p) m -> p k m", p=128)[:, :, m * 128:(m + 1) * 128])
                for k in range(32):
                    S.op("tensor", lambda e, k=k: e.matmul(
                        out=psq, lhsT=w32[:, k, :], rhs=rq[:, k, :],
                        start=(k == 0), stop=(k == 31)))
                S.op("scalar", lambda e, m=m: e.activation(
                    out=t0[:, 0:512], in_=psq, func=AF.Identity,
                    scale=s3c[:, m:m + 1]))
                S.dma(t1[:, 0:512], x1[m * 128:(m + 1) * 128, f4 * 512:(f4 + 1) * 512])
                S.dma(ab[:, 0:512], attn1[m * 128:(m + 1) * 128, f4 * 512:(f4 + 1) * 512])
                S.op("vector", lambda e: e.tensor_add(
                    out=t1[:, 0:512], in0=t1[:, 0:512], in1=ab[:, 0:512]))
                S.op("vector", lambda e: e.tensor_add(
                    out=t0[:, 0:512], in0=t0[:, 0:512], in1=t1[:, 0:512]))
                S.dma(out[m * 128:(m + 1) * 128, f4 * 512:(f4 + 1) * 512],
                      t0[:, 0:512], queue="gpsimd")


def build_l1out():
    nc, stack, ctx = new_prog()
    S = Seq(ctx)
    with stack:
        io = l1out_io(nc, lambda n, s, d, o: (dout if o else din)(nc, n, s, d))
        ctx.psA = ctx.psum("psA", [128, 4, 512], F32)
        ctx.psB = ctx.psum("psB", [128, 4, 512], F32)
        ctx.phase("l1out")
        emit_l1out2(nc, ctx, S, io)
        S.done()
    return nc


_PROGS = {}


def _prog(name, builder):
    if name not in _PROGS:
        _PROGS[name] = builder()
    return _PROGS[name]


def _run(name, builder, maps):
    res = run_bass_kernel_spmd(_prog(name, builder), maps, core_ids=list(range(8)))
    return res.results


def kernel(**inp):
    asc = np.ascontiguousarray
    g = {k: np.asarray(v) for k, v in inp.items()}
    x = g["L0_attn_x"]
    xfm = [asc(x[c].T) for c in range(8)]

    f1T = asc(g["L1_ffn_w1_w"].T)
    f2T = asc(g["L1_ffn_w2_w"].T)
    h1T = asc(g["L1_ffn_h1_w"].T)
    h2T = asc(g["L1_ffn_h2_w"].T)
    w3T = asc(g["L1_ffn_w3_w"].T)
    ipT = asc(g["L1_attn_in_proj_w"].T)
    opT = asc(g["L1_attn_out_proj_w"].T)
    xpT = asc(g["L1_attn_x_proj_w"].T)
    dpT = asc(g["L1_attn_dt_proj_w"].T)
    cw0 = asc(g["L0_attn_conv_w"].reshape(DI, 4))
    cw1 = asc(g["L1_attn_conv_w"].reshape(DI, 4))

    if FUSED:
        return kernel_fused(g, xfm, (f1T, f2T, h1T, h2T, w3T, ipT, opT, xpT,
                                     dpT, cw0, cw1))

    pm = {"win": g["L0_attn_W_in"], "wout": g["L0_attn_W_out"],
          "w1": g["L0_ffn_w1"], "w2": g["L0_ffn_w2"],
          "wxp": g["L0_attn_W_xproj"],
          "f1t": f1T, "f2t": f2T, "h1t": h1T, "h2t": h2T, "w3t": w3T}
    W = _run("prep", build_prep, [pm] * 8)[0]

    mA = [{"x": xfm[c], "lnw": g["L0_attn_ln_w"], "lnb": g["L0_attn_ln_b"],
           "win_bf": W["win_bf"], "convw": cw0, "convb": g["L0_attn_conv_b"],
           "wxp_bf": W["wxp_bf"], "wdt": g["L0_attn_W_dt"],
           "bdt": g["L0_attn_b_dt"]} for c in range(8)]
    rA = _run("l0a", lambda: build_lxa(0), mA)

    mS = [{"delta": rA[c]["delta"], "du": rA[c]["du"],
           "du_f16": rA[c]["du_f16"], "Bm": rA[c]["B0"],
           "Cm": rA[c]["C0"], "alog": g["L0_attn_A_log"]} for c in range(8)]
    rS = _run("scan0", lambda: build_scan(0), mS)

    mO = [{"y": rS[c]["y"], "u_f": rA[c]["u_f"], "z_bf": rA[c]["z_bf"],
           "d0": g["L0_attn_D"], "wout_bf": W["wout_bf"], "xn_f": rA[c]["xn_f"],
           "xn_bf": rA[c]["xn_bf"], "w1_bf": W["w1_bf"], "b1": g["L0_ffn_b1"],
           "w2_bf": W["w2_bf"], "b2": g["L0_ffn_b2"], "x": xfm[c]}
          for c in range(8)]
    rO = _run("l0out", build_l0out, mO)

    mA1 = [{"x": rO[c]["x1"], "win_bf": ipT, "convw": cw1,
            "convb": g["L1_attn_conv_b"], "wxp_bf": xpT, "wdt": dpT,
            "bdt": g["L1_attn_dt_proj_b"]} for c in range(8)]
    rA1 = _run("l1a", lambda: build_lxa(1), mA1)

    mS1 = [{"delta": rA1[c]["delta"], "du": rA1[c]["du"],
            "du_f16": rA1[c]["du_f16"], "Bm": rA1[c]["B0"],
            "Cm": rA1[c]["C0"], "alog": g["L1_attn_A_log"]} for c in range(8)]
    rS1 = _run("scan1", lambda: build_scan(1), mS1)

    mO1 = [{"y": rS1[c]["y"], "u_f": rA1[c]["u_f"], "z_bf": rA1[c]["z_bf"],
            "dskip": g["L1_attn_D_skip"], "opT": opT, "xn_bf": rA1[c]["xn_bf"],
            "f1t_bf": W["f1t_bf"], "w1s": g["L1_ffn_w1_s"],
            "f2t_bf": W["f2t_bf"], "w2s": g["L1_ffn_w2_s"],
            "h1t_bf": W["h1t_bf"], "h1s": g["L1_ffn_h1_s"],
            "h2t_bf": W["h2t_bf"], "h2s": g["L1_ffn_h2_s"],
            "w3t_bf": W["w3t_bf"], "w3s": g["L1_ffn_w3_s"],
            "x1": rO[c]["x1"]} for c in range(8)]
    rO1 = _run("l1out", build_l1out, mO1)

    return np.stack([asc(rO1[c]["out"].T) for c in range(8)], axis=0)



def build_fused():
    nc, stack, ctx = new_prog()
    S = Seq(ctx)
    with stack:
        ctx.psA = ctx.psum("psA", [128, 4, 512], F32)
        ctx.psB = ctx.psum("psB", [128, 4, 512], F32)

        def D_(n, s, d):
            return nc.dram_tensor(n, list(s), d, kind="Internal").ap()

        iop = prep_io(nc, lambda n, s, d, o: D_(n, s, d) if o else din(nc, n, s, d))
        ctx.phase("prep")
        emit_prep(nc, ctx, S, iop)

        def mk0(n, s, d, o):
            if n in ("win_bf", "wxp_bf"):
                return iop[n]
            if o:
                return D_("l0_" + n, s, d)
            return din(nc, "l0_" + n, s, d)
        io0 = lxa_io(nc, 0, mk0)
        ctx.phase("l0a")
        emit_lxa2(nc, ctx, S, io0, 0)

        y0 = D_("y0", (DI, T), F32)
        ios0 = {"delta": io0["delta"], "du": io0["du"],
                "du_f16": io0["du_f16"], "Bm": io0["B0"],
                "Cm": io0["C0"], "alog": din(nc, "alog0", (DI, 64), F32),
                "y": y0}
        ctx.phase("scan0")
        emit_scan3(nc, ctx, S, ios0, 0)

        def mko0(n, s, d, o):
            m = {"y": y0, "u_f": io0["u_f"], "z_bf": io0["z_bf"],
                 "wout_bf": iop["wout_bf"], "xn_f": io0["xn_f"],
                 "xn_bf": io0["xn_bf"], "w1_bf": iop["w1_bf"],
                 "w2_bf": iop["w2_bf"], "x": io0["x"]}
            if n in m:
                return m[n]
            if o:
                return D_("o0_" + n, s, d)
            return din(nc, "o0_" + n, s, d)
        ioo0 = l0out_io(nc, mko0)
        ctx.phase("l0out")
        emit_l0out2(nc, ctx, S, ioo0)

        def mk1(n, s, d, o):
            if n == "x":
                return ioo0["x1"]
            if o:
                return D_("l1_" + n, s, d)
            return din(nc, "l1_" + n, s, d)
        io1 = lxa_io(nc, 1, mk1)
        ctx.phase("l1a")
        emit_lxa2(nc, ctx, S, io1, 1)

        y1 = D_("y1", (DI, T), F32)
        ios1 = {"delta": io1["delta"], "du": io1["du"],
                "du_f16": io1["du_f16"], "Bm": io1["B0"],
                "Cm": io1["C0"], "alog": din(nc, "alog1", (DI, 16), BF16),
                "y": y1}
        ctx.phase("scan1")
        emit_scan3(nc, ctx, S, ios1, 1)

        def mko1(n, s, d, o):
            m = {"y": y1, "u_f": io1["u_f"], "z_bf": io1["z_bf"],
                 "xn_bf": io1["xn_bf"], "f1t_bf": iop["f1t_bf"],
                 "f2t_bf": iop["f2t_bf"], "h1t_bf": iop["h1t_bf"],
                 "h2t_bf": iop["h2t_bf"], "w3t_bf": iop["w3t_bf"],
                 "x1": ioo0["x1"]}
            if n in m:
                return m[n]
            if n == "out":
                return dout(nc, "out", s, d)
            if o:
                return D_("o1_" + n, s, d)
            return din(nc, "o1_" + n, s, d)
        ioo1 = l1out_io(nc, mko1)
        ctx.phase("l1out")
        emit_l1out2(nc, ctx, S, ioo1)
        S.done()
    return nc


FUSED = True


def kernel_fused(g, xfm, hosted):
    (f1T, f2T, h1T, h2T, w3T, ipT, opT, xpT, dpT, cw0, cw1) = hosted
    base = {"win": g["L0_attn_W_in"], "wout": g["L0_attn_W_out"],
            "w1": g["L0_ffn_w1"], "w2": g["L0_ffn_w2"],
            "wxp": g["L0_attn_W_xproj"],
            "f1t": f1T, "f2t": f2T, "h1t": h1T, "h2t": h2T, "w3t": w3T,
            "l0_lnw": g["L0_attn_ln_w"], "l0_lnb": g["L0_attn_ln_b"],
            "l0_convw": cw0, "l0_convb": g["L0_attn_conv_b"],
            "l0_wdt": g["L0_attn_W_dt"], "l0_bdt": g["L0_attn_b_dt"],
            "alog0": g["L0_attn_A_log"],
            "o0_d0": g["L0_attn_D"], "o0_b1": g["L0_ffn_b1"],
            "o0_b2": g["L0_ffn_b2"],
            "l1_win_bf": ipT, "l1_convw": cw1, "l1_convb": g["L1_attn_conv_b"],
            "l1_wxp_bf": xpT, "l1_wdt": dpT, "l1_bdt": g["L1_attn_dt_proj_b"],
            "alog1": g["L1_attn_A_log"],
            "o1_dskip": g["L1_attn_D_skip"], "o1_opT": opT,
            "o1_w1s": g["L1_ffn_w1_s"], "o1_w2s": g["L1_ffn_w2_s"],
            "o1_h1s": g["L1_ffn_h1_s"], "o1_h2s": g["L1_ffn_h2_s"],
            "o1_w3s": g["L1_ffn_w3_s"]}
    maps = [dict(base, l0_x=xfm[c]) for c in range(8)]
    res = _run("fused", build_fused, maps)
    return np.stack([np.ascontiguousarray(res[c]["out"].T) for c in range(8)],
                    axis=0)



class DStream:
    def __init__(self, ctx, name, npar=2):
        self.ctx = ctx
        self.name = name
        self.npar = npar
        self.n = [0] * npar

    def emit(self, queue, par, fn, waits=()):
        s = f"{self.name}{par}"
        self.n[par] += 1
        self.ctx.emit(queue, fn, waits=waits, fire=self.ctx.fire(s, 16))
        return (s, 16 * self.n[par])

    def tok(self, par, back=0):
        s = f"{self.name}{par}"
        return (s, 16 * max(0, self.n[par] - back))



def emit_scan2(nc, ctx, S, io, layer):
    NST = 64 if layer == 0 else 16
    a_dt = F32 if layer == 0 else BF16
    tg = "sc_"
    Wb, Wc = tg + "rwb", tg + "rwc"
    P, A, V, Y = tg + "pe", tg + "act", tg + "dve", tg + "yad"
    PDd, PDu, PDa = tg + "pdd", tg + "pdu", tg + "pda"
    PA, PV, PP = tg + "pa", tg + "pv", tg + "pp"
    YS0, YS1 = tg + "ys0", tg + "ys1"

    delta = io["delta"]
    duf16 = io["du_f16"]
    Bm = io["Bm"]
    Cm = io["Cm"]
    alog = io["alog"]
    y = io["y"]

    q0 = ctx.val("q")
    nb = NST
    NB = 16 * nb

    dl = ctx.sbuf("dl", [128, 2, T], F32)
    duf = ctx.sbuf("duf", [128, 2, T], F16)
    alr = ctx.sbuf("alr", [128, 2, NST], a_dt)
    asb0 = ctx.sbuf("asb0", [128, 2, NST], F32)
    asb = ctx.sbuf("asb", [128, 2, NST], F32)
    brow = ctx.sbuf("brow", [1, 2, T], F32)
    crow = ctx.sbuf("crow", [1, 2, T], F32)
    dat = ctx.sbuf("dat", [128, 2, T], F16)
    besb = ctx.sbuf("besb", [128, 2, T], F16)
    cesb = ctx.sbuf("cesb", [128, 2, T], F16)
    dbut = ctx.sbuf("dbut", [128, 2, T], F16)
    ht = ctx.sbuf("ht", [128, 2, T], F16)
    hct = ctx.sbuf("hct", [128, 2, T], F16)
    yac = ctx.sbuf("yac", [128, 2, T], F32)
    ones_row = ctx.sbuf("ones_row", [1, 128], F32)
    psB, psC = ctx.psA, ctx.psB

    S.op("vector", lambda e: e.memset(ones_row[:], 1.0))
    q0 = ctx.val("q")

    def qw(i):
        return [("q", q0)] if i == 0 else []

    for dg in range(16):
        g2 = dg % 2
        pw = ([("q", q0)] if dg < 2 else
              [(A, 3 * nb * (dg - 1)), (V, 3 * nb * (dg - 1)), (PA, dg)])
        ctx.emit("sync", (lambda dg=dg, g2=g2: lambda e: e.dma_start(
            out=dl[:, g2, :], in_=delta[dg * 128:(dg + 1) * 128, :]))(),
            waits=tuple(pw + ([(PDd, 16)] if dg == 1 else [])),
            fire=ctx.fire(PDd, 16))
        ctx.emit("sync", (lambda dg=dg, g2=g2: lambda e: e.dma_start(
            out=duf[:, g2, :], in_=duf16[dg * 128:(dg + 1) * 128, :]))(),
            waits=tuple(pw + ([(PDu, 16)] if dg == 1 else [])),
            fire=ctx.fire(PDu, 16))
        ctx.emit("sync", (lambda dg=dg, g2=g2: lambda e: e.dma_start(
            out=alr[:, g2, :], in_=alog[dg * 128:(dg + 1) * 128, :]))(),
            waits=tuple(pw + ([(PDa, 16)] if dg == 1 else [])),
            fire=ctx.fire(PDa, 16))
        ctx.emit("scalar", (lambda g2=g2: lambda e: e.activation(
            out=asb0[:, g2, :], in_=alr[:, g2, :], func=AF.Exp))(),
            waits=((PDd, 16 * (dg + 1)), (PDu, 16 * (dg + 1)),
                   (PDa, 16 * (dg + 1)), (PV, max(0, dg - 1))),
            fire=ctx.fire(PA, 1))
        ctx.emit("vector", (lambda g2=g2: lambda e: e.tensor_single_scalar(
            out=asb[:, g2, :], in_=asb0[:, g2, :], scalar=-1.0, op=ALU.mult))(),
            waits=((PA, dg + 1),) + ((A, 3 * nb * (dg - 1)),) if dg >= 2 else ((PA, dg + 1),),
            fire=ctx.fire(PV, 1))
        ctx.emit("gpsimd", (lambda g2=g2: lambda e: e.memset(yac[:, g2, :], 0.0))(),
            waits=tuple(([("q", q0)] if dg < 2 else [])
                        + [((YS0 if g2 == 0 else YS1), 16 * (dg // 2))]),
            fire=ctx.fire(PP, 1))

        for n in range(nb):
            i = dg * nb + n
            p = i % 2
            ctx.emit("sync", (lambda n=n, p=p: lambda e: e.dma_start(
                out=brow[:, p, :], in_=Bm[n:n + 1, :]))(),
                waits=tuple(qw(i) + ([(P, i - 1)] if i >= 2 else [])),
                fire=ctx.fire(Wb + str(p), 16))
            ctx.emit("sync", (lambda n=n, p=p: lambda e: e.dma_start(
                out=crow[:, p, :], in_=Cm[n:n + 1, :]))(),
                waits=tuple(qw(i) + ([(P, i - 1)] if i >= 2 else [])),
                fire=ctx.fire(Wc + str(p), 16))
            for fc in range(4):
                wts = ([(Wb + str(p), 16 * (i // 2 + 1)),
                        (Wc + str(p), 16 * (i // 2 + 1)), (A, 3 * i)]
                       if fc == 0 else [])
                ctx.emit("tensor", (lambda p=p, fc=fc: lambda e: e.matmul(
                    out=psB[:, fc, :], lhsT=ones_row[:],
                    rhs=brow[:, p, fc * 512:(fc + 1) * 512],
                    start=True, stop=True))(),
                    waits=tuple(qw(i) + wts))
            for fc in range(4):
                ctx.emit("tensor", (lambda p=p, fc=fc: lambda e: e.matmul(
                    out=psC[:, fc, :], lhsT=ones_row[:],
                    rhs=crow[:, p, fc * 512:(fc + 1) * 512],
                    start=True, stop=True))(),
                    fire=ctx.fire(P, 1) if fc == 3 else None)
            ctx.emit("scalar", (lambda g2=g2, n=n, p=p: lambda e: e.activation(
                out=dat[:, p, :], in_=dl[:, g2, :], func=AF.Exp,
                scale=asb[:, g2, n:n + 1]))(),
                waits=tuple(qw(i) + [(PV, dg + 1)] + ([(V, 3 * (i - 1))] if i >= 2 else [])),
                fire=ctx.fire(A, 1))
            ctx.emit("scalar", (lambda p=p: lambda e: e.activation(
                out=besb[:, p, :],
                in_=psB[:].rearrange("p a f -> p (a f)"),
                func=AF.Identity))(),
                waits=((P, i + 1),), fire=ctx.fire(A, 1))
            ctx.emit("scalar", (lambda p=p: lambda e: e.activation(
                out=cesb[:, p, :],
                in_=psC[:].rearrange("p a f -> p (a f)"),
                func=AF.Identity))(),
                fire=ctx.fire(A, 1))
            ctx.emit("vector", (lambda g2=g2, p=p: lambda e: e.tensor_mul(
                out=dbut[:, p, :], in0=duf[:, g2, :], in1=besb[:, p, :]))(),
                waits=tuple(qw(i) + [(A, 3 * i + 2)]),
                fire=ctx.fire(V, 1))
            ctx.emit("vector", (lambda p=p: lambda e: e.tensor_tensor_scan(
                out=ht[:, p, :], data0=dat[:, p, :], data1=dbut[:, p, :],
                initial=0.0, op0=ALU.mult, op1=ALU.add))(),
                waits=((V, 3 * i + 1),), fire=ctx.fire(V, 1))
            ctx.emit("vector", (lambda p=p: lambda e: e.tensor_mul(
                out=hct[:, p, :], in0=ht[:, p, :], in1=cesb[:, p, :]))(),
                waits=((V, 3 * i + 2), (A, 3 * i + 3)) + (((Y, i - 1),) if i >= 2 else ()),
                fire=ctx.fire(V, 1))
            ctx.emit("gpsimd", (lambda g2=g2, p=p: lambda e: e.tensor_add(
                out=yac[:, g2, :], in0=yac[:, g2, :], in1=hct[:, p, :]))(),
                waits=((V, 3 * i + 3), (Y, i), (PP, dg + 1)),
                fire=ctx.fire(Y, 1))
        ctx.emit("gpsimd", (lambda dg=dg, g2=g2: lambda e: e.dma_start(
            out=y[dg * 128:(dg + 1) * 128, :], in_=yac[:, g2, :]))(),
            waits=((Y, nb * (dg + 1)),),
            fire=ctx.fire(YS0 if g2 == 0 else YS1, 16))
    ctx.emit("vector", lambda e: e.engine_nop(),
             waits=((YS0, 16 * 8), (YS1, 16 * 8)), fire=ctx.fire("q", 1))



def emit_l1out2(nc, ctx, S, io):
    H = T // 2
    KD = D // 128
    tg = "o1_"

    y1, u1, z1 = io["y"], io["u_f"], io["z_bf"]
    dskip, opt, xn1 = io["dskip"], io["opT"], io["xn_bf"]
    f1t, w1s, f2t, w2s = io["f1t_bf"], io["w1s"], io["f2t_bf"], io["w2s"]
    h1t, h1s, h2t, h2s = io["h1t_bf"], io["h1s"], io["h2t_bf"], io["h2s"]
    w3t, w3s, x1 = io["w3t_bf"], io["w3s"], io["x1"]
    attn1, gf, a1, a2, out = io["attn1"], io["gf"], io["a1"], io["a2"], io["out"]

    gbuf = ctx.sbuf("gbuf", [128, 16, T], BF16)
    t0 = ctx.sbuf("t0", [128, T], F32)
    t1 = ctx.sbuf("t1", [128, T], F32)
    zt = ctx.sbuf("zt", [128, T], BF16)
    dcr = ctx.sbuf("dcr", [128, 16], BF16)
    dcol = ctx.sbuf("dcol", [128, 16], F32)
    wm = ctx.sbuf("wm", [128, 2, 16, 128], BF16)
    wm2 = ctx.sbuf("wm2", [128, 2, 16, 128], BF16)
    ab = ctx.sbuf("ab", [128, 2, H], BF16)
    ev0 = ctx.sbuf("ev0", [128, 2, H], BF16)
    ev1 = ctx.sbuf("ev1", [128, 2, H], BF16)
    s1c = ctx.sbuf("s1c", [128, 32], F32)
    s2c = ctx.sbuf("s2c", [128, 32], F32)
    sh1c = ctx.sbuf("sh1c", [128, 32], F32)
    sh2c = ctx.sbuf("sh2c", [128, 32], F32)
    s3c = ctx.sbuf("s3c", [128, 8], F32)
    rq = ctx.sbuf("rq", [128, 2, 32, 512], BF16)
    w32 = ctx.sbuf("w32", [128, 2, 32, 128], BF16)
    fx = ctx.sbuf("fx", [128, 2, 512], F32)
    fa = ctx.sbuf("fa", [128, 2, 512], BF16)
    fo = ctx.sbuf("fo", [128, 2, 512], F32)
    ps = ctx.psA

    q0 = ctx.val("q")
    S.dma(dcr[:], dskip.rearrange("(g p) -> p g", p=128), slow=True)
    S.op("vector", lambda e: e.tensor_copy(out=dcol[:], in_=dcr[:]))
    S.dma(s1c[:], w1s.rearrange("(g p) -> p g", p=128), slow=True)
    S.dma(s2c[:], w2s.rearrange("(g p) -> p g", p=128), slow=True)
    S.dma(sh1c[:], h1s.rearrange("(g p) -> p g", p=128), slow=True)
    S.dma(sh2c[:], h2s.rearrange("(g p) -> p g", p=128), slow=True)
    S.dma(s3c[:], w3s.rearrange("(g p) -> p g", p=128), slow=True)
    q0 = ctx.val("q")

    for g in range(16):
        S.dma(t0[:], y1[g * 128:(g + 1) * 128, :])
        S.dma(t1[:], u1[g * 128:(g + 1) * 128, :])
        S.dma(zt[:], z1[g * 128:(g + 1) * 128, :])
        S.op("vector", (lambda g=g: lambda e: e.scalar_tensor_tensor(
            out=t0[:], in0=t1[:], scalar=dcol[:, g:g + 1],
            in1=t0[:], op0=ALU.mult, op1=ALU.add))())
        S.op("scalar", lambda e: e.activation(out=t1[:], in_=zt[:], func=AF.Silu))
        S.op("vector", (lambda g=g: lambda e: e.tensor_mul(
            out=gbuf[:, g, :], in0=t0[:], in1=t1[:]))())
    q1 = ctx.val("q")

    BL, BP, BE, BS = tg + "bl", tg + "bp", tg + "be", tg + "bs"
    optr = opt.rearrange("(k p) m -> p k m", p=128)
    for m in range(KD):
        pw = m % 2
        ctx.emit("sync", (lambda m=m, pw=pw: lambda e: e.dma_start(
            out=wm[:, pw, :, :], in_=optr[:, :, m * 128:(m + 1) * 128]))(),
            waits=tuple([("q", q1)] + ([(BP, 2 * (m - 1))] if m >= 2 else [])),
            fire=ctx.fire(BL + str(pw), 16))
        for th in range(2):
            it = m * 2 + th
            pb = it % 2
            sl = pb * 2
            for k in range(16):
                for j in range(2):
                    wts = []
                    if k == 0 and j == 0:
                        wts = [(BL + str(pw), 16 * (m // 2 + 1)), ("q", q1)]
                        if it >= 2:
                            wts.append((BE, it - 1))
                    ctx.emit("tensor", (lambda k=k, th=th, j=j, sl=sl, pw=pw: lambda e: e.matmul(
                        out=ps[:, sl + j, :], lhsT=wm[:, pw, k, :],
                        rhs=gbuf[:, k, th * H + j * 512:th * H + (j + 1) * 512],
                        start=(k == 0), stop=(k == 15)))(),
                        waits=tuple(wts),
                        fire=ctx.fire(BP, 1) if (k, j) == (15, 1) else None)
            ctx.emit("vector", (lambda it=it, sl=sl: lambda e: e.tensor_copy(
                out=ab[:, it % 2, :],
                in_=ps[:, sl:sl + 2, :].rearrange("p a f -> p (a f)")))(),
                waits=tuple([(BP, it + 1)]
                            + ([(BS + str(pb), 16 * (it // 2))] if it >= 2 else [])),
                fire=ctx.fire(BE, 1))
            ctx.emit("gpsimd", (lambda m=m, th=th, it=it, pb=pb: lambda e: e.dma_start(
                out=attn1[m * 128:(m + 1) * 128, th * H:(th + 1) * H],
                in_=ab[:, it % 2, :]))(),
                waits=((BE, it + 1),), fire=ctx.fire(BS + str(pb), 16))

    CL, CP, CEa, CEv, CS, CX = (tg + "cl", tg + "cp", tg + "cea", tg + "cev",
                                tg + "cs", tg + "cx")
    f1r = f1t.rearrange("(k p) m -> p k m", p=128)
    f2r = f2t.rearrange("(k p) m -> p k m", p=128)
    for k in range(KD):
        ctx.emit("sync", (lambda k=k: lambda e: e.dma_start(
            out=gbuf[:, k, :], in_=xn1[k * 128:(k + 1) * 128, :]))(),
            waits=((BP, 16),) if k == 0 else ((CX, 16 * k),),
            fire=ctx.fire(CX, 16))
    for m in range(32):
        pw = m % 2
        w = [(CP, 2 * (m - 1))] if m >= 2 else [(BP, 14 + 2 * m)]
        ctx.emit("sync", (lambda m=m, pw=pw: lambda e: e.dma_start(
            out=wm[:, pw, 0:KD, :], in_=f1r[:, :, m * 128:(m + 1) * 128]))(),
            waits=tuple(w), fire=ctx.fire(CL + "a" + str(pw), 16))
        ctx.emit("sync", (lambda m=m, pw=pw: lambda e: e.dma_start(
            out=wm2[:, pw, 0:KD, :], in_=f2r[:, :, m * 128:(m + 1) * 128]))(),
            waits=tuple(w), fire=ctx.fire(CL + "b" + str(pw), 16))
        for th in range(2):
            it = m * 2 + th
            for k in range(KD):
                for j in range(2):
                    wts = []
                    if k == 0 and j == 0:
                        wts = [(CL + "a" + str(pw), 16 * (m // 2 + 1)),
                               (CL + "b" + str(pw), 16 * (m // 2 + 1)),
                               (CX, 16 * KD)]
                        if it >= 1:
                            wts.append((CEa, 2 * it))
                        else:
                            wts.append((BE, 16))
                    ctx.emit("tensor", (lambda k=k, th=th, j=j, pw=pw: lambda e: e.matmul(
                        out=ps[:, j, :], lhsT=wm[:, pw, k, :],
                        rhs=gbuf[:, k, th * H + j * 512:th * H + (j + 1) * 512],
                        start=(k == 0), stop=(k == KD - 1)))(),
                        waits=tuple(wts))
            for k in range(KD):
                for j in range(2):
                    ctx.emit("tensor", (lambda k=k, th=th, j=j, pw=pw: lambda e: e.matmul(
                        out=ps[:, 2 + j, :], lhsT=wm2[:, pw, k, :],
                        rhs=gbuf[:, k, th * H + j * 512:th * H + (j + 1) * 512],
                        start=(k == 0), stop=(k == KD - 1)))(),
                        fire=ctx.fire(CP, 1) if (k, j) == (KD - 1, 1) else None)
            pb = it % 2
            ctx.emit("scalar", (lambda m=m, pb=pb: lambda e: e.activation(
                out=ev0[:, pb, :], in_=ps[:, 0:2, :].rearrange("p a f -> p (a f)"),
                func=AF.Silu, scale=s1c[:, m:m + 1]))(),
                waits=tuple([(CP, it + 1)] + ([(CEv, it - 1)] if it >= 2 else [])),
                fire=ctx.fire(CEa, 1))
            ctx.emit("scalar", (lambda m=m, pb=pb: lambda e: e.activation(
                out=ev1[:, pb, :], in_=ps[:, 2:4, :].rearrange("p a f -> p (a f)"),
                func=AF.Identity, scale=s2c[:, m:m + 1]))(),
                waits=((CEa, 2 * it + 1),), fire=ctx.fire(CEa, 1))
            ctx.emit("vector", (lambda pb=pb: lambda e: e.tensor_mul(
                out=ab[:, pb, :], in0=ev0[:, pb, :], in1=ev1[:, pb, :]))(),
                waits=tuple([(CEa, 2 * it + 2)]
                            + ([(CS + str(pb), 16 * (it // 2))] if it >= 2
                               else [(BS + str(pb), 16 * 8)])),
                fire=ctx.fire(CEv, 1))
            ctx.emit("gpsimd", (lambda m=m, th=th, pb=pb: lambda e: e.dma_start(
                out=gf[m * 128:(m + 1) * 128, th * H:(th + 1) * H],
                in_=ab[:, pb, :]))(),
                waits=((CEv, it + 1),), fire=ctx.fire(CS + str(pb), 16))

    def hstage(st, src, dst, wmat, scol, prev_done):
        L, R, P, E = st + "l", st + "r", st + "p", st + "e"
        wr = wmat.rearrange("(k p) m -> p k m", p=128)
        srcr = src.rearrange("(k p) m -> p k m", p=128)
        for f4 in range(4):
            pf = f4 % 2
            w = list(prev_done) if f4 == 0 else []
            if f4 >= 2:
                w.append((P, 32 * (f4 - 1)))
            ctx.emit("sync", (lambda f4=f4, pf=pf: lambda e: e.dma_start(
                out=rq[:, pf, :, :],
                in_=srcr[:, :, f4 * 512:(f4 + 1) * 512]))(),
                waits=tuple(w), fire=ctx.fire(R + str(pf), 16))
            for m in range(32):
                it = f4 * 32 + m
                pw = it % 2
                w = list(prev_done) if it == 0 else []
                if it >= 2:
                    w.append((P, it - 1))
                ctx.emit("scalar", (lambda it=it, m=m, pw=pw: lambda e: e.dma_start(
                    out=w32[:, pw, :, :], in_=wr[:, :, m * 128:(m + 1) * 128]))(),
                    waits=tuple(w), fire=ctx.fire(L + str(pw), 16))
                for k in range(32):
                    wts = []
                    if k == 0:
                        wts = [(L + str(pw), 16 * (it // 2 + 1)),
                               (R + str(pf), 16 * (f4 // 2 + 1))]
                        if it >= 2:
                            wts.append((E, it - 1))
                    ctx.emit("tensor", (lambda k=k, pw=pw, pf=pf, it=it: lambda e: e.matmul(
                        out=ps[:, it % 4, :], lhsT=w32[:, pw, k, :],
                        rhs=rq[:, pf, k, :], start=(k == 0), stop=(k == 31)))(),
                        waits=tuple(wts),
                        fire=ctx.fire(P, 1) if k == 31 else None)
                ctx.emit("scalar", (lambda it=it, m=m: lambda e: e.activation(
                    out=fa[:, it % 2, :], in_=ps[:, it % 4, :], func=AF.Gelu,
                    scale=scol[:, m:m + 1]))(),
                    waits=tuple([(P, it + 1)]
                                + ([(st + "s" + str(it % 2), 16 * (it // 2))]
                                   if it >= 2 else [])),
                    fire=ctx.fire(E, 1))
                ctx.emit("gpsimd", (lambda it=it, m=m, f4=f4: lambda e: e.dma_start(
                    out=dst[m * 128:(m + 1) * 128, f4 * 512:(f4 + 1) * 512],
                    in_=fa[:, it % 2, :]))(),
                    waits=((E, it + 1),), fire=ctx.fire(st + "s" + str(it % 2), 16))
        return [(st + "s0", 16 * 64), (st + "s1", 16 * 64)]

    dD = hstage(tg + "D", gf, a1, h1t, sh1c, [(CS + "0", 16 * 32), (CS + "1", 16 * 32)])
    dE = hstage(tg + "E", a1, a2, h2t, sh2c, dD)

    FL, FR, FP, FEa, FEv, FS = (tg + "fl", tg + "fr", tg + "fp", tg + "fea",
                                tg + "fev", tg + "fs")
    w3r = w3t.rearrange("(k p) m -> p k m", p=128)
    a2r = a2.rearrange("(k p) m -> p k m", p=128)
    for f4 in range(4):
        pf = f4 % 2
        w = (dE + [(BS + "0", 16 * 8), (BS + "1", 16 * 8)]) if f4 == 0 else []
        if f4 >= 2:
            w.append((FP, 8 * (f4 - 1)))
        ctx.emit("sync", (lambda f4=f4, pf=pf: lambda e: e.dma_start(
            out=rq[:, pf, :, :],
            in_=a2r[:, :, f4 * 512:(f4 + 1) * 512]))(),
            waits=tuple(w), fire=ctx.fire(FR + str(pf), 16))
        for m in range(KD):
            it = f4 * KD + m
            pw = it % 2
            w = list(dE) if it == 0 else []
            if it >= 2:
                w.append((FP, it - 1))
            ctx.emit("scalar", (lambda it=it, m=m, pw=pw: lambda e: e.dma_start(
                out=w32[:, pw, :, :], in_=w3r[:, :, m * 128:(m + 1) * 128]))(),
                waits=tuple(w), fire=ctx.fire(FL + "w" + str(pw), 16))
            ctx.emit("sync", (lambda it=it, m=m, f4=f4, pw=pw: lambda e: e.dma_start(
                out=fx[:, pw, :],
                in_=x1[m * 128:(m + 1) * 128, f4 * 512:(f4 + 1) * 512]))(),
                waits=((FEv, 2 * (it - 1)),) if it >= 2 else (),
                fire=ctx.fire(FL + "x" + str(pw), 16))
            ctx.emit("sync", (lambda it=it, m=m, f4=f4, pw=pw: lambda e: e.dma_start(
                out=fa[:, pw, :],
                in_=attn1[m * 128:(m + 1) * 128, f4 * 512:(f4 + 1) * 512]))(),
                waits=((FEv, 2 * (it - 1)),) if it >= 2 else (),
                fire=ctx.fire(FL + "a" + str(pw), 16))
            for k in range(32):
                wts = []
                if k == 0:
                    wts = [(FL + "w" + str(pw), 16 * (it // 2 + 1)),
                           (FR + str(pf), 16 * (f4 // 2 + 1))]
                    if it >= 2:
                        wts.append((FEa, it - 1))
                ctx.emit("tensor", (lambda k=k, pw=pw, pf=pf, it=it: lambda e: e.matmul(
                    out=ps[:, it % 4, :], lhsT=w32[:, pw, k, :],
                    rhs=rq[:, pf, k, :], start=(k == 0), stop=(k == 31)))(),
                    waits=tuple(wts),
                    fire=ctx.fire(FP, 1) if k == 31 else None)
            ctx.emit("scalar", (lambda it=it, m=m, pw=pw: lambda e: e.activation(
                out=fo[:, pw, :], in_=ps[:, it % 4, :], func=AF.Identity,
                scale=s3c[:, m:m + 1]))(),
                waits=tuple([(FP, it + 1)]
                            + ([(FS + str(pw), 16 * (it // 2))] if it >= 2 else [])),
                fire=ctx.fire(FEa, 1))
            ctx.emit("vector", (lambda pw=pw: lambda e: e.tensor_add(
                out=fx[:, pw, :], in0=fx[:, pw, :], in1=fa[:, pw, :]))(),
                waits=((FL + "x" + str(pw), 16 * (it // 2 + 1)),
                       (FL + "a" + str(pw), 16 * (it // 2 + 1))),
                fire=ctx.fire(FEv, 1))
            ctx.emit("vector", (lambda pw=pw: lambda e: e.tensor_add(
                out=fo[:, pw, :], in0=fo[:, pw, :], in1=fx[:, pw, :]))(),
                waits=((FEa, it + 1), (FEv, 2 * it + 1)),
                fire=ctx.fire(FEv, 1))
            ctx.emit("gpsimd", (lambda it=it, m=m, f4=f4, pw=pw: lambda e: e.dma_start(
                out=out[m * 128:(m + 1) * 128, f4 * 512:(f4 + 1) * 512],
                in_=fo[:, pw, :]))(),
                waits=((FEv, 2 * it + 2),), fire=ctx.fire(FS + str(pw), 16))
    ctx.emit("vector", lambda e: e.engine_nop(),
             waits=((FS + "0", 16 * 16), (FS + "1", 16 * 16),
                    (BS + "0", 16 * 8), (BS + "1", 16 * 8)),
             fire=ctx.fire("q", 1))



def emit_l0out2(nc, ctx, S, io):
    H = T // 2
    KD = D // 128
    tg = "o0_"

    y, u_f, z_bf, d0 = io["y"], io["u_f"], io["z_bf"], io["d0"]
    wout, xn_f, xn_bf = io["wout_bf"], io["xn_f"], io["xn_bf"]
    w1, b1, w2, b2, x = io["w1_bf"], io["b1"], io["w2_bf"], io["b2"], io["x"]
    attn0, h_bf, x1 = io["attn0"], io["h_bf"], io["x1"]

    gbuf = ctx.sbuf("gbuf", [128, 16, T], BF16)
    xnb = ctx.sbuf("xnb", [128, KD, T], BF16)
    t0 = ctx.sbuf("t0", [128, T], F32)
    t1 = ctx.sbuf("t1", [128, T], F32)
    zt = ctx.sbuf("zt", [128, T], BF16)
    dcol = ctx.sbuf("dcol", [128, 16], F32)
    wom = ctx.sbuf("wom", [128, 2, 16, 128], BF16)
    w1m = ctx.sbuf("w1m", [128, 2, KD, 128], BF16)
    b1c = ctx.sbuf("b1c", [128, 32], F32)
    b2c = ctx.sbuf("b2c", [128, 8], F32)
    knot = ctx.sbuf("knot", [128, 18], F32)
    xnt = ctx.sbuf("xnt", [128, 2, H], F32)
    abf = ctx.sbuf("abf", [128, 2, H], F32)
    s0b = ctx.sbuf("s0b", [128, 2, H], F32)
    qb = ctx.sbuf("qb", [128, 2, H], F32)
    vb = ctx.sbuf("vb", [128, 2, H], F32)
    acc = ctx.sbuf("acc", [128, 2, H], F32)
    rr = ctx.sbuf("rr", [128, 2, H], F32)
    hb = ctx.sbuf("hb", [128, 2, H], BF16)
    fx = ctx.sbuf("fx", [128, 2, 512], F32)
    fa = ctx.sbuf("fa", [128, 2, 512], F32)
    fo = ctx.sbuf("fo", [128, 2, 512], F32)
    ps = ctx.psA

    sgn = []
    for i, (vk, ck) in enumerate(PWL_KNOTS):
        S.op("vector", (lambda i=i, vk=vk, ck=ck: lambda e: e.memset(
            knot[:, i:i + 1], -abs(ck) * vk))())
        sgn.append(1.0 if ck >= 0 else -1.0)
    S.op("vector", lambda e: e.memset(knot[:, 16:17], -1024.0))
    S.dma(dcol[:], d0.rearrange("(g p) -> p g", p=128), slow=True)
    S.dma(b1c[:], b1.rearrange("(g p) -> p g", p=128), slow=True)
    S.dma(b2c[:], b2.rearrange("(g p) -> p g", p=128), slow=True)
    S.op("vector", lambda e: e.tensor_single_scalar(
        out=b2c[:], in_=b2c[:], scalar=256.0, op=ALU.mult))

    for g in range(16):
        S.dma(t0[:], y[g * 128:(g + 1) * 128, :])
        S.dma(t1[:], u_f[g * 128:(g + 1) * 128, :])
        S.dma(zt[:], z_bf[g * 128:(g + 1) * 128, :])
        S.op("vector", (lambda g=g: lambda e: e.scalar_tensor_tensor(
            out=t0[:], in0=t1[:], scalar=dcol[:, g:g + 1],
            in1=t0[:], op0=ALU.mult, op1=ALU.add))())
        S.op("scalar", lambda e: e.activation(out=t1[:], in_=zt[:], func=AF.Silu))
        S.op("vector", (lambda g=g: lambda e: e.tensor_mul(
            out=gbuf[:, g, :], in0=t0[:], in1=t1[:]))())
    for k in range(KD):
        S.dma(xnb[:, k, :], xn_bf[k * 128:(k + 1) * 128, :])
    q1 = ctx.val("q")

    BL, BX, BP, BE, BS = tg + "bl", tg + "bx", tg + "bp", tg + "be", tg + "bs"
    womr = wout.rearrange("(k p) m -> p k m", p=128)
    for m in range(KD):
        pw = m % 2
        ctx.emit("sync", (lambda m=m, pw=pw: lambda e: e.dma_start(
            out=wom[:, pw, :, :], in_=womr[:, :, m * 128:(m + 1) * 128]))(),
            waits=tuple([("q", q1)] + ([(BP, 2 * (m - 1))] if m >= 2 else [])),
            fire=ctx.fire(BL + str(pw), 16))
        for th in range(2):
            it = m * 2 + th
            pb = it % 2
            sl = pb * 2
            ctx.emit("sync", (lambda m=m, th=th, pb=pb: lambda e: e.dma_start(
                out=xnt[:, pb, :],
                in_=xn_f[m * 128:(m + 1) * 128, th * H:(th + 1) * H]))(),
                waits=((BE, 2 * (it - 1)),) if it >= 2 else (),
                fire=ctx.fire(BX + str(pb), 16))
            for k in range(16):
                for j in range(2):
                    wts = []
                    if k == 0 and j == 0:
                        wts = [(BL + str(pw), 16 * (m // 2 + 1)), ("q", q1)]
                        if it >= 2:
                            wts.append((BE, 2 * (it - 1)))
                    ctx.emit("tensor", (lambda k=k, th=th, j=j, sl=sl, pw=pw: lambda e: e.matmul(
                        out=ps[:, sl + j, :], lhsT=wom[:, pw, k, :],
                        rhs=gbuf[:, k, th * H + j * 512:th * H + (j + 1) * 512],
                        start=(k == 0), stop=(k == 15)))(),
                        waits=tuple(wts),
                        fire=ctx.fire(BP, 1) if (k, j) == (15, 1) else None)
            ctx.emit("vector", (lambda it=it, pb=pb, sl=sl: lambda e: e.tensor_add(
                out=abf[:, pb, :], in0=xnt[:, pb, :],
                in1=ps[:, sl:sl + 2, :].rearrange("p a f -> p (a f)")))(),
                waits=tuple([(BP, it + 1), (BX + str(pb), 16 * (it // 2 + 1))]
                            + ([(BS + str(pb), 16 * (it // 2))] if it >= 2 else [])),
                fire=ctx.fire(BE, 2))
            ctx.emit("gpsimd", (lambda m=m, th=th, pb=pb: lambda e: e.dma_start(
                out=attn0[m * 128:(m + 1) * 128, th * H:(th + 1) * H],
                in_=abf[:, pb, :]))(),
                waits=((BE, 2 * it + 2),), fire=ctx.fire(BS + str(pb), 16))

    CL, CP, CA, CV, CS = tg + "cl", tg + "cp", tg + "ca", tg + "cv", tg + "cs"
    w1r = w1.rearrange("(k p) m -> p k m", p=128)
    NA, NV = 18, 24
    for m in range(32):
        pw = m % 2
        ctx.emit("sync", (lambda m=m, pw=pw: lambda e: e.dma_start(
            out=w1m[:, pw, :, :], in_=w1r[:, :, m * 128:(m + 1) * 128]))(),
            waits=tuple([("q", q1)] + ([(CP, 2 * (m - 1))] if m >= 2 else [])),
            fire=ctx.fire(CL + str(pw), 16))
        for th in range(2):
            it = m * 2 + th
            p = it % 2
            va = NV * it
            aa = NA * it
            for k in range(KD):
                for j in range(2):
                    wts = []
                    if k == 0 and j == 0:
                        wts = [(CL + str(pw), 16 * (m // 2 + 1)), ("q", q1)]
                        if it >= 1:
                            wts.append((CV, NV * (it - 1) + 1))
                        else:
                            wts.append((BE, 32))
                    ctx.emit("tensor", (lambda k=k, th=th, j=j, p=p, pw=pw: lambda e: e.matmul(
                        out=ps[:, p * 2 + j, :], lhsT=w1m[:, pw, k, :],
                        rhs=xnb[:, k, th * H + j * 512:th * H + (j + 1) * 512],
                        start=(k == 0), stop=(k == KD - 1)))(),
                        waits=tuple(wts),
                        fire=ctx.fire(CP, 1) if (k, j) == (KD - 1, 1) else None)
            psv = ps[:, p * 2:p * 2 + 2, :].rearrange("p a f -> p (a f)")
            ctx.emit("vector", (lambda m=m, p=p, psv=psv: lambda e: e.tensor_scalar(
                out=s0b[:, p, :], in0=psv, scalar1=b1c[:, m:m + 1],
                scalar2=256.0, op0=ALU.add, op1=ALU.mult))(),
                waits=tuple([(CP, it + 1)] + ([(CV, NV * (it - 1) + 2)] if it >= 1 else [])),
                fire=ctx.fire(CV, 1))
            ctx.emit("vector", (lambda p=p: lambda e: e.tensor_scalar(
                out=qb[:, p, :], in0=s0b[:, p, :], scalar1=RNE_C,
                scalar2=RNE_C, op0=ALU.add, op1=ALU.subtract))(),
                waits=((CV, va + 1),) + (((CV, NV * (it - 1) + NV - 1),) if it >= 1 else ()),
                fire=ctx.fire(CV, 1))
            ctx.emit("scalar", (lambda p=p: lambda e: e.activation(
                out=vb[:, p, :], in_=qb[:, p, :], func=AF.Abs))(),
                waits=((CV, va + 2),) + (((CA, NA * (it - 1) + NA),) if it >= 1 else ()),
                fire=ctx.fire(CA, 1))
            ctx.emit("vector", (lambda p=p: lambda e: e.tensor_single_scalar(
                out=acc[:, p, :], in_=vb[:, p, :], scalar=PWL_S0, op=ALU.mult))(),
                waits=((CA, aa + 1),), fire=ctx.fire(CV, 1))
            for kk, (vk, ck) in enumerate(PWL_KNOTS):
                rp = kk % 2
                w = [(CA, aa + 1)]
                if kk >= 2:
                    w = [(CV, va + 3 + kk - 1)]
                ctx.emit("scalar", (lambda kk=kk, ck=ck, p=p, rp=rp: lambda e: e.activation(
                    out=rr[:, rp, :], in_=vb[:, p, :], func=AF.Relu,
                    scale=abs(ck), bias=knot[:, kk:kk + 1]))(),
                    waits=tuple(w), fire=ctx.fire(CA, 1))
                ctx.emit("vector", (lambda kk=kk, p=p, rp=rp: lambda e: e.scalar_tensor_tensor(
                    out=acc[:, p, :], in0=rr[:, rp, :], scalar=sgn[kk],
                    in1=acc[:, p, :], op0=ALU.mult, op1=ALU.add))(),
                    waits=((CA, aa + 2 + kk), (CV, va + 3 + kk)),
                    fire=ctx.fire(CV, 1))
            ctx.emit("scalar", (lambda p=p: lambda e: e.activation(
                out=rr[:, 0, :], in_=vb[:, p, :], func=AF.Relu,
                bias=knot[:, 16:17]))(),
                waits=((CV, va + 3 + 15),), fire=ctx.fire(CA, 1))
            ctx.emit("vector", (lambda p=p: lambda e: e.tensor_scalar(
                out=rr[:, 0, :], in0=rr[:, 0, :], scalar1=1.0,
                scalar2=PWL_J256, op0=ALU.min, op1=ALU.mult))(),
                waits=((CA, aa + NA),), fire=ctx.fire(CV, 1))
            ctx.emit("vector", (lambda p=p: lambda e: e.tensor_add(
                out=acc[:, p, :], in0=acc[:, p, :], in1=rr[:, 0, :]))(),
                waits=((CV, va + 20),), fire=ctx.fire(CV, 1))
            ctx.emit("vector", (lambda p=p: lambda e: e.scalar_tensor_tensor(
                out=s0b[:, p, :], in0=qb[:, p, :], scalar=0.5,
                in1=acc[:, p, :], op0=ALU.mult, op1=ALU.add))(),
                waits=((CV, va + 21),), fire=ctx.fire(CV, 1))
            ctx.emit("vector", (lambda p=p: lambda e: e.tensor_scalar(
                out=qb[:, p, :], in0=s0b[:, p, :], scalar1=RNE_C,
                scalar2=RNE_C, op0=ALU.add, op1=ALU.subtract))(),
                waits=((CV, va + 22),), fire=ctx.fire(CV, 1))
            ctx.emit("vector", (lambda p=p: lambda e: e.tensor_copy(
                out=hb[:, p, :], in_=qb[:, p, :]))(),
                waits=tuple([(CV, va + 23)]
                            + ([(CS + str(p), 16 * (it // 2))] if it >= 2 else [])),
                fire=ctx.fire(CV, 1))
            ctx.emit("gpsimd", (lambda m=m, th=th, p=p: lambda e: e.dma_start(
                out=h_bf[m * 128:(m + 1) * 128, th * H:(th + 1) * H],
                in_=hb[:, p, :]))(),
                waits=((CV, va + 24),), fire=ctx.fire(CS + str(p), 16))

    DL, DR, DP, DE, DS = tg + "dl", tg + "dr", tg + "dp", tg + "de", tg + "ds"
    w2r = w2.rearrange("(k p) m -> p k m", p=128)

    def rqv(pf, k):
        fl = pf * 32 + k
        return gbuf[:, fl // 4, (fl % 4) * 512:(fl % 4) * 512 + 512]

    def w2v(pw):
        return xnb[:, pw * 2:pw * 2 + 2, :].rearrange(
            "p a t -> p (a t)").rearrange("p (k m) -> p k m", m=128)

    hbr = h_bf.rearrange("(k p) m -> p k m", p=128)
    for f4 in range(4):
        pf = f4 % 2
        w = ([(CS + "0", 16 * 32), (CS + "1", 16 * 32),
              (BS + "0", 16 * 8), (BS + "1", 16 * 8)] if f4 == 0 else [])
        if f4 >= 2:
            w.append((DP, 8 * (f4 - 1)))
        ctx.emit("sync", (lambda f4=f4, pf=pf: lambda e: e.dma_start(
            out=gbuf[:, pf * 8:pf * 8 + 8, :].rearrange("p a t -> p (a t)"),
            in_=hbr[:, :, f4 * 512:(f4 + 1) * 512]))(),
            waits=tuple(w), fire=ctx.fire(DR + str(pf), 16))
        for m in range(KD):
            it = f4 * KD + m
            pw = it % 2
            w = ([(CS + "0", 16 * 32), (CS + "1", 16 * 32)] if it < 2 else
                 [(DP, it - 1)])
            ctx.emit("scalar", (lambda it=it, m=m, pw=pw: lambda e: e.dma_start(
                out=w2v(pw), in_=w2r[:, :, m * 128:(m + 1) * 128]))(),
                waits=tuple(w), fire=ctx.fire(DL + "w" + str(pw), 16))
            ctx.emit("sync", (lambda it=it, m=m, f4=f4, pw=pw: lambda e: e.dma_start(
                out=fx[:, pw, :],
                in_=x[m * 128:(m + 1) * 128, f4 * 512:(f4 + 1) * 512]))(),
                waits=((DE, 4 * (it - 1)),) if it >= 2 else (),
                fire=ctx.fire(DL + "x" + str(pw), 16))
            ctx.emit("sync", (lambda it=it, m=m, f4=f4, pw=pw: lambda e: e.dma_start(
                out=fa[:, pw, :],
                in_=attn0[m * 128:(m + 1) * 128, f4 * 512:(f4 + 1) * 512]))(),
                waits=((DE, 4 * (it - 1)),) if it >= 2 else (),
                fire=ctx.fire(DL + "a" + str(pw), 16))
            for k in range(32):
                wts = []
                if k == 0:
                    wts = [(DL + "w" + str(pw), 16 * (it // 2 + 1)),
                           (DR + str(pf), 16 * (f4 // 2 + 1))]
                    if it >= 2:
                        wts.append((DE, 4 * (it - 1) + 1))
                ctx.emit("tensor", (lambda k=k, pw=pw, pf=pf, it=it: lambda e: e.matmul(
                    out=ps[:, it % 4, :], lhsT=w2v(pw)[:, k, :],
                    rhs=rqv(pf, k), start=(k == 0), stop=(k == 31)))(),
                    waits=tuple(wts),
                    fire=ctx.fire(DP, 1) if k == 31 else None)
            ctx.emit("vector", (lambda it=it, m=m, pw=pw: lambda e: e.tensor_scalar(
                out=fo[:, pw, :], in0=ps[:, it % 4, :], scalar1=b2c[:, m:m + 1],
                scalar2=RNE_C, op0=ALU.add, op1=ALU.add))(),
                waits=tuple([(DP, it + 1),
                             (DL + "x" + str(pw), 16 * (it // 2 + 1)),
                             (DL + "a" + str(pw), 16 * (it // 2 + 1))]
                            + ([(DS + str(pw), 16 * (it // 2))] if it >= 2 else [])),
                fire=ctx.fire(DE, 1))
            ctx.emit("vector", (lambda pw=pw: lambda e: e.tensor_scalar(
                out=fo[:, pw, :], in0=fo[:, pw, :], scalar1=RNE_C,
                scalar2=1.0 / 256.0, op0=ALU.subtract, op1=ALU.mult))(),
                waits=((DE, 4 * it + 1),), fire=ctx.fire(DE, 1))
            ctx.emit("vector", (lambda pw=pw: lambda e: e.tensor_add(
                out=fx[:, pw, :], in0=fx[:, pw, :], in1=fa[:, pw, :]))(),
                waits=((DE, 4 * it + 1),), fire=ctx.fire(DE, 1))
            ctx.emit("vector", (lambda pw=pw: lambda e: e.tensor_add(
                out=fo[:, pw, :], in0=fo[:, pw, :], in1=fx[:, pw, :]))(),
                waits=((DE, 4 * it + 3),),
                fire=ctx.fire(DE, 1))
            ctx.emit("gpsimd", (lambda it=it, m=m, f4=f4, pw=pw: lambda e: e.dma_start(
                out=x1[m * 128:(m + 1) * 128, f4 * 512:(f4 + 1) * 512],
                in_=fo[:, pw, :]))(),
                waits=((DE, 4 * it + 4),), fire=ctx.fire(DS + str(pw), 16))
    ctx.emit("vector", lambda e: e.engine_nop(),
             waits=((DS + "0", 16 * 16), (DS + "1", 16 * 16)),
             fire=ctx.fire("q", 1))


def emit_lxa2(nc, ctx, S, io, layer, stop=None):
    KD = D // 128
    H = T // 2
    NST = 64 if layer == 0 else 16
    MX = 192 if layer == 0 else 96
    tg = "ax_"

    x = io["x"]
    win = io["win_bf"]
    convw, convb = io["convw"], io["convb"]
    wxp, wdt, bdt = io["wxp_bf"], io["wdt"], io["bdt"]
    if layer == 0:
        lnw, lnb, xn_f = io["lnw"], io["lnb"], io["xn_f"]
    xn_bf, xi_bf, z_bf = io["xn_bf"], io["xi_bf"], io["z_bf"]
    u_f, u_bf = io["u_f"], io["u_bf"]
    B0, C0, delta, du, du_f16 = io["B0"], io["C0"], io["delta"], io["du"], io["du_f16"]

    xt = ctx.sbuf("xt", [128, 2, H], F32)
    sq = ctx.sbuf("sq", [128, 2, H], F32)
    ones_col = ctx.sbuf("ones_col", [128, 1], F32)
    ones_row = ctx.sbuf("ones_row", [1, 128], F32)
    rows = ctx.sbuf("rows", [1, 4, T], F32)
    rep_a = ctx.sbuf("rep_a", [128, T], F32)
    rep_b = ctx.sbuf("rep_b", [128, T], F32)
    xbf = ctx.sbuf("xbf", [128, 2, H], BF16)
    xn2 = ctx.sbuf("xn2", [128, KD, T], BF16)
    wm = ctx.sbuf("wm", [128, 2, KD, 128], BF16)
    ebf = ctx.sbuf("ebf", [128, 2, H], BF16)
    ct = ctx.sbuf("ct", [128, 2, T + 3], BF16)
    cwr = ctx.sbuf("cwr", [128, 16, 4], F32 if layer == 0 else BF16)
    cbr = ctx.sbuf("cbr", [128, 16], F32 if layer == 0 else BF16)
    cw = ctx.sbuf("cw", [128, 16, 4], F32)
    cb = ctx.sbuf("cb", [128, 16], F32)
    ua = ctx.sbuf("ua", [128, 2, T], F32)
    ub = ctx.sbuf("ub", [128, 2, T], BF16)
    wxpt = ctx.sbuf("wxpt", [128, 16, MX], BF16)
    uh = ctx.sbuf("uh", [128, 2, H], BF16)
    dtr = ctx.sbuf("dtr", [64, T], BF16)
    wdtf = ctx.sbuf("wdtf", [64, DI], F32 if layer == 0 else BF16)
    wdtb = ctx.sbuf("wdtb", [64, DI], BF16)
    bdtr = ctx.sbuf("bdtr", [128, 16], F32 if layer == 0 else BF16)
    bdtc = ctx.sbuf("bdtc", [128, 16], F32)
    tb = ctx.sbuf("tb", [128, 2, 4, H], F32)
    duh = ctx.sbuf("duh", [128, 2, H], F16)
    lncol = ctx.sbuf("lncol", [128, 2, KD], F32)
    ps_big, ps_sm = ctx.psA, ctx.psB[0:64]
    MU, SQS, RSTD, SCR = range(4)

    S.op("vector", lambda e: e.memset(ones_col[:], 1.0))
    S.op("vector", lambda e: e.memset(ones_row[:], 1.0))
    cst = ConstCols(S, [1e-5])
    if layer == 0:
        S.dma(lncol[:, 0, :], lnw.rearrange("(k p) -> p k", p=128), slow=True)
        S.dma(lncol[:, 1, :], lnb.rearrange("(k p) -> p k", p=128), slow=True)
    S.dma(cwr[:], convw.rearrange("(g p) k -> p g k", p=128), slow=True)
    S.dma(cbr[:], convb.rearrange("(g p) -> p g", p=128), slow=True)
    S.op("vector", lambda e: e.tensor_copy(out=cw[:], in_=cwr[:]))
    S.op("vector", lambda e: e.tensor_copy(out=cb[:], in_=cbr[:]))
    S.op("vector", lambda e: e.memset(ct[:, 0, 0:3], 0.0))
    S.op("vector", lambda e: e.memset(ct[:, 1, 0:3], 0.0))
    wxpr = wxp.rearrange("(k p) m -> p k m", p=128)
    if MX == 192:
        S.dma(wxpt[:], wxpr)
    else:
        S.dma(wxpt[:, :, 0:64], wxpr[:, :, 0:64])
        S.dma(wxpt[:, :, 64:80], wxpr[:, :, 80:96])
        S.dma(wxpt[:, :, 80:96], wxpr[:, :, 64:80])
    S.dma(wdtf[:], wdt)
    S.op("vector", lambda e: e.tensor_copy(out=wdtb[:], in_=wdtf[:]))
    S.dma(bdtr[:], bdt.rearrange("(g p) -> p g", p=128), slow=True)
    S.op("vector", lambda e: e.tensor_copy(out=bdtc[:], in_=bdtr[:]))
    q0 = ctx.val("q")

    tok_xnf = {}
    tok_xnbf = {}
    tok_s6 = {}
    tok_uf = {}
    tok_ub = {}

    def stats_stage(st, src_ap, xn_mode):
        A, P, E, V = st + "a", st + "p", st + "e", st + "v"
        dsl = DStream(ctx, st + "L")
        dsf = DStream(ctx, st + "F")
        dsb = DStream(ctx, st + "B")
        nv = 3 if xn_mode else 0
        do_stats = (not xn_mode) or layer == 0
        qs = ctx.val("q")
        for th in range(2):
            for k in range(KD):
                i = th * KD + k
                p = i % 2
                w = [("q", qs)]
                if i >= 2:
                    w = [(V, nv * (i - 1))] if xn_mode else [(P, i - 1), (A, i - 1)]
                    if do_stats and xn_mode:
                        w += [(P, i - 1), (A, i - 1)]
                    if xn_mode:
                        w.append(dsb.tok(p))
                        if layer == 0:
                            w.append(dsf.tok(p))
                lt = dsl.emit("sync", p, (lambda k=k, th=th, p=p: lambda e: e.dma_start(
                    out=xt[:, p, :],
                    in_=src_ap[k * 128:(k + 1) * 128, th * H:(th + 1) * H]))(),
                    waits=tuple(w))
                if xn_mode:
                    ctx.emit("vector", (lambda th=th, p=p: lambda e: e.tensor_sub(
                        out=xt[:, p, :], in0=xt[:, p, :],
                        in1=rep_a[:, th * H:(th + 1) * H]))(),
                        waits=(lt, ("q", qs)), fire=ctx.fire(V, 1))
                    ctx.emit("vector", (lambda th=th, p=p: lambda e: e.tensor_mul(
                        out=xt[:, p, :], in0=xt[:, p, :],
                        in1=rep_b[:, th * H:(th + 1) * H]))(),
                        waits=((V, nv * i + 1),), fire=ctx.fire(V, 1))
                    if layer == 0:
                        tok_xnf[i] = dsf.emit("gpsimd", p, (lambda k=k, th=th, p=p: lambda e: e.dma_start(
                            out=xn_f[k * 128:(k + 1) * 128, th * H:(th + 1) * H],
                            in_=xt[:, p, :]))(),
                            waits=((V, nv * i + 2),))
                    ctx.emit("vector", (lambda p=p: lambda e: e.tensor_copy(
                        out=xbf[:, p, :], in_=xt[:, p, :]))(),
                        waits=(dsb.tok(p), (V, nv * i + 2)), fire=ctx.fire(V, 1))
                    tok_xnbf[i] = dsb.emit("gpsimd", p, (lambda k=k, th=th, p=p: lambda e: e.dma_start(
                        out=xn_bf[k * 128:(k + 1) * 128, th * H:(th + 1) * H],
                        in_=xbf[:, p, :]))(),
                        waits=((V, nv * i + 3),))
                if do_stats:
                    ctx.emit("scalar", (lambda p=p: lambda e: e.activation(
                        out=sq[:, p, :], in_=xt[:, p, :], func=AF.Square))(),
                        waits=(lt, (P, i - 1)) if not xn_mode
                        else ((V, nv * i + 2), (P, i - 1)),
                        fire=ctx.fire(A, 1))
                    for which in range(2):
                        stt = xt if which == 0 else sq
                        for fc in range(2):
                            wts = []
                            if which == 0 and fc == 0:
                                wts = [(A, i + 1), ("q", qs), lt]
                                if th == 1 and k == 0:
                                    wts.append((E, 4))
                            ctx.emit("tensor", (lambda w2=which, fc=fc, k=k, p=p, stt=stt: lambda e: e.matmul(
                                out=ps_sm[0:1, w2 * 2 + fc, :],
                                lhsT=ones_col[:],
                                rhs=stt[:, p, fc * 512:(fc + 1) * 512],
                                start=(k == 0), stop=(k == KD - 1)))(),
                                waits=tuple(wts),
                                fire=ctx.fire(P, 1) if (which, fc) == (1, 1) else None)
            if do_stats:
                for which, slot in ((0, MU), (1, SQS)):
                    for fc in range(2):
                        ctx.emit("scalar", (lambda w2=which, s=slot, fc=fc, th=th: lambda e: e.activation(
                            out=rows[:, s, th * H + fc * 512:th * H + (fc + 1) * 512],
                            in_=ps_sm[0:1, w2 * 2 + fc, :],
                            func=AF.Identity, scale=1.0 / D))(),
                            waits=((P, 8 * (th + 1)),), fire=ctx.fire(E, 1))
        joins = [(E, 8)] if do_stats else []
        if xn_mode:
            joins += [dsb.tok(0), dsb.tok(1)]
            if layer == 0:
                joins += [dsf.tok(0), dsf.tok(1)]
        return joins

    def row_math(pre):
        S.opw("vector", lambda e: e.tensor_mul(
            out=rows[:, SCR, :], in0=rows[:, MU, :], in1=rows[:, MU, :]), pre)
        S.op("vector", lambda e: e.tensor_sub(
            out=rows[:, SCR, :], in0=rows[:, SQS, :], in1=rows[:, SCR, :]))
        S.op("scalar", lambda e: e.activation(
            out=rows[:, SCR, :], in_=rows[:, SCR, :], func=AF.Sqrt,
            bias=cst.ap(1e-5)[0:1, :]))
        S.op("vector", lambda e: e.reciprocal(
            out=rows[:, RSTD, :], in_=rows[:, SCR, :]))

    def rep_pass():
        for slot, dst in ((MU, rep_a), (RSTD, rep_b)):
            for fc in range(4):
                S.op("tensor", (lambda s=slot, fc=fc: lambda e: e.matmul(
                    out=ps_big[:, fc, :], lhsT=ones_row[:],
                    rhs=rows[:, s, fc * 512:(fc + 1) * 512],
                    start=True, stop=True))())
            S.op("vector", (lambda d=dst: lambda e: e.tensor_copy(
                out=d[:], in_=ps_big[:].rearrange("p a f -> p (a f)")))())

    j1 = stats_stage(tg + "s1", x, False)
    if stop == "s1":
        ctx.emit("vector", lambda e: e.engine_nop(), waits=tuple(j1),
                 fire=ctx.fire("q", 1))
        return
    row_math(j1)
    rep_pass()
    j2 = stats_stage(tg + "s2", x, True)
    if stop == "s2":
        ctx.emit("vector", lambda e: e.engine_nop(), waits=tuple(j2),
                 fire=ctx.fire("q", 1))
        return
    if layer == 0:
        row_math(j2)
        rep_pass()
        X5V, X5A = tg + "x5v", tg + "x5a"
        ds5 = DStream(ctx, tg + "L5")
        q5 = ctx.val("q")
        for th in range(2):
            for k in range(KD):
                i = th * KD + k
                p = i % 2
                w = [("q", q5)] if i < 2 else [(X5A, i - 1)]
                lt = ds5.emit("sync", p, (lambda k=k, th=th, p=p: lambda e: e.dma_start(
                    out=xt[:, p, :],
                    in_=xn_f[k * 128:(k + 1) * 128, th * H:(th + 1) * H]))(),
                    waits=tuple(w))
                ctx.emit("vector", (lambda th=th, p=p: lambda e: e.tensor_sub(
                    out=xt[:, p, :], in0=xt[:, p, :],
                    in1=rep_a[:, th * H:(th + 1) * H]))(),
                    waits=(lt, ("q", q5)), fire=ctx.fire(X5V, 1))
                ctx.emit("vector", (lambda th=th, p=p: lambda e: e.tensor_mul(
                    out=xt[:, p, :], in0=xt[:, p, :],
                    in1=rep_b[:, th * H:(th + 1) * H]))(),
                    waits=((X5V, 2 * i + 1),), fire=ctx.fire(X5V, 1))
                ctx.emit("scalar", (lambda k=k, th=th, p=p: lambda e: e.activation(
                    out=xn2[:, k, th * H:(th + 1) * H], in_=xt[:, p, :],
                    func=AF.Identity, scale=lncol[:, 0, k:k + 1],
                    bias=lncol[:, 1, k:k + 1]))(),
                    waits=((X5V, 2 * i + 2),), fire=ctx.fire(X5A, 1))
        xn2_done = [(X5A, 16)]
    else:
        ds5 = DStream(ctx, tg + "L5", npar=8)
        t5 = []
        for k in range(KD):
            t5.append(ds5.emit("sync", k, (lambda k=k: lambda e: e.dma_start(
                out=xn2[:, k, :], in_=xn_bf[k * 128:(k + 1) * 128, :]))(),
                waits=tuple([tok_xnbf[k], tok_xnbf[k + 8]])))
        xn2_done = t5

    if stop == "x5":
        ctx.emit("vector", lambda e: e.engine_nop(), waits=tuple(xn2_done),
                 fire=ctx.fire("q", 1))
        return
    P6, E6 = tg + "p6", tg + "e6"
    dsw6 = DStream(ctx, tg + "W6")
    ds6 = DStream(ctx, tg + "S6")
    winr = win.rearrange("(k p) m -> p k m", p=128)
    for m in range(32):
        pw = m % 2
        w = [(P6, 2 * (m - 1))] if m >= 2 else []
        wt = dsw6.emit("sync", pw, (lambda m=m, pw=pw: lambda e: e.dma_start(
            out=wm[:, pw, :, :], in_=winr[:, :, m * 128:(m + 1) * 128]))(),
            waits=tuple(w))
        for th in range(2):
            it = m * 2 + th
            pb = it % 2
            for k in range(KD):
                for j in range(2):
                    wts = []
                    if k == 0 and j == 0:
                        wts = [wt] + list(xn2_done)
                        if it >= 2:
                            wts.append((E6, it - 1))
                    ctx.emit("tensor", (lambda k=k, th=th, j=j, pb=pb, pw=pw: lambda e: e.matmul(
                        out=ps_big[:, pb * 2 + j, :], lhsT=wm[:, pw, k, :],
                        rhs=xn2[:, k, th * H + j * 512:th * H + (j + 1) * 512],
                        start=(k == 0), stop=(k == KD - 1)))(),
                        waits=tuple(wts),
                        fire=ctx.fire(P6, 1) if (k, j) == (KD - 1, 1) else None)
            ctx.emit("vector", (lambda it=it, pb=pb: lambda e: e.tensor_copy(
                out=ebf[:, pb, :],
                in_=ps_big[:, pb * 2:pb * 2 + 2, :].rearrange("p a f -> p (a f)")))(),
                waits=tuple([(P6, it + 1), ds6.tok(pb)]),
                fire=ctx.fire(E6, 1))
            dst = xi_bf if m < 16 else z_bf
            mm2 = m if m < 16 else m - 16
            tok_s6[(mm2, th, m < 16)] = ds6.emit(
                "gpsimd", pb, (lambda mm2=mm2, th=th, pb=pb, dst=dst: lambda e: e.dma_start(
                    out=dst[mm2 * 128:(mm2 + 1) * 128, th * H:(th + 1) * H],
                    in_=ebf[:, pb, :]))(),
                waits=((E6, it + 1),))

    if stop == "inproj":
        ctx.emit("vector", lambda e: e.engine_nop(),
                 waits=(ds6.tok(0), ds6.tok(1)), fire=ctx.fire("q", 1))
        return
    V7, A7 = tg + "v7", tg + "a7"
    NV7 = 5 if layer == 0 else 6
    dsct = DStream(ctx, tg + "CT")
    dsuf = DStream(ctx, tg + "UF")
    dsub = DStream(ctx, tg + "UB")
    for g in range(16):
        p = g % 2
        w = [tok_s6[(g, 0, True)], tok_s6[(g, 1, True)], ("q", q0)]
        if g >= 2:
            w.append((V7, NV7 * (g - 1)))
        lt = dsct.emit("sync", p, (lambda g=g, p=p: lambda e: e.dma_start(
            out=ct[:, p, 3:T + 3], in_=xi_bf[g * 128:(g + 1) * 128, :]))(),
            waits=tuple(w))
        ctx.emit("vector", (lambda g=g, p=p: lambda e: e.tensor_scalar(
            out=ua[:, p, :], in0=ct[:, p, 0:T], scalar1=cw[:, g, 0:1],
            scalar2=None, op0=ALU.mult))(),
            waits=tuple([lt, ("q", q0), dsuf.tok(p)]),
            fire=ctx.fire(V7, 1))
        for kk in range(1, 4):
            ctx.emit("vector", (lambda g=g, kk=kk, p=p: lambda e: e.scalar_tensor_tensor(
                out=ua[:, p, :], in0=ct[:, p, kk:T + kk], scalar=cw[:, g, kk:kk + 1],
                in1=ua[:, p, :], op0=ALU.mult, op1=ALU.add))(),
                waits=((V7, NV7 * g + kk),), fire=ctx.fire(V7, 1))
        ctx.emit("scalar", (lambda g=g, p=p: lambda e: e.activation(
            out=ua[:, p, :], in_=ua[:, p, :], func=AF.Silu, bias=cb[:, g:g + 1]))(),
            waits=((V7, NV7 * g + 4),), fire=ctx.fire(A7, 1))
        ctx.emit("vector", (lambda p=p: lambda e: e.tensor_copy(
            out=ub[:, p, :], in_=ua[:, p, :]))(),
            waits=((A7, g + 1), dsub.tok(p)), fire=ctx.fire(V7, 1))
        if layer == 1:
            ctx.emit("vector", (lambda p=p: lambda e: e.tensor_copy(
                out=ua[:, p, :], in_=ub[:, p, :]))(),
                waits=((V7, NV7 * g + 5),), fire=ctx.fire(V7, 1))
        tok_uf[g] = dsuf.emit("gpsimd", p, (lambda g=g, p=p: lambda e: e.dma_start(
            out=u_f[g * 128:(g + 1) * 128, :], in_=ua[:, p, :]))(),
            waits=((V7, NV7 * (g + 1)),))
        tok_ub[g] = dsub.emit("gpsimd", p, (lambda g=g, p=p: lambda e: e.dma_start(
            out=u_bf[g * 128:(g + 1) * 128, :], in_=ub[:, p, :]))(),
            waits=((V7, NV7 * (g + 1)),))

    if stop == "conv":
        ctx.emit("vector", lambda e: e.engine_nop(),
                 waits=(dsuf.tok(0), dsuf.tok(1), dsub.tok(0), dsub.tok(1)),
                 fire=ctx.fire("q", 1))
        return
    P8, E8 = tg + "p8", tg + "e8"
    dsu8 = DStream(ctx, tg + "U8")
    ds8 = DStream(ctx, tg + "S8")
    psc2 = ctx.psB[0:64]
    for th in range(2):
        for k in range(16):
            i8 = th * 16 + k
            p = i8 % 2
            w = [tok_ub[k]]
            if i8 >= 2:
                w.append((P8, i8 - 1))
            if th == 1 and k == 0:
                w.append((E8, 3))
            lt = dsu8.emit("sync", p, (lambda k=k, th=th, p=p: lambda e: e.dma_start(
                out=uh[:, p, :],
                in_=u_bf[k * 128:(k + 1) * 128, th * H:(th + 1) * H]))(),
                waits=tuple(w))
            for j in range(2):
                wts = []
                if j == 0:
                    wts = [lt, ("q", q0)]
                    if k == 0 and th == 1:
                        wts.append((E8, 3))
                if MX == 192:
                    mms = [(ps_big[:, 2 * th + j, :], wxpt[:, k, 0:128]),
                           (ps_sm[:, 2 * th + j, :], wxpt[:, k, 128:192])]
                else:
                    mms = [(ps_big[0:80, 2 * th + j, :], wxpt[:, k, 0:80]),
                           (ps_sm[0:16, 2 * th + j, :], wxpt[:, k, 80:96])]
                for mi, (pso, lh) in enumerate(mms):
                    ctx.emit("tensor", (lambda pso=pso, lh=lh, k=k, p=p, j=j: lambda e: e.matmul(
                        out=pso, lhsT=lh,
                        rhs=uh[:, p, j * 512:(j + 1) * 512],
                        start=(k == 0), stop=(k == 15)))(),
                        waits=tuple(wts if mi == 0 else []),
                        fire=ctx.fire(P8, 1) if (j == 1 and mi == len(mms) - 1) else None)
        ev = [(P8, 16 * (th + 1))]
        ctx.emit("vector", (lambda th=th: lambda e: e.tensor_copy(
            out=dtr[:, th * H:(th + 1) * H],
            in_=ps_big[0:64, 2 * th:2 * th + 2, :].rearrange("p a f -> p (a f)")))(),
            waits=tuple(ev), fire=ctx.fire(E8, 1))
        if MX == 192:
            ctx.emit("vector", (lambda th=th: lambda e: e.tensor_copy(
                out=tb[64:128, 0, 0, :], in_=ps_big[64:128, 2 * th:2 * th + 2, :]
                .rearrange("p a f -> p (a f)")))(),
                waits=(ds8.tok(0),), fire=ctx.fire(E8, 1))
            tk = ds8.emit("gpsimd", 0, (lambda th=th: lambda e: e.dma_start(
                out=B0[:, th * H:(th + 1) * H], in_=tb[64:128, 0, 0, :]))(),
                waits=((E8, 3 * th + 2),))
            ctx.emit("vector", (lambda th=th: lambda e: e.tensor_copy(
                out=tb[0:64, 0, 1, :], in_=psc2[:, 2 * th:2 * th + 2, :]
                .rearrange("p a f -> p (a f)")))(),
                waits=(ds8.tok(1),), fire=ctx.fire(E8, 1))
            tk = ds8.emit("gpsimd", 1, (lambda th=th: lambda e: e.dma_start(
                out=C0[:, th * H:(th + 1) * H], in_=tb[0:64, 0, 1, :]))(),
                waits=((E8, 3 * th + 3),))
        else:
            ctx.emit("vector", (lambda th=th: lambda e: e.tensor_copy(
                out=tb[0:16, 0, 0, :], in_=psc2[0:16, 2 * th:2 * th + 2, :]
                .rearrange("p a f -> p (a f)")))(),
                waits=(ds8.tok(0),), fire=ctx.fire(E8, 1))
            tk = ds8.emit("gpsimd", 0, (lambda th=th: lambda e: e.dma_start(
                out=B0[:, th * H:(th + 1) * H], in_=tb[0:16, 0, 0, :]))(),
                waits=((E8, 3 * th + 2),))
            ctx.emit("vector", (lambda th=th: lambda e: e.tensor_copy(
                out=tb[64:80, 0, 1, :], in_=ps_big[64:80, 2 * th:2 * th + 2, :]
                .rearrange("p a f -> p (a f)")))(),
                waits=(ds8.tok(1),), fire=ctx.fire(E8, 1))
            tk = ds8.emit("gpsimd", 1, (lambda th=th: lambda e: e.dma_start(
                out=C0[:, th * H:(th + 1) * H], in_=tb[64:80, 0, 1, :]))(),
                waits=((E8, 3 * th + 3),))
    bc_done = [ds8.tok(0), ds8.tok(1)]

    if stop == "xdbl":
        ctx.emit("vector", lambda e: e.engine_nop(),
                 waits=tuple(bc_done), fire=ctx.fire("q", 1))
        return
    P9, A9, V9 = tg + "p9", tg + "a9", tg + "v9"
    NV9 = 5
    dsu9 = DStream(ctx, tg + "U9")
    ds9 = DStream(ctx, tg + "S9")
    for m in range(16):
        for th in range(2):
            it = m * 2 + th
            p = it % 2
            for j in range(2):
                wts = []
                if j == 0:
                    wts = [(E8, 6), ("q", q0)] + (bc_done if it < 2 else [])
                    if it >= 2:
                        wts.append((V9, NV9 * (it - 1) + 1))
                        wts.append((A9, 3 * (it - 1)))
                ctx.emit("tensor", (lambda m=m, th=th, j=j, p=p: lambda e: e.matmul(
                    out=ps_big[:, p * 2 + j, :], lhsT=wdtb[:, m * 128:(m + 1) * 128],
                    rhs=dtr[:, th * H + j * 512:th * H + (j + 1) * 512],
                    start=True, stop=True))(),
                    waits=tuple(wts),
                    fire=ctx.fire(P9, 1) if j == 1 else None)
            psv = ps_big[:, p * 2:p * 2 + 2, :].rearrange("p a f -> p (a f)")
            va = NV9 * it
            aa = 3 * it
            ctx.emit("vector", (lambda m=m, p=p, psv=psv: lambda e: e.tensor_scalar(
                out=tb[:, p, 1, :], in0=psv, scalar1=bdtc[:, m:m + 1],
                scalar2=None, op0=ALU.add))(),
                waits=tuple([(P9, it + 1)]
                            + (bc_done if it < 2 else [])),
                fire=ctx.fire(V9, 1))
            ctx.emit("vector", (lambda m=m, p=p, psv=psv: lambda e: e.tensor_scalar(
                out=tb[:, p, 0, :], in0=psv, scalar1=bdtc[:, m:m + 1],
                scalar2=0.0, op0=ALU.add, op1=ALU.max))(),
                waits=tuple((bc_done if it < 2 else [ds9.tok(p)])),
                fire=ctx.fire(V9, 1))
            ctx.emit("scalar", (lambda p=p: lambda e: e.activation(
                out=tb[:, p, 1, :], in_=tb[:, p, 1, :], func=AF.Abs))(),
                waits=tuple([(V9, va + 1)] + ([ds9.tok(p)] if it >= 2 else [])),
                fire=ctx.fire(A9, 1))
            ctx.emit("scalar", (lambda p=p: lambda e: e.activation(
                out=tb[:, p, 2, :], in_=tb[:, p, 1, :], func=AF.Exp, scale=-1.0))(),
                waits=((A9, aa + 1),), fire=ctx.fire(A9, 1))
            ctx.emit("scalar", (lambda p=p: lambda e: e.activation(
                out=tb[:, p, 1, :], in_=tb[:, p, 2, :], func=AF.Ln, bias=1.0))(),
                waits=((A9, aa + 2),), fire=ctx.fire(A9, 1))
            ut = dsu9.emit("sync", p, (lambda m=m, th=th, p=p: lambda e: e.dma_start(
                out=tb[:, p, 3, :],
                in_=u_f[m * 128:(m + 1) * 128, th * H:(th + 1) * H]))(),
                waits=tuple([tok_uf[m]]
                            + ([(V9, NV9 * (it - 1) + 4)] if it >= 2 else [])))
            ctx.emit("vector", (lambda p=p: lambda e: e.tensor_add(
                out=tb[:, p, 2, :], in0=tb[:, p, 0, :], in1=tb[:, p, 1, :]))(),
                waits=((A9, aa + 3), (V9, va + 2)), fire=ctx.fire(V9, 1))
            ctx.emit("vector", (lambda p=p: lambda e: e.tensor_mul(
                out=tb[:, p, 0, :], in0=tb[:, p, 2, :], in1=tb[:, p, 3, :]))(),
                waits=tuple([(V9, va + 3), ut] + ([ds9.tok(p)] if it >= 2 else [])),
                fire=ctx.fire(V9, 1))
            ctx.emit("vector", (lambda p=p: lambda e: e.tensor_copy(
                out=duh[:, p, :], in_=tb[:, p, 0, :]))(),
                waits=((V9, va + 4),), fire=ctx.fire(V9, 1))
            ds9.emit("gpsimd", p, (lambda m=m, th=th, p=p: lambda e: e.dma_start(
                out=delta[m * 128:(m + 1) * 128, th * H:(th + 1) * H],
                in_=tb[:, p, 2, :]))(),
                waits=((V9, va + 4),))
            ds9.emit("gpsimd", p, (lambda m=m, th=th, p=p: lambda e: e.dma_start(
                out=du[m * 128:(m + 1) * 128, th * H:(th + 1) * H],
                in_=tb[:, p, 0, :]))(),
                waits=((V9, va + 5),))
            ds9.emit("gpsimd", p, (lambda m=m, th=th, p=p: lambda e: e.dma_start(
                out=du_f16[m * 128:(m + 1) * 128, th * H:(th + 1) * H],
                in_=duh[:, p, :]))(),
                waits=((V9, va + 5),))
    joins = [ds9.tok(0), ds9.tok(1)] + bc_done
    joins += [dsuf.tok(0), dsuf.tok(1), dsub.tok(0), dsub.tok(1)]
    joins += [ds6.tok(0), ds6.tok(1)]
    joins += j2[-4:] if layer == 0 else j2
    ctx.emit("vector", lambda e: e.engine_nop(),
             waits=tuple(joins), fire=ctx.fire("q", 1))




# version 11 (final)
# speedup vs baseline: 1.0440x; 1.0440x over previous
import math
import numpy as np
import ml_dtypes
from contextlib import ExitStack

import concourse.bass as bass
import concourse.mybir as mybir
from concourse.bass_utils import run_bass_kernel_spmd

F32 = mybir.dt.float32
BF16 = mybir.dt.bfloat16
F16 = mybir.dt.float16
ALU = mybir.AluOpType
AF = mybir.ActivationFunctionType

BF = ml_dtypes.bfloat16

T = 2048
D = 1024
DI = 2048
RNE_C = 12582912.0

def _pwl_constants():
    xs = [0.25 * k for k in range(17)]
    ev = [0.5 * x * (1.0 + math.erf(x / math.sqrt(2.0))) - x / 2.0 for x in xs]
    s = [(ev[k + 1] - ev[k]) / 0.25 for k in range(16)]
    knots = []
    for k in range(1, 16):
        knots.append((64.0 * k, s[k] - s[k - 1]))
    knots.append((1024.0, 0.5 - s[15]))
    j256 = 256.0 * (4.0 - (0.5 * 4.0 * (1.0 + math.erf(4.0 / math.sqrt(2.0)))))
    return s[0], knots, j256


PWL_S0, PWL_KNOTS, PWL_J256 = _pwl_constants()


class Ctx:
    ENGINES = ("sync", "vector", "scalar", "tensor", "gpsimd")

    def __init__(self, nc, stack):
        self.nc = nc
        self.stack = stack
        self.ops = {e: [] for e in self.ENGINES}
        self.sems = {}
        self.cnt = {}
        self.bump = 16512
        self.phase_tag = "g"
        self.psA = None
        self.psB = None

    def _bind(self, name, srckey):
        if name in self.sems:
            if srckey is not None:
                prev = self.srck.get(name)
                if prev is None:
                    self.srck[name] = srckey
                else:
                    assert prev == srckey or name == "q", (
                        f"sem {name} fired from {prev} and {srckey}")
            return self.sems[name]
        if not hasattr(self, "pools"):
            self.pools, self.htot, self.hbase, self.srck = {}, {}, {}, {}
            self.suid = 0
        pool = self.pools.setdefault(srckey, []) if srckey is not None else None
        if pool and name != "q":
            h = pool.pop()
        else:
            self.suid += 1
            h = self.stack.enter_context(
                self.nc.semaphore(f"s{self.suid:03d}_{name}"))
            self.htot[id(h)] = 0
        self.sems[name] = h
        self.hbase[name] = self.htot[id(h)]
        self.srck[name] = srckey
        return h

    def fire(self, name, amount):
        self.cnt[name] = self.cnt.get(name, 0) + amount
        return (name, amount)

    def val(self, name):
        return self.cnt.get(name, 0)

    def emit(self, engine, fn, waits=(), fire=None):
        wl = []
        for (n, v) in waits:
            if v <= 0:
                continue
            assert self.cnt.get(n, 0) >= v or n == "q", (
                f"forward wait {n} >= {v} (cnt {self.cnt.get(n, 0)})")
            h = self._bind(n, None)
            wl.append((h, self.hbase[n] + v))
        fr = None
        if fire is not None:
            name, amount = fire
            srckey = ("dma", engine) if amount >= 16 else ("eng", engine)
            h = self._bind(name, srckey)
            self.htot[id(h)] += amount
            fr = (h, amount)

        def runner(eng):
            for h, v in wl:
                eng.wait_ge(h, v)
            ins = fn(eng)
            if fr is not None:
                ins.then_inc(fr[0], fr[1])
        self.ops[engine].append(runner)

    def dma(self, queue, out, in_, waits=(), sem="dma", transpose=False):
        f = self.fire(sem, 16)
        if transpose:
            self.emit(queue, lambda e: e.dma_start_transpose(out=out, in_=in_),
                      waits, f)
        else:
            self.emit(queue, lambda e: e.dma_start(out=out, in_=in_), waits, f)
        return self.cnt[sem]

    SBUF_BASE = 16512
    SBUF_LIMIT = 229344

    def phase(self, tag):
        self.phase_tag = tag
        self.bump = self.SBUF_BASE
        for n in [n for n in self.sems if n != "q"]:
            h = self.sems.pop(n)
            self.cnt.pop(n, None)
            sk = self.srck.pop(n, None)
            self.hbase.pop(n, None)
            if sk is not None:
                self.pools.setdefault(sk, []).append(h)

    def sbuf(self, name, shape, dt):
        per = 1
        for s in shape[1:]:
            per *= s
        per *= {F32: 4, BF16: 2, F16: 2}[dt]
        off = (self.bump + 31) & ~31
        self.bump = off + per
        assert self.bump <= self.SBUF_LIMIT, (
            f"sbuf overflow in phase {self.phase_tag}: {self.bump}")
        h = self.nc.alloc_sbuf_tensor_at(
            f"{self.phase_tag}_{name}", list(shape), dt, offset=off)
        return h

    def psum(self, name, shape, dt=F32):
        return self.stack.enter_context(self.nc.psum_tensor(name, shape, dt))

    def finalize(self, final_waits):
        ops = self.ops
        wl = [(self.sems[n], self.hbase.get(n, 0) + v) for (n, v) in final_waits]
        with self.nc.Block() as block:
            @block.sync
            def _(eng):
                for f in ops["sync"]:
                    f(eng)
                for h, v in wl:
                    eng.wait_ge(h, v)

            @block.vector
            def _(eng):
                for f in ops["vector"]:
                    f(eng)

            @block.scalar
            def _(eng):
                for f in ops["scalar"]:
                    f(eng)

            @block.tensor
            def _(eng):
                for f in ops["tensor"]:
                    f(eng)

            @block.gpsimd
            def _(eng):
                for f in ops["gpsimd"]:
                    f(eng)


def new_prog():
    nc = bass.Bass("TRN2", target_bir_lowering=False)
    stack = ExitStack()
    ctx = Ctx(nc, stack)
    return nc, stack, ctx


def din(nc, name, shape, dt=F32):
    return nc.dram_tensor(name, list(shape), dt, kind="ExternalInput").ap()


def dout(nc, name, shape, dt=F32):
    return nc.dram_tensor(name, list(shape), dt, kind="ExternalOutput").ap()



def _seq(ctx, engine, fn):
    v = ctx.val("q")
    ctx.emit(engine, fn, waits=(("q", v),), fire=ctx.fire("q", 1))


def _seqdma(ctx, queue, out, in_, transpose=False, slow=False):
    v = ctx.val("q")
    f = ctx.fire("q", 16)
    if transpose:
        ctx.emit(queue, lambda e: e.dma_start_transpose(out=out, in_=in_),
                 waits=(("q", v),), fire=f)
    elif slow:
        ctx.emit(queue, lambda e: e.dma_start(out=out, in_=in_,
                                              allow_slow_non_contiguous=True),
                 waits=(("q", v),), fire=f)
    else:
        ctx.emit(queue, lambda e: e.dma_start(out=out, in_=in_),
                 waits=(("q", v),), fire=f)


class Seq:

    def __init__(self, ctx):
        self.ctx = ctx

    def op(self, engine, fn):
        _seq(self.ctx, engine, fn)

    def dma(self, out, in_, queue="sync", transpose=False, slow=False):
        _seqdma(self.ctx, queue, out, in_, transpose, slow)

    def opw(self, engine, fn, extra):
        ctx = self.ctx
        v = ctx.val("q")
        ctx.emit(engine, fn, waits=tuple([("q", v)] + list(extra)),
                 fire=ctx.fire("q", 1))

    def done(self):
        self.ctx.finalize([("q", self.ctx.val("q"))])


class ConstCols:

    def __init__(self, S, values):
        self.tile = S.ctx.sbuf("cstc", [128, max(1, len(values))], F32)
        self.idx = {}
        for i, v in enumerate(values):
            self.idx[v] = i
            S.op("vector", lambda e, i=i, v=v: e.memset(self.tile[:, i:i + 1], v))

    def ap(self, v):
        i = self.idx[v]
        return self.tile[:, i:i + 1]


CASTS = [
    ("win", "win_bf", D, 4096),
    ("wout", "wout_bf", DI, D),
    ("w1", "w1_bf", D, 4096),
    ("w2", "w2_bf", 4096, D),
    ("wxp", "wxp_bf", DI, 192),
    ("f1t", "f1t_bf", D, 4096),
    ("f2t", "f2t_bf", D, 4096),
    ("h1t", "h1t_bf", 4096, 4096),
    ("h2t", "h2t_bf", 4096, 4096),
    ("w3t", "w3t_bf", 4096, D),
]


def prep_io(nc, mk):
    io = {}
    for (iname, oname, rows, cols) in CASTS:
        io[iname] = mk(iname, (rows, cols), F32, 0)
        io[oname] = mk(oname, (rows, cols), BF16, 1)
    return io


def emit_prep(nc, ctx, S, io):
    tin = ctx.sbuf("tin", [128, 2, 4096], F32)
    tout = ctx.sbuf("tout", [128, 2, 4096], BF16)
    dsl = DStream(ctx, "pr_L")
    dss = DStream(ctx, "pr_S")
    V = "pr_v"
    tiles = []
    for (iname, oname, rows, cols) in CASTS:
        for i in range(rows // 128):
            tiles.append((io[iname], io[oname], i, cols))
    q0 = ctx.val("q")
    for t, (srcp, dstp, i, cols) in enumerate(tiles):
        p = t % 2
        lt = dsl.emit("sync", p, (lambda srcp=srcp, i=i, cols=cols, p=p:
                                  lambda e: e.dma_start(
            out=tin[:, p, :cols], in_=srcp[i * 128:(i + 1) * 128, :]))(),
            waits=tuple([("q", q0)] if t < 2 else [(V, t - 1)]))
        ctx.emit("vector", (lambda cols=cols, p=p: lambda e: e.tensor_copy(
            out=tout[:, p, :cols], in_=tin[:, p, :cols]))(),
            waits=tuple([lt] + ([dss.tok(p)] if t >= 2 else [])),
            fire=ctx.fire(V, 1))
        dss.emit("gpsimd", p, (lambda dstp=dstp, i=i, cols=cols, p=p:
                               lambda e: e.dma_start(
            out=dstp[i * 128:(i + 1) * 128, :], in_=tout[:, p, :cols]))(),
            waits=((V, t + 1),))
    ctx.emit("vector", lambda e: e.engine_nop(),
             waits=(dss.tok(0), dss.tok(1)), fire=ctx.fire("q", 1))


def build_prep():
    nc, stack, ctx = new_prog()
    with stack:
        S = Seq(ctx)
        io = prep_io(nc, lambda n, s, d, o: (dout if o else din)(nc, n, s, d))
        ctx.phase("prep")
        emit_prep(nc, ctx, S, io)
        S.done()
    return nc



LXA_IO = ["x", "win_bf", "convw", "convb", "wxp_bf", "wdt", "bdt",
          "xn_bf", "xi_bf", "z_bf", "u_f", "u_bf", "B0", "C0", "delta", "du"]


def lxa_io(nc, layer, mk):
    NST = 64 if layer == 0 else 16
    MX = 192 if layer == 0 else 96
    fl = F32 if layer == 0 else BF16
    io = {}
    io["x"] = mk("x", (D, T), F32, 0)
    io["win_bf"] = mk("win_bf", (D, 4096), BF16, 0)
    io["convw"] = mk("convw", (DI, 4), fl, 0)
    io["convb"] = mk("convb", (DI,), fl, 0)
    io["wxp_bf"] = mk("wxp_bf", (DI, MX), BF16, 0)
    io["wdt"] = mk("wdt", (64, DI), fl, 0)
    io["bdt"] = mk("bdt", (DI,), fl, 0)
    if layer == 0:
        io["lnw"] = mk("lnw", (D,), F32, 0)
        io["lnb"] = mk("lnb", (D,), F32, 0)
        io["xn_f"] = mk("xn_f", (D, T), F32, 1)
    io["xn_bf"] = mk("xn_bf", (D, T), BF16, 1)
    io["xi_bf"] = mk("xi_bf", (DI, T), BF16, 1)
    io["z_bf"] = mk("z_bf", (DI, T), BF16, 1)
    io["u_f"] = mk("u_f", (DI, T), F32, 1)
    io["u_bf"] = mk("u_bf", (DI, T), BF16, 1)
    io["B0"] = mk("B0", (NST, T), F32, 1)
    io["C0"] = mk("C0", (NST, T), F32, 1)
    io["delta"] = mk("delta", (DI, T), F32, 1)
    io["du"] = mk("du", (DI, T), F32, 1)
    io["du_f16"] = mk("du_f16", (DI, T), F16, 1)
    return io


def emit_lxa(nc, ctx, S, io, layer):
    KD = D // 128
    H = T // 2
    NST = 64 if layer == 0 else 16
    MX = 192 if layer == 0 else 96

    x = io["x"]
    win = io["win_bf"]
    convw = io["convw"]
    convb = io["convb"]
    wxp = io["wxp_bf"]
    wdt = io["wdt"]
    bdt = io["bdt"]
    if layer == 0:
        lnw = io["lnw"]
        lnb = io["lnb"]
        xn_f = io["xn_f"]
    xn_bf = io["xn_bf"]
    xi_bf = io["xi_bf"]
    z_bf = io["z_bf"]
    u_f = io["u_f"]
    u_bf = io["u_bf"]
    B0 = io["B0"]
    C0 = io["C0"]
    delta = io["delta"]
    du = io["du"]
    du_f16 = io["du_f16"]

    if True:
        xt = ctx.sbuf("xt", [128, H], F32)
        sq = ctx.sbuf("sq", [128, H], F32)
        ones_col = ctx.sbuf("ones_col", [128, 1], F32)
        ones_row = ctx.sbuf("ones_row", [1, 128], F32)
        rows = ctx.sbuf("rows", [1, 4, T], F32)
        rep_a = ctx.sbuf("rep_a", [128, T], F32)
        rep_b = ctx.sbuf("rep_b", [128, T], F32)
        xbf = ctx.sbuf("xbf", [128, H], BF16)
        xn2 = ctx.sbuf("xn2", [128, KD, T], BF16)
        wm = ctx.sbuf("wm", [128, KD, 128], BF16)
        ebf = ctx.sbuf("ebf", [128, H], BF16)
        ct = ctx.sbuf("ct", [128, T + 3], BF16)
        cwr = ctx.sbuf("cwr", [128, 16, 4], F32 if layer == 0 else BF16)
        cbr = ctx.sbuf("cbr", [128, 16], F32 if layer == 0 else BF16)
        cw = ctx.sbuf("cw", [128, 16, 4], F32)
        cb = ctx.sbuf("cb", [128, 16], F32)
        ua = ctx.sbuf("ua", [128, T], F32)
        ub = ctx.sbuf("ub", [128, T], BF16)
        wxpt = ctx.sbuf("wxpt", [128, 16, MX], BF16)
        uh = ctx.sbuf("uh", [128, H], BF16)
        dtr = ctx.sbuf("dtr", [64, T], BF16)
        wdtf = ctx.sbuf("wdtf", [64, DI], F32 if layer == 0 else BF16)
        wdtb = ctx.sbuf("wdtb", [64, DI], BF16)
        bdtr = ctx.sbuf("bdtr", [128, 16], F32 if layer == 0 else BF16)
        bdtc = ctx.sbuf("bdtc", [128, 16], F32)
        tb = ctx.sbuf("tb", [128, 4, H], F32)
        duh = ctx.sbuf("duh", [128, H], F16)
        lncol = ctx.sbuf("lncol", [128, 2, KD], F32)
        ps_big = ctx.psA
        ps_sm = ctx.psB[0:64]

        MU, SQS, RSTD, SCR = range(4)
        psb2 = ps_big[:, 0:2, :].rearrange("p a f -> p (a f)")

        S.op("vector", lambda e: e.memset(ones_col[:], 1.0))
        S.op("vector", lambda e: e.memset(ones_row[:], 1.0))
        cst = ConstCols(S, [1e-5])

        def stats_pass(src_ap, xn_mode):
            for th in range(2):
                for k in range(KD):
                    S.dma(xt[:], src_ap[k * 128:(k + 1) * 128, th * H:(th + 1) * H])
                    if xn_mode:
                        S.op("vector", lambda e, th=th: e.tensor_sub(
                            out=xt[:], in0=xt[:], in1=rep_a[:, th * H:(th + 1) * H]))
                        S.op("vector", lambda e, th=th: e.tensor_mul(
                            out=xt[:], in0=xt[:], in1=rep_b[:, th * H:(th + 1) * H]))
                        if layer == 0:
                            S.dma(xn_f[k * 128:(k + 1) * 128, th * H:(th + 1) * H],
                                  xt[:], queue="gpsimd")
                        S.op("vector", lambda e: e.tensor_copy(out=xbf[:], in_=xt[:]))
                        S.dma(xn_bf[k * 128:(k + 1) * 128, th * H:(th + 1) * H],
                              xbf[:], queue="gpsimd")
                    if (not xn_mode) or layer == 0:
                        S.op("scalar", lambda e: e.activation(
                            out=sq[:], in_=xt[:], func=AF.Square))
                        for which in range(2):
                            st = xt if which == 0 else sq
                            for fc in range(2):
                                S.op("tensor", lambda e, w=which, fc=fc, k=k, st=st:
                                     e.matmul(out=ps_sm[0:1, w * 2 + fc, :],
                                              lhsT=ones_col[:],
                                              rhs=st[:, fc * 512:(fc + 1) * 512],
                                              start=(k == 0), stop=(k == KD - 1)))
                if (not xn_mode) or layer == 0:
                    for which, slot in ((0, MU), (1, SQS)):
                        for fc in range(2):
                            S.op("scalar", lambda e, w=which, s=slot, fc=fc, th=th:
                                 e.activation(
                                     out=rows[:, s, th * H + fc * 512:
                                              th * H + (fc + 1) * 512],
                                     in_=ps_sm[0:1, w * 2 + fc, :],
                                     func=AF.Identity, scale=1.0 / D))

        def row_math():
            S.op("vector", lambda e: e.tensor_mul(
                out=rows[:, SCR, :], in0=rows[:, MU, :], in1=rows[:, MU, :]))
            S.op("vector", lambda e: e.tensor_sub(
                out=rows[:, SCR, :], in0=rows[:, SQS, :], in1=rows[:, SCR, :]))
            S.op("scalar", lambda e: e.activation(
                out=rows[:, SCR, :], in_=rows[:, SCR, :], func=AF.Sqrt,
                bias=cst.ap(1e-5)[0:1, :]))
            S.op("vector", lambda e: e.reciprocal(
                out=rows[:, RSTD, :], in_=rows[:, SCR, :]))

        def rep_pass():
            for slot, dst in ((MU, rep_a), (RSTD, rep_b)):
                for fc in range(4):
                    S.op("tensor", lambda e, s=slot, fc=fc: e.matmul(
                        out=ps_big[:, fc, :], lhsT=ones_row[:],
                        rhs=rows[:, s, fc * 512:(fc + 1) * 512],
                        start=True, stop=True))
                S.op("vector", lambda e, d=dst: e.tensor_copy(
                    out=d[:], in_=ps_big[:].rearrange("p a f -> p (a f)")))

        stats_pass(x, False)
        row_math()
        rep_pass()
        stats_pass(x, True)
        if layer == 0:
            S.dma(lncol[:, 0, :], lnw.rearrange("(k p) -> p k", p=128), slow=True)
            S.dma(lncol[:, 1, :], lnb.rearrange("(k p) -> p k", p=128), slow=True)
            row_math()
            rep_pass()
            for th in range(2):
                for k in range(KD):
                    S.dma(xt[:], xn_f[k * 128:(k + 1) * 128, th * H:(th + 1) * H])
                    S.op("vector", lambda e, th=th: e.tensor_sub(
                        out=xt[:], in0=xt[:], in1=rep_a[:, th * H:(th + 1) * H]))
                    S.op("vector", lambda e, th=th: e.tensor_mul(
                        out=xt[:], in0=xt[:], in1=rep_b[:, th * H:(th + 1) * H]))
                    S.op("scalar", lambda e, k=k, th=th: e.activation(
                        out=xn2[:, k, th * H:(th + 1) * H], in_=xt[:],
                        func=AF.Identity, scale=lncol[:, 0, k:k + 1],
                        bias=lncol[:, 1, k:k + 1]))
        else:
            for k in range(KD):
                S.dma(xn2[:, k, :], xn_bf[k * 128:(k + 1) * 128, :])

        for m in range(32):
            S.dma(wm[:], win.rearrange("(k p) m -> p k m", p=128)[:, :, m * 128:(m + 1) * 128])
            for th in range(2):
                for k in range(KD):
                    for j in range(2):
                        S.op("tensor", lambda e, k=k, th=th, j=j: e.matmul(
                            out=ps_big[:, j, :], lhsT=wm[:, k, :],
                            rhs=xn2[:, k, th * H + j * 512:th * H + (j + 1) * 512],
                            start=(k == 0), stop=(k == KD - 1)))
                S.op("vector", lambda e: e.tensor_copy(out=ebf[:], in_=psb2))
                dst = xi_bf if m < 16 else z_bf
                mm2 = m if m < 16 else m - 16
                S.dma(dst[mm2 * 128:(mm2 + 1) * 128, th * H:(th + 1) * H],
                      ebf[:], queue="gpsimd")

        S.dma(cwr[:], convw.rearrange("(g p) k -> p g k", p=128), slow=True)
        S.dma(cbr[:], convb.rearrange("(g p) -> p g", p=128), slow=True)
        S.op("vector", lambda e: e.tensor_copy(out=cw[:], in_=cwr[:]))
        S.op("vector", lambda e: e.tensor_copy(out=cb[:], in_=cbr[:]))
        S.op("vector", lambda e: e.memset(ct[:, 0:3], 0.0))
        for g in range(16):
            S.dma(ct[:, 3:T + 3], xi_bf[g * 128:(g + 1) * 128, :])
            S.op("vector", lambda e, g=g: e.tensor_scalar(
                out=ua[:], in0=ct[:, 0:T], scalar1=cw[:, g, 0:1],
                scalar2=None, op0=ALU.mult))
            for kk in range(1, 4):
                S.op("vector", lambda e, g=g, kk=kk: e.scalar_tensor_tensor(
                    out=ua[:], in0=ct[:, kk:T + kk], scalar=cw[:, g, kk:kk + 1],
                    in1=ua[:], op0=ALU.mult, op1=ALU.add))
            S.op("scalar", lambda e, g=g: e.activation(
                out=ua[:], in_=ua[:], func=AF.Silu, bias=cb[:, g:g + 1]))
            S.op("vector", lambda e: e.tensor_copy(out=ub[:], in_=ua[:]))
            if layer == 1:
                S.op("vector", lambda e: e.tensor_copy(out=ua[:], in_=ub[:]))
            S.dma(u_f[g * 128:(g + 1) * 128, :], ua[:], queue="gpsimd")
            S.dma(u_bf[g * 128:(g + 1) * 128, :], ub[:], queue="gpsimd")

        S.dma(wxpt[:], wxp.rearrange("(k p) m -> p k m", p=128))
        psc2 = ps_sm[:, 0:2, :].rearrange("p a f -> p (a f)")
        psc3 = ps_sm[:, 2:4, :].rearrange("p a f -> p (a f)")
        for th in range(2):
            for k in range(16):
                S.dma(uh[:], u_bf[k * 128:(k + 1) * 128, th * H:(th + 1) * H])
                for j in range(2):
                    uhj = uh[:, j * 512:(j + 1) * 512]
                    if MX == 192:
                        S.op("tensor", lambda e, k=k, j=j, u2=uhj: e.matmul(
                            out=ps_big[:, j, :], lhsT=wxpt[:, k, 0:128],
                            rhs=u2, start=(k == 0), stop=(k == 15)))
                        S.op("tensor", lambda e, k=k, j=j, u2=uhj: e.matmul(
                            out=ps_sm[:, j, :], lhsT=wxpt[:, k, 128:192], rhs=u2,
                            start=(k == 0), stop=(k == 15)))
                    else:
                        S.op("tensor", lambda e, k=k, j=j, u2=uhj: e.matmul(
                            out=ps_big[0:64, j, :], lhsT=wxpt[:, k, 0:64],
                            rhs=u2, start=(k == 0), stop=(k == 15)))
                        S.op("tensor", lambda e, k=k, j=j, u2=uhj: e.matmul(
                            out=ps_sm[0:16, j, :], lhsT=wxpt[:, k, 64:80], rhs=u2,
                            start=(k == 0), stop=(k == 15)))
                        S.op("tensor", lambda e, k=k, j=j, u2=uhj: e.matmul(
                            out=ps_sm[0:16, 2 + j, :], lhsT=wxpt[:, k, 80:96], rhs=u2,
                            start=(k == 0), stop=(k == 15)))
            S.op("vector", lambda e, th=th: e.tensor_copy(
                out=dtr[:, th * H:(th + 1) * H],
                in_=ps_big[0:64, 0:2, :].rearrange("p a f -> p (a f)")))
            if MX == 192:
                S.op("vector", lambda e: e.tensor_copy(
                    out=tb[64:128, 0, :], in_=psb2[64:128, :]))
                S.op("vector", lambda e: e.tensor_copy(
                    out=tb[0:64, 1, :], in_=psc2[0:64, :]))
                S.dma(B0[:, th * H:(th + 1) * H], tb[64:128, 0, :], queue="gpsimd")
                S.dma(C0[:, th * H:(th + 1) * H], tb[0:64, 1, :], queue="gpsimd")
            else:
                S.op("vector", lambda e: e.tensor_copy(
                    out=tb[0:16, 0, :], in_=psc2[0:16, :]))
                S.op("vector", lambda e: e.tensor_copy(
                    out=tb[0:16, 1, :], in_=psc3[0:16, :]))
                S.dma(B0[:, th * H:(th + 1) * H], tb[0:16, 0, :], queue="gpsimd")
                S.dma(C0[:, th * H:(th + 1) * H], tb[0:16, 1, :], queue="gpsimd")

        S.dma(wdtf[:], wdt)
        S.op("vector", lambda e: e.tensor_copy(out=wdtb[:], in_=wdtf[:]))
        S.dma(bdtr[:], bdt.rearrange("(g p) -> p g", p=128), slow=True)
        S.op("vector", lambda e: e.tensor_copy(out=bdtc[:], in_=bdtr[:]))
        for m in range(16):
            for th in range(2):
                for j in range(2):
                    S.op("tensor", lambda e, m=m, th=th, j=j: e.matmul(
                        out=ps_big[:, j, :], lhsT=wdtb[:, m * 128:(m + 1) * 128],
                        rhs=dtr[:, th * H + j * 512:th * H + (j + 1) * 512],
                        start=True, stop=True))
                S.op("vector", lambda e, m=m: e.tensor_scalar(
                    out=tb[:, 0, :], in0=psb2, scalar1=bdtc[:, m:m + 1],
                    scalar2=0.0, op0=ALU.add, op1=ALU.max))
                S.op("scalar", lambda e, m=m: e.activation(
                    out=tb[:, 1, :], in_=psb2, func=AF.Abs, bias=bdtc[:, m:m + 1]))
                S.op("scalar", lambda e: e.activation(
                    out=tb[:, 2, :], in_=tb[:, 1, :], func=AF.Exp, scale=-1.0))
                S.op("scalar", lambda e: e.activation(
                    out=tb[:, 1, :], in_=tb[:, 2, :], func=AF.Ln, bias=1.0))
                S.op("vector", lambda e: e.tensor_add(
                    out=tb[:, 2, :], in0=tb[:, 0, :], in1=tb[:, 1, :]))
                S.dma(tb[:, 3, :], u_f[m * 128:(m + 1) * 128, th * H:(th + 1) * H])
                S.op("vector", lambda e: e.tensor_mul(
                    out=tb[:, 0, :], in0=tb[:, 2, :], in1=tb[:, 3, :]))
                S.op("vector", lambda e: e.tensor_copy(out=duh[:], in_=tb[:, 0, :]))
                S.dma(delta[m * 128:(m + 1) * 128, th * H:(th + 1) * H],
                      tb[:, 2, :], queue="gpsimd")
                S.dma(du[m * 128:(m + 1) * 128, th * H:(th + 1) * H],
                      tb[:, 0, :], queue="gpsimd")
                S.dma(du_f16[m * 128:(m + 1) * 128, th * H:(th + 1) * H],
                      duh[:], queue="gpsimd")


def build_lxa(layer, stop=None):
    nc, stack, ctx = new_prog()
    S = Seq(ctx)
    with stack:
        io = lxa_io(nc, layer, lambda n, s, d, o: (dout if o else din)(nc, n, s, d))
        ctx.psA = ctx.psum("psA", [128, 4, 512], F32)
        ctx.psB = ctx.psum("psB", [128, 4, 512], F32)
        ctx.phase("lxa")
        emit_lxa2(nc, ctx, S, io, layer, stop=stop)
        S.done()
    return nc



def scan_io(nc, layer, mk):
    NST = 64 if layer == 0 else 16
    a_dt = F32 if layer == 0 else BF16
    io = {}
    io["delta"] = mk("delta", (DI, T), F32, 0)
    io["du"] = mk("du", (DI, T), F32, 0)
    io["du_f16"] = mk("du_f16", (DI, T), F16, 0)
    io["Bm"] = mk("Bm", (NST, T), F32, 0)
    io["Cm"] = mk("Cm", (NST, T), F32, 0)
    io["alog"] = mk("alog", (DI, NST), a_dt, 0)
    io["y"] = mk("y", (DI, T), F32, 1)
    return io


def emit_scan(nc, ctx, S, io, layer, dg0=0, dg1=16):
    NST = 64 if layer == 0 else 16
    a_dt = F32 if layer == 0 else BF16

    delta = io["delta"]
    du = io["du"]
    Bm = io["Bm"]
    Cm = io["Cm"]
    alog = io["alog"]
    y = io["y"]

    if True:
        dl = ctx.sbuf("dl", [128, T], F32)
        dul = ctx.sbuf("dul", [128, T], F32)
        yac = ctx.sbuf("yac", [128, T], F32)
        dat = ctx.sbuf("dat", [128, T], F32)
        dbut = ctx.sbuf("dbut", [128, T], F32)
        ht = ctx.sbuf("ht", [128, T], F32)
        hct = ctx.sbuf("hct", [128, T], F32)
        brow = ctx.sbuf("brow", [1, T], F32)
        crow = ctx.sbuf("crow", [1, T], F32)
        alr = ctx.sbuf("alr", [128, NST], a_dt)
        asb = ctx.sbuf("asb", [128, NST], F32)
        ones_row = ctx.sbuf("ones_row", [1, 128], F32)
        ps_b = ctx.psA
        ps_c = ctx.psB
        psbv = ps_b[:].rearrange("p a f -> p (a f)")
        pscv = ps_c[:].rearrange("p a f -> p (a f)")

        S.op("vector", lambda e: e.memset(ones_row[:], 1.0))
        for dg in range(dg0, dg1):
            S.dma(dl[:], delta[dg * 128:(dg + 1) * 128, :])
            S.dma(dul[:], du[dg * 128:(dg + 1) * 128, :])
            S.dma(alr[:], alog[dg * 128:(dg + 1) * 128, :])
            S.op("scalar", lambda e: e.activation(out=asb[:], in_=alr[:], func=AF.Exp))
            S.op("vector", lambda e: e.tensor_single_scalar(
                out=asb[:], in_=asb[:], scalar=-1.0, op=ALU.mult))
            S.op("vector", lambda e: e.memset(yac[:], 0.0))
            for n in range(NST):
                S.dma(brow[:], Bm[n:n + 1, :])
                S.dma(crow[:], Cm[n:n + 1, :])
                for fc in range(4):
                    S.op("tensor", lambda e, fc=fc: e.matmul(
                        out=ps_b[:, fc, :], lhsT=ones_row[:],
                        rhs=brow[:, fc * 512:(fc + 1) * 512], start=True, stop=True))
                for fc in range(4):
                    S.op("tensor", lambda e, fc=fc: e.matmul(
                        out=ps_c[:, fc, :], lhsT=ones_row[:],
                        rhs=crow[:, fc * 512:(fc + 1) * 512], start=True, stop=True))
                S.op("scalar", lambda e, n=n: e.activation(
                    out=dat[:], in_=dl[:], func=AF.Exp, scale=asb[:, n:n + 1]))
                S.op("vector", lambda e: e.tensor_mul(out=dbut[:], in0=dul[:], in1=psbv))
                S.op("vector", lambda e: e.tensor_tensor_scan(
                    out=ht[:], data0=dat[:], data1=dbut[:], initial=0.0,
                    op0=ALU.mult, op1=ALU.add))
                S.op("vector", lambda e: e.tensor_mul(out=hct[:], in0=ht[:], in1=pscv))
                S.op("vector", lambda e: e.tensor_add(out=yac[:], in0=yac[:], in1=hct[:]))
            S.dma(y[dg * 128:(dg + 1) * 128, :], yac[:], queue="gpsimd")


def emit_scan3(nc, ctx, S, io, layer):
    NST = 64 if layer == 0 else 16
    a_dt = F32 if layer == 0 else BF16
    G = 4
    NG = 16 // G
    tg = "sn_"
    Wb, Wc = tg + "wb", tg + "wc"
    P, AE, AX, V, Y, PP = (tg + "pe", tg + "ae", tg + "ax", tg + "dve",
                           tg + "yad", tg + "pp")

    delta = io["delta"]
    duf16 = io["du_f16"]
    Bm = io["Bm"]
    Cm = io["Cm"]
    alog = io["alog"]
    y = io["y"]

    dl = ctx.sbuf("dl", [128, G, T], F32)
    duf = ctx.sbuf("duf", [128, G, T], F16)
    alr = ctx.sbuf("alr", [128, 16, NST], a_dt)
    asb = ctx.sbuf("asb", [128, 16, NST], F32)
    brow = ctx.sbuf("brow", [1, 2, T], F32)
    crow = ctx.sbuf("crow", [1, 2, T], F32)
    dat = ctx.sbuf("dat", [128, 2, T], F16)
    besb = ctx.sbuf("besb", [128, 2, T], F16)
    cesb = ctx.sbuf("cesb", [128, 2, T], F16)
    dbut = ctx.sbuf("dbut", [128, 2, T], F16)
    ht = ctx.sbuf("ht", [128, 2, T], F16)
    hct = ctx.sbuf("hct", [128, 4, T], F16)
    yac = ctx.sbuf("yac", [128, G, T], F32)
    ones_row = ctx.sbuf("ones_row", [1, 128], F32)
    psB, psC = ctx.psA, ctx.psB

    S.op("vector", lambda e: e.memset(ones_row[:], 1.0))
    S.dma(alr[:], alog.rearrange("(g p) n -> p g n", p=128))
    S.op("scalar", lambda e: e.activation(
        out=asb[:].rearrange("p g n -> p (g n)"),
        in_=alr[:].rearrange("p g n -> p (g n)"), func=AF.Exp))
    S.op("vector", lambda e: e.tensor_single_scalar(
        out=asb[:].rearrange("p g n -> p (g n)"),
        in_=asb[:].rearrange("p g n -> p (g n)"), scalar=-1.0, op=ALU.mult))
    q0 = ctx.val("q")

    for gg in range(NG):
        for dgl in range(G):
            dg = gg * G + dgl
            w = [("q", q0)] if gg == 0 else [(AX, G * NST * gg)]
            ctx.emit("sync", (lambda dg=dg, dgl=dgl: lambda e: e.dma_start(
                out=dl[:, dgl, :], in_=delta[dg * 128:(dg + 1) * 128, :]))(),
                waits=tuple(w), fire=ctx.fire(tg + "dl" + str(dgl), 16))
            ctx.emit("sync", (lambda dg=dg, dgl=dgl: lambda e: e.dma_start(
                out=duf[:, dgl, :], in_=duf16[dg * 128:(dg + 1) * 128, :]))(),
                waits=tuple(([("q", q0)] if gg == 0 else
                             [(V, 3 * (G * NST * gg - G + dgl) + 1)])),
                fire=ctx.fire(tg + "df" + str(dgl), 16))
            ctx.emit("gpsimd", (lambda dgl=dgl: lambda e: e.memset(
                yac[:, dgl, :], 0.0))(),
                waits=tuple(([("q", q0)] if gg == 0 else
                             [(tg + "ys" + str(dgl), 16 * gg)])),
                fire=ctx.fire(PP, 1))
        for n in range(NST):
            i = gg * NST + n
            p = i % 2
            rw = [("q", q0)] if i == 0 else ([(P, i - 1)] if i >= 2 else [])
            ctx.emit("sync", (lambda n=n, p=p: lambda e: e.dma_start(
                out=brow[:, p, :], in_=Bm[n:n + 1, :]))(),
                waits=tuple(rw), fire=ctx.fire(Wb + str(p), 16))
            ctx.emit("sync", (lambda n=n, p=p: lambda e: e.dma_start(
                out=crow[:, p, :], in_=Cm[n:n + 1, :]))(),
                waits=tuple(rw), fire=ctx.fire(Wc + str(p), 16))
            for fc in range(4):
                wts = []
                if fc == 0:
                    wts = [(Wb + str(p), 16 * (i // 2 + 1)),
                           (Wc + str(p), 16 * (i // 2 + 1)), (AE, 2 * i)]
                    if i == 0:
                        wts.append(("q", q0))
                ctx.emit("tensor", (lambda p=p, fc=fc: lambda e: e.matmul(
                    out=psB[:, fc, :], lhsT=ones_row[:],
                    rhs=brow[:, p, fc * 512:(fc + 1) * 512],
                    start=True, stop=True))(),
                    waits=tuple(wts))
            for fc in range(4):
                ctx.emit("tensor", (lambda p=p, fc=fc: lambda e: e.matmul(
                    out=psC[:, fc, :], lhsT=ones_row[:],
                    rhs=crow[:, p, fc * 512:(fc + 1) * 512],
                    start=True, stop=True))(),
                    fire=ctx.fire(P, 1) if fc == 3 else None)
            ctx.emit("scalar", (lambda p=p: lambda e: e.activation(
                out=besb[:, p, :],
                in_=psB[:].rearrange("p a f -> p (a f)"),
                func=AF.Identity))(),
                waits=tuple([(P, i + 1)]
                            + ([(V, 3 * (G * (i - 1)) - 2)] if i >= 2 else [])),
                fire=ctx.fire(AE, 1))
            ctx.emit("scalar", (lambda p=p: lambda e: e.activation(
                out=cesb[:, p, :],
                in_=psC[:].rearrange("p a f -> p (a f)"),
                func=AF.Identity))(),
                waits=tuple([(V, 3 * (G * (i - 1)))] if i >= 2 else []),
                fire=ctx.fire(AE, 1))
            for dgl in range(G):
                dg = gg * G + dgl
                e_ = i * G + dgl
                pe = e_ % 2
                ctx.emit("scalar", (lambda dgl=dgl, dg=dg, n=n, pe=pe, gg=gg:
                                    lambda e: e.activation(
                    out=dat[:, pe, :], in_=dl[:, dgl, :], func=AF.Exp,
                    scale=asb[:, dg, n:n + 1]))(),
                    waits=tuple([(tg + "dl" + str(dgl), 16 * (gg + 1))]
                                + ([(V, 3 * (e_ - 2) + 2)] if e_ >= 2 else
                                   [("q", q0)])),
                    fire=ctx.fire(AX, 1))
                ctx.emit("vector", (lambda dgl=dgl, p=p, pe=pe, gg=gg:
                                    lambda e: e.tensor_mul(
                    out=dbut[:, pe, :], in0=duf[:, dgl, :],
                    in1=besb[:, p, :]))(),
                    waits=tuple([(AE, 2 * i + 1),
                                 (tg + "df" + str(dgl), 16 * (gg + 1))]
                                + ([(V, 3 * (e_ - 2) + 2)] if e_ >= 2 else [])),
                    fire=ctx.fire(V, 1))
                ctx.emit("vector", (lambda pe=pe, e_=e_, i=i, dgl=dgl:
                                    lambda e: e.tensor_tensor_scan(
                    out=ht[:, pe, :], data0=dat[:, pe, :],
                    data1=dbut[:, pe, :],
                    initial=0.0, op0=ALU.mult, op1=ALU.add))(),
                    waits=((AX, G * i + dgl + 1), (V, 3 * e_ + 1)),
                    fire=ctx.fire(V, 1))
                pe4 = e_ % 4
                ctx.emit("vector", (lambda pe4=pe4, pe=pe, p=p: lambda e: e.tensor_mul(
                    out=hct[:, pe4, :], in0=ht[:, pe, :], in1=cesb[:, p, :]))(),
                    waits=tuple([(V, 3 * e_ + 2), (AE, 2 * i + 2)]
                                + ([(Y, e_ - 3)] if e_ >= 3 else [])),
                    fire=ctx.fire(V, 1))
                ctx.emit("gpsimd", (lambda dgl=dgl, pe4=pe4: lambda e: e.tensor_add(
                    out=yac[:, dgl, :], in0=yac[:, dgl, :],
                    in1=hct[:, pe4, :]))(),
                    waits=((V, 3 * e_ + 3), (PP, gg * G + dgl + 1)),
                    fire=ctx.fire(Y, 1))
        for dgl in range(G):
            dg = gg * G + dgl
            ctx.emit("gpsimd", (lambda dg=dg, dgl=dgl: lambda e: e.dma_start(
                out=y[dg * 128:(dg + 1) * 128, :], in_=yac[:, dgl, :]))(),
                waits=((Y, (gg + 1) * NST * G),),
                fire=ctx.fire(tg + "ys" + str(dgl), 16))
    ctx.emit("vector", lambda e: e.engine_nop(),
             waits=tuple((tg + "ys" + str(d), 16 * NG) for d in range(G)),
             fire=ctx.fire("q", 1))


def build_scan(layer, dg0=0, dg1=16):
    nc, stack, ctx = new_prog()
    S = Seq(ctx)
    with stack:
        io = scan_io(nc, layer, lambda n, s, d, o: (dout if o else din)(nc, n, s, d))
        ctx.psA = ctx.psum("psA", [128, 4, 512], F32)
        ctx.psB = ctx.psum("psB", [128, 4, 512], F32)
        ctx.phase("scan")
        emit_scan3(nc, ctx, S, io, layer)
        S.done()
    return nc



def l0out_io(nc, mk):
    io = {}
    io["y"] = mk("y", (DI, T), F32, 0)
    io["u_f"] = mk("u_f", (DI, T), F32, 0)
    io["z_bf"] = mk("z_bf", (DI, T), BF16, 0)
    io["d0"] = mk("d0", (DI,), F32, 0)
    io["wout_bf"] = mk("wout_bf", (DI, D), BF16, 0)
    io["xn_f"] = mk("xn_f", (D, T), F32, 0)
    io["xn_bf"] = mk("xn_bf", (D, T), BF16, 0)
    io["w1_bf"] = mk("w1_bf", (D, 4096), BF16, 0)
    io["b1"] = mk("b1", (4096,), F32, 0)
    io["w2_bf"] = mk("w2_bf", (4096, D), BF16, 0)
    io["b2"] = mk("b2", (D,), F32, 0)
    io["x"] = mk("x", (D, T), F32, 0)
    io["attn0"] = mk("attn0", (D, T), F32, 1)
    io["h_bf"] = mk("h_bf", (4096, T), BF16, 1)
    io["x1"] = mk("x1", (D, T), F32, 1)
    return io


def emit_l0out(nc, ctx, S, io):
    H = T // 2

    y = io["y"]
    u_f = io["u_f"]
    z_bf = io["z_bf"]
    d0 = io["d0"]
    wout = io["wout_bf"]
    xn_f = io["xn_f"]
    xn_bf = io["xn_bf"]
    w1 = io["w1_bf"]
    b1 = io["b1"]
    w2 = io["w2_bf"]
    b2 = io["b2"]
    x = io["x"]
    attn0 = io["attn0"]
    h_bf = io["h_bf"]
    x1 = io["x1"]

    KD = D // 128
    if True:
        gbuf = ctx.sbuf("gbuf", [128, 16, T], BF16)
        xnb = ctx.sbuf("xnb", [128, KD, T], BF16)
        t0 = ctx.sbuf("t0", [128, T], F32)
        t1 = ctx.sbuf("t1", [128, T], F32)
        dcol = ctx.sbuf("dcol", [128, 16], F32)
        zt = ctx.sbuf("zt", [128, T], BF16)
        wom = ctx.sbuf("wom", [128, 16, 128], BF16)
        w1m = ctx.sbuf("w1m", [128, KD, 128], BF16)
        b1c = ctx.sbuf("b1c", [128, 32], F32)
        b2c = ctx.sbuf("b2c", [128, 8], F32)
        tb = ctx.sbuf("tb", [128, 4, H], F32)
        hb = ctx.sbuf("hb", [128, H], BF16)
        hq = ctx.sbuf("hq", [128, 32, 512], BF16)
        w2m = ctx.sbuf("w2m", [128, 32, 128], BF16)
        knot = ctx.sbuf("knot", [128, 18], F32)
        ps_big = ctx.psA
        psb2 = ps_big[:, 0:2, :].rearrange("p a f -> p (a f)")
        psq = ps_big[:, 0:1, :].rearrange("p a f -> p (a f)")

        sgn = []
        for i, (vk, ck) in enumerate(PWL_KNOTS):
            S.op("vector", lambda e, i=i, vk=vk, ck=ck: e.memset(
                knot[:, i:i + 1], -abs(ck) * vk))
            sgn.append(1.0 if ck >= 0 else -1.0)
        S.op("vector", lambda e: e.memset(knot[:, 16:17], -1024.0))
        S.dma(dcol[:], d0.rearrange("(g p) -> p g", p=128), slow=True)
        S.dma(b1c[:], b1.rearrange("(g p) -> p g", p=128), slow=True)
        S.dma(b2c[:], b2.rearrange("(g p) -> p g", p=128), slow=True)
        S.op("vector", lambda e: e.tensor_single_scalar(
            out=b2c[:], in_=b2c[:], scalar=256.0, op=ALU.mult))

        for g in range(16):
            S.dma(t0[:], y[g * 128:(g + 1) * 128, :])
            S.dma(t1[:], u_f[g * 128:(g + 1) * 128, :])
            S.dma(zt[:], z_bf[g * 128:(g + 1) * 128, :])
            S.op("vector", lambda e, g=g: e.scalar_tensor_tensor(
                out=t0[:], in0=t1[:], scalar=dcol[:, g:g + 1], in1=t0[:],
                op0=ALU.mult, op1=ALU.add))
            S.op("scalar", lambda e: e.activation(out=t1[:], in_=zt[:], func=AF.Silu))
            S.op("vector", lambda e, g=g: e.tensor_mul(
                out=gbuf[:, g, :], in0=t0[:], in1=t1[:]))

        for m in range(KD):
            S.dma(wom[:], wout.rearrange("(k p) m -> p k m", p=128)[:, :, m * 128:(m + 1) * 128])
            for th in range(2):
                for k in range(16):
                    for j in range(2):
                        S.op("tensor", lambda e, k=k, th=th, j=j: e.matmul(
                            out=ps_big[:, j, :], lhsT=wom[:, k, :],
                            rhs=gbuf[:, k, th * H + j * 512:th * H + (j + 1) * 512],
                            start=(k == 0), stop=(k == 15)))
                S.dma(t0[:, 0:H], xn_f[m * 128:(m + 1) * 128, th * H:(th + 1) * H])
                S.op("vector", lambda e: e.tensor_add(
                    out=t1[:, 0:H], in0=t0[:, 0:H], in1=psb2))
                S.dma(attn0[m * 128:(m + 1) * 128, th * H:(th + 1) * H],
                      t1[:, 0:H], queue="gpsimd")

        for k in range(KD):
            S.dma(xnb[:, k, :], xn_bf[k * 128:(k + 1) * 128, :])
        for m in range(32):
            S.dma(w1m[:], w1.rearrange("(k p) m -> p k m", p=128)[:, :, m * 128:(m + 1) * 128])
            for th in range(2):
                for k in range(KD):
                    for j in range(2):
                        S.op("tensor", lambda e, k=k, th=th, j=j: e.matmul(
                            out=ps_big[:, j, :], lhsT=w1m[:, k, :],
                            rhs=xnb[:, k, th * H + j * 512:th * H + (j + 1) * 512],
                            start=(k == 0), stop=(k == KD - 1)))
                S.op("vector", lambda e, m=m: e.tensor_scalar(
                    out=tb[:, 0, :], in0=psb2, scalar1=b1c[:, m:m + 1],
                    scalar2=256.0, op0=ALU.add, op1=ALU.mult))
                S.op("vector", lambda e: e.tensor_scalar(
                    out=tb[:, 1, :], in0=tb[:, 0, :], scalar1=RNE_C,
                    scalar2=RNE_C, op0=ALU.add, op1=ALU.subtract))
                S.op("scalar", lambda e: e.activation(
                    out=tb[:, 2, :], in_=tb[:, 1, :], func=AF.Abs))
                S.op("vector", lambda e: e.tensor_single_scalar(
                    out=tb[:, 3, :], in_=tb[:, 2, :], scalar=PWL_S0, op=ALU.mult))
                for i, (vk, ck) in enumerate(PWL_KNOTS):
                    S.op("scalar", lambda e, i=i, ck=ck: e.activation(
                        out=tb[:, 0, :], in_=tb[:, 2, :], func=AF.Relu,
                        scale=abs(ck), bias=knot[:, i:i + 1]))
                    S.op("vector", lambda e, i=i: e.scalar_tensor_tensor(
                        out=tb[:, 3, :], in0=tb[:, 0, :], scalar=sgn[i],
                        in1=tb[:, 3, :], op0=ALU.mult, op1=ALU.add))
                S.op("scalar", lambda e: e.activation(
                    out=tb[:, 0, :], in_=tb[:, 2, :], func=AF.Relu,
                    bias=knot[:, 16:17]))
                S.op("vector", lambda e: e.tensor_scalar(
                    out=tb[:, 0, :], in0=tb[:, 0, :], scalar1=1.0,
                    scalar2=PWL_J256, op0=ALU.min, op1=ALU.mult))
                S.op("vector", lambda e: e.tensor_add(
                    out=tb[:, 3, :], in0=tb[:, 3, :], in1=tb[:, 0, :]))
                S.op("vector", lambda e: e.scalar_tensor_tensor(
                    out=tb[:, 0, :], in0=tb[:, 1, :], scalar=0.5,
                    in1=tb[:, 3, :], op0=ALU.mult, op1=ALU.add))
                S.op("vector", lambda e: e.tensor_scalar(
                    out=tb[:, 1, :], in0=tb[:, 0, :], scalar1=RNE_C,
                    scalar2=RNE_C, op0=ALU.add, op1=ALU.subtract))
                S.op("vector", lambda e: e.tensor_copy(out=hb[:], in_=tb[:, 1, :]))
                S.dma(h_bf[m * 128:(m + 1) * 128, th * H:(th + 1) * H],
                      hb[:], queue="gpsimd")

        for f4 in range(4):
            for k in range(32):
                S.dma(hq[:, k, :], h_bf[k * 128:(k + 1) * 128, f4 * 512:(f4 + 1) * 512])
            for m in range(KD):
                S.dma(w2m[:], w2.rearrange("(k p) m -> p k m", p=128)[:, :, m * 128:(m + 1) * 128])
                for k in range(32):
                    S.op("tensor", lambda e, k=k: e.matmul(
                        out=psq, lhsT=w2m[:, k, :], rhs=hq[:, k, :],
                        start=(k == 0), stop=(k == 31)))
                S.op("vector", lambda e, m=m: e.tensor_scalar(
                    out=tb[:, 0, 0:512], in0=psq, scalar1=b2c[:, m:m + 1],
                    scalar2=RNE_C, op0=ALU.add, op1=ALU.add))
                S.op("vector", lambda e: e.tensor_scalar(
                    out=tb[:, 1, 0:512], in0=tb[:, 0, 0:512], scalar1=RNE_C,
                    scalar2=1.0 / 256.0, op0=ALU.subtract, op1=ALU.mult))
                S.dma(tb[:, 2, 0:512], x[m * 128:(m + 1) * 128, f4 * 512:(f4 + 1) * 512])
                S.dma(tb[:, 3, 0:512], attn0[m * 128:(m + 1) * 128, f4 * 512:(f4 + 1) * 512])
                S.op("vector", lambda e: e.tensor_add(
                    out=tb[:, 0, 0:512], in0=tb[:, 2, 0:512], in1=tb[:, 3, 0:512]))
                S.op("vector", lambda e: e.tensor_add(
                    out=tb[:, 1, 0:512], in0=tb[:, 0, 0:512], in1=tb[:, 1, 0:512]))
                S.dma(x1[m * 128:(m + 1) * 128, f4 * 512:(f4 + 1) * 512],
                      tb[:, 1, 0:512], queue="gpsimd")


def build_l0out():
    nc, stack, ctx = new_prog()
    S = Seq(ctx)
    with stack:
        io = l0out_io(nc, lambda n, s, d, o: (dout if o else din)(nc, n, s, d))
        ctx.psA = ctx.psum("psA", [128, 4, 512], F32)
        ctx.psB = ctx.psum("psB", [128, 4, 512], F32)
        ctx.phase("l0out")
        emit_l0out2(nc, ctx, S, io)
        S.done()
    return nc



def l1out_io(nc, mk):
    io = {}
    io["y"] = mk("y", (DI, T), F32, 0)
    io["u_f"] = mk("u_f", (DI, T), F32, 0)
    io["z_bf"] = mk("z_bf", (DI, T), BF16, 0)
    io["dskip"] = mk("dskip", (DI,), BF16, 0)
    io["opT"] = mk("opT", (DI, D), BF16, 0)
    io["xn_bf"] = mk("xn_bf", (D, T), BF16, 0)
    io["f1t_bf"] = mk("f1t_bf", (D, 4096), BF16, 0)
    io["w1s"] = mk("w1s", (4096,), F32, 0)
    io["f2t_bf"] = mk("f2t_bf", (D, 4096), BF16, 0)
    io["w2s"] = mk("w2s", (4096,), F32, 0)
    io["h1t_bf"] = mk("h1t_bf", (4096, 4096), BF16, 0)
    io["h1s"] = mk("h1s", (4096,), F32, 0)
    io["h2t_bf"] = mk("h2t_bf", (4096, 4096), BF16, 0)
    io["h2s"] = mk("h2s", (4096,), F32, 0)
    io["w3t_bf"] = mk("w3t_bf", (4096, D), BF16, 0)
    io["w3s"] = mk("w3s", (D,), F32, 0)
    io["x1"] = mk("x1", (D, T), F32, 0)
    io["attn1"] = mk("attn1", (D, T), BF16, 1)
    io["gf"] = mk("gf", (4096, T), BF16, 1)
    io["a1"] = mk("a1", (4096, T), BF16, 1)
    io["a2"] = mk("a2", (4096, T), BF16, 1)
    io["out"] = mk("out", (D, T), F32, 1)
    return io


def emit_l1out(nc, ctx, S, io):
    H = T // 2
    KD = D // 128

    y1 = io["y"]
    u1 = io["u_f"]
    z1 = io["z_bf"]
    dskip = io["dskip"]
    opt = io["opT"]
    xn1 = io["xn_bf"]
    f1t = io["f1t_bf"]
    w1s = io["w1s"]
    f2t = io["f2t_bf"]
    w2s = io["w2s"]
    h1t = io["h1t_bf"]
    h1s = io["h1s"]
    h2t = io["h2t_bf"]
    h2s = io["h2s"]
    w3t = io["w3t_bf"]
    w3s = io["w3s"]
    x1 = io["x1"]
    attn1 = io["attn1"]
    gf = io["gf"]
    a1 = io["a1"]
    a2 = io["a2"]
    out = io["out"]

    if True:
        gbuf = ctx.sbuf("gbuf", [128, 16, T], BF16)
        t0 = ctx.sbuf("t0", [128, T], F32)
        t1 = ctx.sbuf("t1", [128, T], F32)
        zt = ctx.sbuf("zt", [128, T], BF16)
        dcr = ctx.sbuf("dcr", [128, 16], BF16)
        dcol = ctx.sbuf("dcol", [128, 16], F32)
        wm = ctx.sbuf("wm", [128, 16, 128], BF16)
        wm2 = ctx.sbuf("wm2", [128, 16, 128], BF16)
        ab = ctx.sbuf("ab", [128, H], BF16)
        s1c = ctx.sbuf("s1c", [128, 32], F32)
        s2c = ctx.sbuf("s2c", [128, 32], F32)
        sh1c = ctx.sbuf("sh1c", [128, 32], F32)
        sh2c = ctx.sbuf("sh2c", [128, 32], F32)
        s3c = ctx.sbuf("s3c", [128, 8], F32)
        rq = ctx.sbuf("rq", [128, 32, 512], BF16)
        w32 = ctx.sbuf("w32", [128, 32, 128], BF16)
        ps_big = ctx.psA
        psb2 = ps_big[:, 0:2, :].rearrange("p a f -> p (a f)")
        psc2 = ps_big[:, 2:4, :].rearrange("p a f -> p (a f)")
        psq = ps_big[:, 0:1, :].rearrange("p a f -> p (a f)")

        S.dma(dcr[:], dskip.rearrange("(g p) -> p g", p=128), slow=True)
        S.op("vector", lambda e: e.tensor_copy(out=dcol[:], in_=dcr[:]))
        S.dma(s1c[:], w1s.rearrange("(g p) -> p g", p=128), slow=True)
        S.dma(s2c[:], w2s.rearrange("(g p) -> p g", p=128), slow=True)
        S.dma(sh1c[:], h1s.rearrange("(g p) -> p g", p=128), slow=True)
        S.dma(sh2c[:], h2s.rearrange("(g p) -> p g", p=128), slow=True)
        S.dma(s3c[:], w3s.rearrange("(g p) -> p g", p=128), slow=True)

        for g in range(16):
            S.dma(t0[:], y1[g * 128:(g + 1) * 128, :])
            S.dma(t1[:], u1[g * 128:(g + 1) * 128, :])
            S.dma(zt[:], z1[g * 128:(g + 1) * 128, :])
            S.op("vector", lambda e, g=g: e.scalar_tensor_tensor(
                out=t0[:], in0=t1[:], scalar=dcol[:, g:g + 1], in1=t0[:],
                op0=ALU.mult, op1=ALU.add))
            S.op("scalar", lambda e: e.activation(out=t1[:], in_=zt[:], func=AF.Silu))
            S.op("vector", lambda e, g=g: e.tensor_mul(
                out=gbuf[:, g, :], in0=t0[:], in1=t1[:]))

        for m in range(KD):
            S.dma(wm[:], opt.rearrange("(k p) m -> p k m", p=128)[:, :, m * 128:(m + 1) * 128])
            for th in range(2):
                for k in range(16):
                    for j in range(2):
                        S.op("tensor", lambda e, k=k, th=th, j=j: e.matmul(
                            out=ps_big[:, j, :], lhsT=wm[:, k, :],
                            rhs=gbuf[:, k, th * H + j * 512:th * H + (j + 1) * 512],
                            start=(k == 0), stop=(k == 15)))
                S.op("vector", lambda e: e.tensor_copy(out=ab[:], in_=psb2))
                S.dma(attn1[m * 128:(m + 1) * 128, th * H:(th + 1) * H],
                      ab[:], queue="gpsimd")

        for k in range(KD):
            S.dma(gbuf[:, k, :], xn1[k * 128:(k + 1) * 128, :])
        for m in range(32):
            S.dma(wm[:, 0:KD, :], f1t.rearrange("(k p) m -> p k m", p=128)[:, :, m * 128:(m + 1) * 128])
            S.dma(wm2[:, 0:KD, :], f2t.rearrange("(k p) m -> p k m", p=128)[:, :, m * 128:(m + 1) * 128])
            for th in range(2):
                for k in range(KD):
                    for j in range(2):
                        S.op("tensor", lambda e, k=k, th=th, j=j: e.matmul(
                            out=ps_big[:, j, :], lhsT=wm[:, k, :],
                            rhs=gbuf[:, k, th * H + j * 512:th * H + (j + 1) * 512],
                            start=(k == 0), stop=(k == KD - 1)))
                for k in range(KD):
                    for j in range(2):
                        S.op("tensor", lambda e, k=k, th=th, j=j: e.matmul(
                            out=ps_big[:, 2 + j, :], lhsT=wm2[:, k, :],
                            rhs=gbuf[:, k, th * H + j * 512:th * H + (j + 1) * 512],
                            start=(k == 0), stop=(k == KD - 1)))
                S.op("scalar", lambda e, m=m: e.activation(
                    out=t0[:, 0:H], in_=psb2, func=AF.Silu, scale=s1c[:, m:m + 1]))
                S.op("scalar", lambda e, m=m: e.activation(
                    out=t1[:, 0:H], in_=psc2, func=AF.Identity, scale=s2c[:, m:m + 1]))
                S.op("vector", lambda e: e.tensor_mul(
                    out=ab[:], in0=t0[:, 0:H], in1=t1[:, 0:H]))
                S.dma(gf[m * 128:(m + 1) * 128, th * H:(th + 1) * H],
                      ab[:], queue="gpsimd")

        for (src, dst, wmat, scol) in ((gf, a1, h1t, sh1c), (a1, a2, h2t, sh2c)):
            for f4 in range(4):
                for k in range(32):
                    S.dma(rq[:, k, :], src[k * 128:(k + 1) * 128,
                                           f4 * 512:(f4 + 1) * 512])
                for m in range(32):
                    S.dma(w32[:], wmat.rearrange("(k p) m -> p k m", p=128)[:, :, m * 128:(m + 1) * 128])
                    for k in range(32):
                        S.op("tensor", lambda e, k=k: e.matmul(
                            out=psq, lhsT=w32[:, k, :], rhs=rq[:, k, :],
                            start=(k == 0), stop=(k == 31)))
                    S.op("scalar", lambda e, m=m, scol=scol: e.activation(
                        out=ab[:, 0:512], in_=psq, func=AF.Gelu,
                        scale=scol[:, m:m + 1]))
                    S.dma(dst[m * 128:(m + 1) * 128, f4 * 512:(f4 + 1) * 512],
                          ab[:, 0:512], queue="gpsimd")

        for f4 in range(4):
            for k in range(32):
                S.dma(rq[:, k, :], a2[k * 128:(k + 1) * 128, f4 * 512:(f4 + 1) * 512])
            for m in range(KD):
                S.dma(w32[:], w3t.rearrange("(k p) m -> p k m", p=128)[:, :, m * 128:(m + 1) * 128])
                for k in range(32):
                    S.op("tensor", lambda e, k=k: e.matmul(
                        out=psq, lhsT=w32[:, k, :], rhs=rq[:, k, :],
                        start=(k == 0), stop=(k == 31)))
                S.op("scalar", lambda e, m=m: e.activation(
                    out=t0[:, 0:512], in_=psq, func=AF.Identity,
                    scale=s3c[:, m:m + 1]))
                S.dma(t1[:, 0:512], x1[m * 128:(m + 1) * 128, f4 * 512:(f4 + 1) * 512])
                S.dma(ab[:, 0:512], attn1[m * 128:(m + 1) * 128, f4 * 512:(f4 + 1) * 512])
                S.op("vector", lambda e: e.tensor_add(
                    out=t1[:, 0:512], in0=t1[:, 0:512], in1=ab[:, 0:512]))
                S.op("vector", lambda e: e.tensor_add(
                    out=t0[:, 0:512], in0=t0[:, 0:512], in1=t1[:, 0:512]))
                S.dma(out[m * 128:(m + 1) * 128, f4 * 512:(f4 + 1) * 512],
                      t0[:, 0:512], queue="gpsimd")


def build_l1out():
    nc, stack, ctx = new_prog()
    S = Seq(ctx)
    with stack:
        io = l1out_io(nc, lambda n, s, d, o: (dout if o else din)(nc, n, s, d))
        ctx.psA = ctx.psum("psA", [128, 4, 512], F32)
        ctx.psB = ctx.psum("psB", [128, 4, 512], F32)
        ctx.phase("l1out")
        emit_l1out2(nc, ctx, S, io)
        S.done()
    return nc


_PROGS = {}


def _prog(name, builder):
    if name not in _PROGS:
        _PROGS[name] = builder()
    return _PROGS[name]


def _run(name, builder, maps):
    res = run_bass_kernel_spmd(_prog(name, builder), maps, core_ids=list(range(8)))
    return res.results


def kernel(**inp):
    asc = np.ascontiguousarray
    g = {k: np.asarray(v) for k, v in inp.items()}
    x = g["L0_attn_x"]
    xfm = [asc(x[c].T) for c in range(8)]

    f1T = asc(g["L1_ffn_w1_w"].T)
    f2T = asc(g["L1_ffn_w2_w"].T)
    h1T = asc(g["L1_ffn_h1_w"].T)
    h2T = asc(g["L1_ffn_h2_w"].T)
    w3T = asc(g["L1_ffn_w3_w"].T)
    ipT = asc(g["L1_attn_in_proj_w"].T)
    opT = asc(g["L1_attn_out_proj_w"].T)
    xpT = asc(g["L1_attn_x_proj_w"].T)
    dpT = asc(g["L1_attn_dt_proj_w"].T)
    cw0 = asc(g["L0_attn_conv_w"].reshape(DI, 4))
    cw1 = asc(g["L1_attn_conv_w"].reshape(DI, 4))

    if FUSED:
        return kernel_fused(g, xfm, (f1T, f2T, h1T, h2T, w3T, ipT, opT, xpT,
                                     dpT, cw0, cw1))

    pm = {"win": g["L0_attn_W_in"], "wout": g["L0_attn_W_out"],
          "w1": g["L0_ffn_w1"], "w2": g["L0_ffn_w2"],
          "wxp": g["L0_attn_W_xproj"],
          "f1t": f1T, "f2t": f2T, "h1t": h1T, "h2t": h2T, "w3t": w3T}
    W = _run("prep", build_prep, [pm] * 8)[0]

    mA = [{"x": xfm[c], "lnw": g["L0_attn_ln_w"], "lnb": g["L0_attn_ln_b"],
           "win_bf": W["win_bf"], "convw": cw0, "convb": g["L0_attn_conv_b"],
           "wxp_bf": W["wxp_bf"], "wdt": g["L0_attn_W_dt"],
           "bdt": g["L0_attn_b_dt"]} for c in range(8)]
    rA = _run("l0a", lambda: build_lxa(0), mA)

    mS = [{"delta": rA[c]["delta"], "du": rA[c]["du"],
           "du_f16": rA[c]["du_f16"], "Bm": rA[c]["B0"],
           "Cm": rA[c]["C0"], "alog": g["L0_attn_A_log"]} for c in range(8)]
    rS = _run("scan0", lambda: build_scan(0), mS)

    mO = [{"y": rS[c]["y"], "u_f": rA[c]["u_f"], "z_bf": rA[c]["z_bf"],
           "d0": g["L0_attn_D"], "wout_bf": W["wout_bf"], "xn_f": rA[c]["xn_f"],
           "xn_bf": rA[c]["xn_bf"], "w1_bf": W["w1_bf"], "b1": g["L0_ffn_b1"],
           "w2_bf": W["w2_bf"], "b2": g["L0_ffn_b2"], "x": xfm[c]}
          for c in range(8)]
    rO = _run("l0out", build_l0out, mO)

    mA1 = [{"x": rO[c]["x1"], "win_bf": ipT, "convw": cw1,
            "convb": g["L1_attn_conv_b"], "wxp_bf": xpT, "wdt": dpT,
            "bdt": g["L1_attn_dt_proj_b"]} for c in range(8)]
    rA1 = _run("l1a", lambda: build_lxa(1), mA1)

    mS1 = [{"delta": rA1[c]["delta"], "du": rA1[c]["du"],
            "du_f16": rA1[c]["du_f16"], "Bm": rA1[c]["B0"],
            "Cm": rA1[c]["C0"], "alog": g["L1_attn_A_log"]} for c in range(8)]
    rS1 = _run("scan1", lambda: build_scan(1), mS1)

    mO1 = [{"y": rS1[c]["y"], "u_f": rA1[c]["u_f"], "z_bf": rA1[c]["z_bf"],
            "dskip": g["L1_attn_D_skip"], "opT": opT, "xn_bf": rA1[c]["xn_bf"],
            "f1t_bf": W["f1t_bf"], "w1s": g["L1_ffn_w1_s"],
            "f2t_bf": W["f2t_bf"], "w2s": g["L1_ffn_w2_s"],
            "h1t_bf": W["h1t_bf"], "h1s": g["L1_ffn_h1_s"],
            "h2t_bf": W["h2t_bf"], "h2s": g["L1_ffn_h2_s"],
            "w3t_bf": W["w3t_bf"], "w3s": g["L1_ffn_w3_s"],
            "x1": rO[c]["x1"]} for c in range(8)]
    rO1 = _run("l1out", build_l1out, mO1)

    return np.stack([asc(rO1[c]["out"].T) for c in range(8)], axis=0)



def build_fused():
    nc, stack, ctx = new_prog()
    S = Seq(ctx)
    with stack:
        ctx.psA = ctx.psum("psA", [128, 4, 512], F32)
        ctx.psB = ctx.psum("psB", [128, 4, 512], F32)

        def D_(n, s, d):
            return nc.dram_tensor(n, list(s), d, kind="Internal").ap()

        iop = prep_io(nc, lambda n, s, d, o: D_(n, s, d) if o else din(nc, n, s, d))
        ctx.phase("prep")
        emit_prep(nc, ctx, S, iop)

        def mk0(n, s, d, o):
            if n in ("win_bf", "wxp_bf"):
                return iop[n]
            if o:
                return D_("l0_" + n, s, d)
            return din(nc, "l0_" + n, s, d)
        io0 = lxa_io(nc, 0, mk0)
        ctx.phase("l0a")
        emit_lxa2(nc, ctx, S, io0, 0)

        y0 = D_("y0", (DI, T), F32)
        ios0 = {"delta": io0["delta"], "du": io0["du"],
                "du_f16": io0["du_f16"], "Bm": io0["B0"],
                "Cm": io0["C0"], "alog": din(nc, "alog0", (DI, 64), F32),
                "y": y0}
        ctx.phase("scan0")
        emit_scan3(nc, ctx, S, ios0, 0)

        def mko0(n, s, d, o):
            m = {"y": y0, "u_f": io0["u_f"], "z_bf": io0["z_bf"],
                 "wout_bf": iop["wout_bf"], "xn_f": io0["xn_f"],
                 "xn_bf": io0["xn_bf"], "w1_bf": iop["w1_bf"],
                 "w2_bf": iop["w2_bf"], "x": io0["x"]}
            if n in m:
                return m[n]
            if o:
                return D_("o0_" + n, s, d)
            return din(nc, "o0_" + n, s, d)
        ioo0 = l0out_io(nc, mko0)
        ctx.phase("l0out")
        emit_l0out2(nc, ctx, S, ioo0)

        def mk1(n, s, d, o):
            if n == "x":
                return ioo0["x1"]
            if o:
                return D_("l1_" + n, s, d)
            return din(nc, "l1_" + n, s, d)
        io1 = lxa_io(nc, 1, mk1)
        ctx.phase("l1a")
        emit_lxa2(nc, ctx, S, io1, 1)

        y1 = D_("y1", (DI, T), F32)
        ios1 = {"delta": io1["delta"], "du": io1["du"],
                "du_f16": io1["du_f16"], "Bm": io1["B0"],
                "Cm": io1["C0"], "alog": din(nc, "alog1", (DI, 16), BF16),
                "y": y1}
        ctx.phase("scan1")
        emit_scan3(nc, ctx, S, ios1, 1)

        def mko1(n, s, d, o):
            m = {"y": y1, "u_f": io1["u_f"], "z_bf": io1["z_bf"],
                 "xn_bf": io1["xn_bf"], "f1t_bf": iop["f1t_bf"],
                 "f2t_bf": iop["f2t_bf"], "h1t_bf": iop["h1t_bf"],
                 "h2t_bf": iop["h2t_bf"], "w3t_bf": iop["w3t_bf"],
                 "x1": ioo0["x1"]}
            if n in m:
                return m[n]
            if n == "out":
                return dout(nc, "out", s, d)
            if o:
                return D_("o1_" + n, s, d)
            return din(nc, "o1_" + n, s, d)
        ioo1 = l1out_io(nc, mko1)
        ctx.phase("l1out")
        emit_l1out2(nc, ctx, S, ioo1)
        S.done()
    return nc


FUSED = True


def kernel_fused(g, xfm, hosted):
    (f1T, f2T, h1T, h2T, w3T, ipT, opT, xpT, dpT, cw0, cw1) = hosted
    base = {"win": g["L0_attn_W_in"], "wout": g["L0_attn_W_out"],
            "w1": g["L0_ffn_w1"], "w2": g["L0_ffn_w2"],
            "wxp": g["L0_attn_W_xproj"],
            "f1t": f1T, "f2t": f2T, "h1t": h1T, "h2t": h2T, "w3t": w3T,
            "l0_lnw": g["L0_attn_ln_w"], "l0_lnb": g["L0_attn_ln_b"],
            "l0_convw": cw0, "l0_convb": g["L0_attn_conv_b"],
            "l0_wdt": g["L0_attn_W_dt"], "l0_bdt": g["L0_attn_b_dt"],
            "alog0": g["L0_attn_A_log"],
            "o0_d0": g["L0_attn_D"], "o0_b1": g["L0_ffn_b1"],
            "o0_b2": g["L0_ffn_b2"],
            "l1_win_bf": ipT, "l1_convw": cw1, "l1_convb": g["L1_attn_conv_b"],
            "l1_wxp_bf": xpT, "l1_wdt": dpT, "l1_bdt": g["L1_attn_dt_proj_b"],
            "alog1": g["L1_attn_A_log"],
            "o1_dskip": g["L1_attn_D_skip"], "o1_opT": opT,
            "o1_w1s": g["L1_ffn_w1_s"], "o1_w2s": g["L1_ffn_w2_s"],
            "o1_h1s": g["L1_ffn_h1_s"], "o1_h2s": g["L1_ffn_h2_s"],
            "o1_w3s": g["L1_ffn_w3_s"]}
    maps = [dict(base, l0_x=xfm[c]) for c in range(8)]
    res = _run("fused", build_fused, maps)
    return np.stack([np.ascontiguousarray(res[c]["out"].T) for c in range(8)],
                    axis=0)



class DStream:
    def __init__(self, ctx, name, npar=2):
        self.ctx = ctx
        self.name = name
        self.npar = npar
        self.n = [0] * npar

    def emit(self, queue, par, fn, waits=()):
        s = f"{self.name}{par}"
        self.n[par] += 1
        self.ctx.emit(queue, fn, waits=waits, fire=self.ctx.fire(s, 16))
        return (s, 16 * self.n[par])

    def tok(self, par, back=0):
        s = f"{self.name}{par}"
        return (s, 16 * max(0, self.n[par] - back))



def emit_scan2(nc, ctx, S, io, layer):
    NST = 64 if layer == 0 else 16
    a_dt = F32 if layer == 0 else BF16
    tg = "sc_"
    Wb, Wc = tg + "rwb", tg + "rwc"
    P, A, V, Y = tg + "pe", tg + "act", tg + "dve", tg + "yad"
    PDd, PDu, PDa = tg + "pdd", tg + "pdu", tg + "pda"
    PA, PV, PP = tg + "pa", tg + "pv", tg + "pp"
    YS0, YS1 = tg + "ys0", tg + "ys1"

    delta = io["delta"]
    duf16 = io["du_f16"]
    Bm = io["Bm"]
    Cm = io["Cm"]
    alog = io["alog"]
    y = io["y"]

    q0 = ctx.val("q")
    nb = NST
    NB = 16 * nb

    dl = ctx.sbuf("dl", [128, 2, T], F32)
    duf = ctx.sbuf("duf", [128, 2, T], F16)
    alr = ctx.sbuf("alr", [128, 2, NST], a_dt)
    asb0 = ctx.sbuf("asb0", [128, 2, NST], F32)
    asb = ctx.sbuf("asb", [128, 2, NST], F32)
    brow = ctx.sbuf("brow", [1, 2, T], F32)
    crow = ctx.sbuf("crow", [1, 2, T], F32)
    dat = ctx.sbuf("dat", [128, 2, T], F16)
    besb = ctx.sbuf("besb", [128, 2, T], F16)
    cesb = ctx.sbuf("cesb", [128, 2, T], F16)
    dbut = ctx.sbuf("dbut", [128, 2, T], F16)
    ht = ctx.sbuf("ht", [128, 2, T], F16)
    hct = ctx.sbuf("hct", [128, 2, T], F16)
    yac = ctx.sbuf("yac", [128, 2, T], F32)
    ones_row = ctx.sbuf("ones_row", [1, 128], F32)
    psB, psC = ctx.psA, ctx.psB

    S.op("vector", lambda e: e.memset(ones_row[:], 1.0))
    q0 = ctx.val("q")

    def qw(i):
        return [("q", q0)] if i == 0 else []

    for dg in range(16):
        g2 = dg % 2
        pw = ([("q", q0)] if dg < 2 else
              [(A, 3 * nb * (dg - 1)), (V, 3 * nb * (dg - 1)), (PA, dg)])
        ctx.emit("sync", (lambda dg=dg, g2=g2: lambda e: e.dma_start(
            out=dl[:, g2, :], in_=delta[dg * 128:(dg + 1) * 128, :]))(),
            waits=tuple(pw + ([(PDd, 16)] if dg == 1 else [])),
            fire=ctx.fire(PDd, 16))
        ctx.emit("sync", (lambda dg=dg, g2=g2: lambda e: e.dma_start(
            out=duf[:, g2, :], in_=duf16[dg * 128:(dg + 1) * 128, :]))(),
            waits=tuple(pw + ([(PDu, 16)] if dg == 1 else [])),
            fire=ctx.fire(PDu, 16))
        ctx.emit("sync", (lambda dg=dg, g2=g2: lambda e: e.dma_start(
            out=alr[:, g2, :], in_=alog[dg * 128:(dg + 1) * 128, :]))(),
            waits=tuple(pw + ([(PDa, 16)] if dg == 1 else [])),
            fire=ctx.fire(PDa, 16))
        ctx.emit("scalar", (lambda g2=g2: lambda e: e.activation(
            out=asb0[:, g2, :], in_=alr[:, g2, :], func=AF.Exp))(),
            waits=((PDd, 16 * (dg + 1)), (PDu, 16 * (dg + 1)),
                   (PDa, 16 * (dg + 1)), (PV, max(0, dg - 1))),
            fire=ctx.fire(PA, 1))
        ctx.emit("vector", (lambda g2=g2: lambda e: e.tensor_single_scalar(
            out=asb[:, g2, :], in_=asb0[:, g2, :], scalar=-1.0, op=ALU.mult))(),
            waits=((PA, dg + 1),) + ((A, 3 * nb * (dg - 1)),) if dg >= 2 else ((PA, dg + 1),),
            fire=ctx.fire(PV, 1))
        ctx.emit("gpsimd", (lambda g2=g2: lambda e: e.memset(yac[:, g2, :], 0.0))(),
            waits=tuple(([("q", q0)] if dg < 2 else [])
                        + [((YS0 if g2 == 0 else YS1), 16 * (dg // 2))]),
            fire=ctx.fire(PP, 1))

        for n in range(nb):
            i = dg * nb + n
            p = i % 2
            ctx.emit("sync", (lambda n=n, p=p: lambda e: e.dma_start(
                out=brow[:, p, :], in_=Bm[n:n + 1, :]))(),
                waits=tuple(qw(i) + ([(P, i - 1)] if i >= 2 else [])),
                fire=ctx.fire(Wb + str(p), 16))
            ctx.emit("sync", (lambda n=n, p=p: lambda e: e.dma_start(
                out=crow[:, p, :], in_=Cm[n:n + 1, :]))(),
                waits=tuple(qw(i) + ([(P, i - 1)] if i >= 2 else [])),
                fire=ctx.fire(Wc + str(p), 16))
            for fc in range(4):
                wts = ([(Wb + str(p), 16 * (i // 2 + 1)),
                        (Wc + str(p), 16 * (i // 2 + 1)), (A, 3 * i)]
                       if fc == 0 else [])
                ctx.emit("tensor", (lambda p=p, fc=fc: lambda e: e.matmul(
                    out=psB[:, fc, :], lhsT=ones_row[:],
                    rhs=brow[:, p, fc * 512:(fc + 1) * 512],
                    start=True, stop=True))(),
                    waits=tuple(qw(i) + wts))
            for fc in range(4):
                ctx.emit("tensor", (lambda p=p, fc=fc: lambda e: e.matmul(
                    out=psC[:, fc, :], lhsT=ones_row[:],
                    rhs=crow[:, p, fc * 512:(fc + 1) * 512],
                    start=True, stop=True))(),
                    fire=ctx.fire(P, 1) if fc == 3 else None)
            ctx.emit("scalar", (lambda g2=g2, n=n, p=p: lambda e: e.activation(
                out=dat[:, p, :], in_=dl[:, g2, :], func=AF.Exp,
                scale=asb[:, g2, n:n + 1]))(),
                waits=tuple(qw(i) + [(PV, dg + 1)] + ([(V, 3 * (i - 1))] if i >= 2 else [])),
                fire=ctx.fire(A, 1))
            ctx.emit("scalar", (lambda p=p: lambda e: e.activation(
                out=besb[:, p, :],
                in_=psB[:].rearrange("p a f -> p (a f)"),
                func=AF.Identity))(),
                waits=((P, i + 1),), fire=ctx.fire(A, 1))
            ctx.emit("scalar", (lambda p=p: lambda e: e.activation(
                out=cesb[:, p, :],
                in_=psC[:].rearrange("p a f -> p (a f)"),
                func=AF.Identity))(),
                fire=ctx.fire(A, 1))
            ctx.emit("vector", (lambda g2=g2, p=p: lambda e: e.tensor_mul(
                out=dbut[:, p, :], in0=duf[:, g2, :], in1=besb[:, p, :]))(),
                waits=tuple(qw(i) + [(A, 3 * i + 2)]),
                fire=ctx.fire(V, 1))
            ctx.emit("vector", (lambda p=p: lambda e: e.tensor_tensor_scan(
                out=ht[:, p, :], data0=dat[:, p, :], data1=dbut[:, p, :],
                initial=0.0, op0=ALU.mult, op1=ALU.add))(),
                waits=((V, 3 * i + 1),), fire=ctx.fire(V, 1))
            ctx.emit("vector", (lambda p=p: lambda e: e.tensor_mul(
                out=hct[:, p, :], in0=ht[:, p, :], in1=cesb[:, p, :]))(),
                waits=((V, 3 * i + 2), (A, 3 * i + 3)) + (((Y, i - 1),) if i >= 2 else ()),
                fire=ctx.fire(V, 1))
            ctx.emit("gpsimd", (lambda g2=g2, p=p: lambda e: e.tensor_add(
                out=yac[:, g2, :], in0=yac[:, g2, :], in1=hct[:, p, :]))(),
                waits=((V, 3 * i + 3), (Y, i), (PP, dg + 1)),
                fire=ctx.fire(Y, 1))
        ctx.emit("gpsimd", (lambda dg=dg, g2=g2: lambda e: e.dma_start(
            out=y[dg * 128:(dg + 1) * 128, :], in_=yac[:, g2, :]))(),
            waits=((Y, nb * (dg + 1)),),
            fire=ctx.fire(YS0 if g2 == 0 else YS1, 16))
    ctx.emit("vector", lambda e: e.engine_nop(),
             waits=((YS0, 16 * 8), (YS1, 16 * 8)), fire=ctx.fire("q", 1))



def emit_l1out2(nc, ctx, S, io):
    H = T // 2
    KD = D // 128
    tg = "o1_"

    y1, u1, z1 = io["y"], io["u_f"], io["z_bf"]
    dskip, opt, xn1 = io["dskip"], io["opT"], io["xn_bf"]
    f1t, w1s, f2t, w2s = io["f1t_bf"], io["w1s"], io["f2t_bf"], io["w2s"]
    h1t, h1s, h2t, h2s = io["h1t_bf"], io["h1s"], io["h2t_bf"], io["h2s"]
    w3t, w3s, x1 = io["w3t_bf"], io["w3s"], io["x1"]
    attn1, gf, a1, a2, out = io["attn1"], io["gf"], io["a1"], io["a2"], io["out"]

    gbuf = ctx.sbuf("gbuf", [128, 16, T], BF16)
    t0 = ctx.sbuf("t0", [128, T], F32)
    t1 = ctx.sbuf("t1", [128, T], F32)
    zt = ctx.sbuf("zt", [128, T], BF16)
    dcr = ctx.sbuf("dcr", [128, 16], BF16)
    dcol = ctx.sbuf("dcol", [128, 16], F32)
    wm = ctx.sbuf("wm", [128, 2, 16, 128], BF16)
    wm2 = ctx.sbuf("wm2", [128, 2, 16, 128], BF16)
    ab = ctx.sbuf("ab", [128, 2, H], BF16)
    ev0 = ctx.sbuf("ev0", [128, 2, H], BF16)
    ev1 = ctx.sbuf("ev1", [128, 2, H], BF16)
    s1c = ctx.sbuf("s1c", [128, 32], F32)
    s2c = ctx.sbuf("s2c", [128, 32], F32)
    sh1c = ctx.sbuf("sh1c", [128, 32], F32)
    sh2c = ctx.sbuf("sh2c", [128, 32], F32)
    s3c = ctx.sbuf("s3c", [128, 8], F32)
    rq = ctx.sbuf("rq", [128, 2, 32, 512], BF16)
    w32 = ctx.sbuf("w32", [128, 2, 32, 128], BF16)
    fx = ctx.sbuf("fx", [128, 2, 512], F32)
    fa = ctx.sbuf("fa", [128, 2, 512], BF16)
    fo = ctx.sbuf("fo", [128, 2, 512], F32)
    ps = ctx.psA

    q0 = ctx.val("q")
    S.dma(dcr[:], dskip.rearrange("(g p) -> p g", p=128), slow=True)
    S.op("vector", lambda e: e.tensor_copy(out=dcol[:], in_=dcr[:]))
    S.dma(s1c[:], w1s.rearrange("(g p) -> p g", p=128), slow=True)
    S.dma(s2c[:], w2s.rearrange("(g p) -> p g", p=128), slow=True)
    S.dma(sh1c[:], h1s.rearrange("(g p) -> p g", p=128), slow=True)
    S.dma(sh2c[:], h2s.rearrange("(g p) -> p g", p=128), slow=True)
    S.dma(s3c[:], w3s.rearrange("(g p) -> p g", p=128), slow=True)
    q0 = ctx.val("q")

    for g in range(16):
        S.dma(t0[:], y1[g * 128:(g + 1) * 128, :])
        S.dma(t1[:], u1[g * 128:(g + 1) * 128, :])
        S.dma(zt[:], z1[g * 128:(g + 1) * 128, :])
        S.op("vector", (lambda g=g: lambda e: e.scalar_tensor_tensor(
            out=t0[:], in0=t1[:], scalar=dcol[:, g:g + 1],
            in1=t0[:], op0=ALU.mult, op1=ALU.add))())
        S.op("scalar", lambda e: e.activation(out=t1[:], in_=zt[:], func=AF.Silu))
        S.op("vector", (lambda g=g: lambda e: e.tensor_mul(
            out=gbuf[:, g, :], in0=t0[:], in1=t1[:]))())
    q1 = ctx.val("q")

    BL, BP, BE, BS = tg + "bl", tg + "bp", tg + "be", tg + "bs"
    optr = opt.rearrange("(k p) m -> p k m", p=128)
    for m in range(KD):
        pw = m % 2
        ctx.emit("sync", (lambda m=m, pw=pw: lambda e: e.dma_start(
            out=wm[:, pw, :, :], in_=optr[:, :, m * 128:(m + 1) * 128]))(),
            waits=tuple([("q", q1)] + ([(BP, 2 * (m - 1))] if m >= 2 else [])),
            fire=ctx.fire(BL + str(pw), 16))
        for th in range(2):
            it = m * 2 + th
            pb = it % 2
            sl = pb * 2
            for k in range(16):
                for j in range(2):
                    wts = []
                    if k == 0 and j == 0:
                        wts = [(BL + str(pw), 16 * (m // 2 + 1)), ("q", q1)]
                        if it >= 2:
                            wts.append((BE, it - 1))
                    ctx.emit("tensor", (lambda k=k, th=th, j=j, sl=sl, pw=pw: lambda e: e.matmul(
                        out=ps[:, sl + j, :], lhsT=wm[:, pw, k, :],
                        rhs=gbuf[:, k, th * H + j * 512:th * H + (j + 1) * 512],
                        start=(k == 0), stop=(k == 15)))(),
                        waits=tuple(wts),
                        fire=ctx.fire(BP, 1) if (k, j) == (15, 1) else None)
            ctx.emit("vector", (lambda it=it, sl=sl: lambda e: e.tensor_copy(
                out=ab[:, it % 2, :],
                in_=ps[:, sl:sl + 2, :].rearrange("p a f -> p (a f)")))(),
                waits=tuple([(BP, it + 1)]
                            + ([(BS + str(pb), 16 * (it // 2))] if it >= 2 else [])),
                fire=ctx.fire(BE, 1))
            ctx.emit("gpsimd", (lambda m=m, th=th, it=it, pb=pb: lambda e: e.dma_start(
                out=attn1[m * 128:(m + 1) * 128, th * H:(th + 1) * H],
                in_=ab[:, it % 2, :]))(),
                waits=((BE, it + 1),), fire=ctx.fire(BS + str(pb), 16))

    CL, CP, CEa, CEv, CS, CX = (tg + "cl", tg + "cp", tg + "cea", tg + "cev",
                                tg + "cs", tg + "cx")
    f1r = f1t.rearrange("(k p) m -> p k m", p=128)
    f2r = f2t.rearrange("(k p) m -> p k m", p=128)
    for k in range(KD):
        ctx.emit("sync", (lambda k=k: lambda e: e.dma_start(
            out=gbuf[:, k, :], in_=xn1[k * 128:(k + 1) * 128, :]))(),
            waits=((BP, 16),) if k == 0 else ((CX, 16 * k),),
            fire=ctx.fire(CX, 16))
    for m in range(32):
        pw = m % 2
        w = [(CP, 2 * (m - 1))] if m >= 2 else [(BP, 14 + 2 * m)]
        ctx.emit("sync", (lambda m=m, pw=pw: lambda e: e.dma_start(
            out=wm[:, pw, 0:KD, :], in_=f1r[:, :, m * 128:(m + 1) * 128]))(),
            waits=tuple(w), fire=ctx.fire(CL + "a" + str(pw), 16))
        ctx.emit("sync", (lambda m=m, pw=pw: lambda e: e.dma_start(
            out=wm2[:, pw, 0:KD, :], in_=f2r[:, :, m * 128:(m + 1) * 128]))(),
            waits=tuple(w), fire=ctx.fire(CL + "b" + str(pw), 16))
        for th in range(2):
            it = m * 2 + th
            for k in range(KD):
                for j in range(2):
                    wts = []
                    if k == 0 and j == 0:
                        wts = [(CL + "a" + str(pw), 16 * (m // 2 + 1)),
                               (CL + "b" + str(pw), 16 * (m // 2 + 1)),
                               (CX, 16 * KD)]
                        if it >= 1:
                            wts.append((CEa, 2 * it))
                        else:
                            wts.append((BE, 16))
                    ctx.emit("tensor", (lambda k=k, th=th, j=j, pw=pw: lambda e: e.matmul(
                        out=ps[:, j, :], lhsT=wm[:, pw, k, :],
                        rhs=gbuf[:, k, th * H + j * 512:th * H + (j + 1) * 512],
                        start=(k == 0), stop=(k == KD - 1)))(),
                        waits=tuple(wts))
            for k in range(KD):
                for j in range(2):
                    ctx.emit("tensor", (lambda k=k, th=th, j=j, pw=pw: lambda e: e.matmul(
                        out=ps[:, 2 + j, :], lhsT=wm2[:, pw, k, :],
                        rhs=gbuf[:, k, th * H + j * 512:th * H + (j + 1) * 512],
                        start=(k == 0), stop=(k == KD - 1)))(),
                        fire=ctx.fire(CP, 1) if (k, j) == (KD - 1, 1) else None)
            pb = it % 2
            ctx.emit("scalar", (lambda m=m, pb=pb: lambda e: e.activation(
                out=ev0[:, pb, :], in_=ps[:, 0:2, :].rearrange("p a f -> p (a f)"),
                func=AF.Silu, scale=s1c[:, m:m + 1]))(),
                waits=tuple([(CP, it + 1)] + ([(CEv, it - 1)] if it >= 2 else [])),
                fire=ctx.fire(CEa, 1))
            ctx.emit("scalar", (lambda m=m, pb=pb: lambda e: e.activation(
                out=ev1[:, pb, :], in_=ps[:, 2:4, :].rearrange("p a f -> p (a f)"),
                func=AF.Identity, scale=s2c[:, m:m + 1]))(),
                waits=((CEa, 2 * it + 1),), fire=ctx.fire(CEa, 1))
            ctx.emit("vector", (lambda pb=pb: lambda e: e.tensor_mul(
                out=ab[:, pb, :], in0=ev0[:, pb, :], in1=ev1[:, pb, :]))(),
                waits=tuple([(CEa, 2 * it + 2)]
                            + ([(CS + str(pb), 16 * (it // 2))] if it >= 2
                               else [(BS + str(pb), 16 * 8)])),
                fire=ctx.fire(CEv, 1))
            ctx.emit("gpsimd", (lambda m=m, th=th, pb=pb: lambda e: e.dma_start(
                out=gf[m * 128:(m + 1) * 128, th * H:(th + 1) * H],
                in_=ab[:, pb, :]))(),
                waits=((CEv, it + 1),), fire=ctx.fire(CS + str(pb), 16))

    def hstage(st, src, dst, wmat, scol, prev_done):
        L, R, P, E = st + "l", st + "r", st + "p", st + "e"
        wr = wmat.rearrange("(k p) m -> p k m", p=128)
        srcr = src.rearrange("(k p) m -> p k m", p=128)
        for f4 in range(4):
            pf = f4 % 2
            w = list(prev_done) if f4 == 0 else []
            if f4 >= 2:
                w.append((P, 32 * (f4 - 1)))
            ctx.emit("sync", (lambda f4=f4, pf=pf: lambda e: e.dma_start(
                out=rq[:, pf, :, :],
                in_=srcr[:, :, f4 * 512:(f4 + 1) * 512]))(),
                waits=tuple(w), fire=ctx.fire(R + str(pf), 16))
            for m in range(32):
                it = f4 * 32 + m
                pw = it % 2
                w = list(prev_done) if it == 0 else []
                if it >= 2:
                    w.append((P, it - 1))
                ctx.emit("scalar", (lambda it=it, m=m, pw=pw: lambda e: e.dma_start(
                    out=w32[:, pw, :, :], in_=wr[:, :, m * 128:(m + 1) * 128]))(),
                    waits=tuple(w), fire=ctx.fire(L + str(pw), 16))
                for k in range(32):
                    wts = []
                    if k == 0:
                        wts = [(L + str(pw), 16 * (it // 2 + 1)),
                               (R + str(pf), 16 * (f4 // 2 + 1))]
                        if it >= 2:
                            wts.append((E, it - 1))
                    ctx.emit("tensor", (lambda k=k, pw=pw, pf=pf, it=it: lambda e: e.matmul(
                        out=ps[:, it % 4, :], lhsT=w32[:, pw, k, :],
                        rhs=rq[:, pf, k, :], start=(k == 0), stop=(k == 31)))(),
                        waits=tuple(wts),
                        fire=ctx.fire(P, 1) if k == 31 else None)
                ctx.emit("scalar", (lambda it=it, m=m: lambda e: e.activation(
                    out=fa[:, it % 2, :], in_=ps[:, it % 4, :], func=AF.Gelu,
                    scale=scol[:, m:m + 1]))(),
                    waits=tuple([(P, it + 1)]
                                + ([(st + "s" + str(it % 2), 16 * (it // 2))]
                                   if it >= 2 else [])),
                    fire=ctx.fire(E, 1))
                ctx.emit("gpsimd", (lambda it=it, m=m, f4=f4: lambda e: e.dma_start(
                    out=dst[m * 128:(m + 1) * 128, f4 * 512:(f4 + 1) * 512],
                    in_=fa[:, it % 2, :]))(),
                    waits=((E, it + 1),), fire=ctx.fire(st + "s" + str(it % 2), 16))
        return [(st + "s0", 16 * 64), (st + "s1", 16 * 64)]

    dD = hstage(tg + "D", gf, a1, h1t, sh1c, [(CS + "0", 16 * 32), (CS + "1", 16 * 32)])
    dE = hstage(tg + "E", a1, a2, h2t, sh2c, dD)

    FL, FR, FP, FEa, FEv, FS = (tg + "fl", tg + "fr", tg + "fp", tg + "fea",
                                tg + "fev", tg + "fs")
    w3r = w3t.rearrange("(k p) m -> p k m", p=128)
    a2r = a2.rearrange("(k p) m -> p k m", p=128)
    for f4 in range(4):
        pf = f4 % 2
        w = (dE + [(BS + "0", 16 * 8), (BS + "1", 16 * 8)]) if f4 == 0 else []
        if f4 >= 2:
            w.append((FP, 8 * (f4 - 1)))
        ctx.emit("sync", (lambda f4=f4, pf=pf: lambda e: e.dma_start(
            out=rq[:, pf, :, :],
            in_=a2r[:, :, f4 * 512:(f4 + 1) * 512]))(),
            waits=tuple(w), fire=ctx.fire(FR + str(pf), 16))
        for m in range(KD):
            it = f4 * KD + m
            pw = it % 2
            w = list(dE) if it == 0 else []
            if it >= 2:
                w.append((FP, it - 1))
            ctx.emit("scalar", (lambda it=it, m=m, pw=pw: lambda e: e.dma_start(
                out=w32[:, pw, :, :], in_=w3r[:, :, m * 128:(m + 1) * 128]))(),
                waits=tuple(w), fire=ctx.fire(FL + "w" + str(pw), 16))
            ctx.emit("sync", (lambda it=it, m=m, f4=f4, pw=pw: lambda e: e.dma_start(
                out=fx[:, pw, :],
                in_=x1[m * 128:(m + 1) * 128, f4 * 512:(f4 + 1) * 512]))(),
                waits=((FEv, 2 * (it - 1)),) if it >= 2 else (),
                fire=ctx.fire(FL + "x" + str(pw), 16))
            ctx.emit("sync", (lambda it=it, m=m, f4=f4, pw=pw: lambda e: e.dma_start(
                out=fa[:, pw, :],
                in_=attn1[m * 128:(m + 1) * 128, f4 * 512:(f4 + 1) * 512]))(),
                waits=((FEv, 2 * (it - 1)),) if it >= 2 else (),
                fire=ctx.fire(FL + "a" + str(pw), 16))
            for k in range(32):
                wts = []
                if k == 0:
                    wts = [(FL + "w" + str(pw), 16 * (it // 2 + 1)),
                           (FR + str(pf), 16 * (f4 // 2 + 1))]
                    if it >= 2:
                        wts.append((FEa, it - 1))
                ctx.emit("tensor", (lambda k=k, pw=pw, pf=pf, it=it: lambda e: e.matmul(
                    out=ps[:, it % 4, :], lhsT=w32[:, pw, k, :],
                    rhs=rq[:, pf, k, :], start=(k == 0), stop=(k == 31)))(),
                    waits=tuple(wts),
                    fire=ctx.fire(FP, 1) if k == 31 else None)
            ctx.emit("scalar", (lambda it=it, m=m, pw=pw: lambda e: e.activation(
                out=fo[:, pw, :], in_=ps[:, it % 4, :], func=AF.Identity,
                scale=s3c[:, m:m + 1]))(),
                waits=tuple([(FP, it + 1)]
                            + ([(FS + str(pw), 16 * (it // 2))] if it >= 2 else [])),
                fire=ctx.fire(FEa, 1))
            ctx.emit("vector", (lambda pw=pw: lambda e: e.tensor_add(
                out=fx[:, pw, :], in0=fx[:, pw, :], in1=fa[:, pw, :]))(),
                waits=((FL + "x" + str(pw), 16 * (it // 2 + 1)),
                       (FL + "a" + str(pw), 16 * (it // 2 + 1))),
                fire=ctx.fire(FEv, 1))
            ctx.emit("vector", (lambda pw=pw: lambda e: e.tensor_add(
                out=fo[:, pw, :], in0=fo[:, pw, :], in1=fx[:, pw, :]))(),
                waits=((FEa, it + 1), (FEv, 2 * it + 1)),
                fire=ctx.fire(FEv, 1))
            ctx.emit("gpsimd", (lambda it=it, m=m, f4=f4, pw=pw: lambda e: e.dma_start(
                out=out[m * 128:(m + 1) * 128, f4 * 512:(f4 + 1) * 512],
                in_=fo[:, pw, :]))(),
                waits=((FEv, 2 * it + 2),), fire=ctx.fire(FS + str(pw), 16))
    ctx.emit("vector", lambda e: e.engine_nop(),
             waits=((FS + "0", 16 * 16), (FS + "1", 16 * 16),
                    (BS + "0", 16 * 8), (BS + "1", 16 * 8)),
             fire=ctx.fire("q", 1))



def emit_l0out2(nc, ctx, S, io):
    H = T // 2
    KD = D // 128
    tg = "o0_"

    y, u_f, z_bf, d0 = io["y"], io["u_f"], io["z_bf"], io["d0"]
    wout, xn_f, xn_bf = io["wout_bf"], io["xn_f"], io["xn_bf"]
    w1, b1, w2, b2, x = io["w1_bf"], io["b1"], io["w2_bf"], io["b2"], io["x"]
    attn0, h_bf, x1 = io["attn0"], io["h_bf"], io["x1"]

    gbuf = ctx.sbuf("gbuf", [128, 16, T], BF16)
    xnb = ctx.sbuf("xnb", [128, KD, T], BF16)
    t0 = ctx.sbuf("t0", [128, T], F32)
    t1 = ctx.sbuf("t1", [128, T], F32)
    zt = ctx.sbuf("zt", [128, T], BF16)
    dcol = ctx.sbuf("dcol", [128, 16], F32)
    wom = ctx.sbuf("wom", [128, 2, 16, 128], BF16)
    w1m = ctx.sbuf("w1m", [128, 2, KD, 128], BF16)
    b1c = ctx.sbuf("b1c", [128, 32], F32)
    b2c = ctx.sbuf("b2c", [128, 8], F32)
    knot = ctx.sbuf("knot", [128, 18], F32)
    xnt = ctx.sbuf("xnt", [128, 2, H], F32)
    abf = ctx.sbuf("abf", [128, 2, H], F32)
    s0b = ctx.sbuf("s0b", [128, 2, H], F32)
    qb = ctx.sbuf("qb", [128, 2, H], F32)
    vb = ctx.sbuf("vb", [128, 2, H], F32)
    acc = ctx.sbuf("acc", [128, 2, H], F32)
    rr = ctx.sbuf("rr", [128, 2, H], F32)
    hb = ctx.sbuf("hb", [128, 2, H], BF16)
    fx = ctx.sbuf("fx", [128, 2, 512], F32)
    fa = ctx.sbuf("fa", [128, 2, 512], F32)
    fo = ctx.sbuf("fo", [128, 2, 512], F32)
    ps = ctx.psA

    sgn = []
    for i, (vk, ck) in enumerate(PWL_KNOTS):
        S.op("vector", (lambda i=i, vk=vk, ck=ck: lambda e: e.memset(
            knot[:, i:i + 1], -abs(ck) * vk))())
        sgn.append(1.0 if ck >= 0 else -1.0)
    S.op("vector", lambda e: e.memset(knot[:, 16:17], -1024.0))
    S.dma(dcol[:], d0.rearrange("(g p) -> p g", p=128), slow=True)
    S.dma(b1c[:], b1.rearrange("(g p) -> p g", p=128), slow=True)
    S.dma(b2c[:], b2.rearrange("(g p) -> p g", p=128), slow=True)
    S.op("vector", lambda e: e.tensor_single_scalar(
        out=b2c[:], in_=b2c[:], scalar=256.0, op=ALU.mult))

    for g in range(16):
        S.dma(t0[:], y[g * 128:(g + 1) * 128, :])
        S.dma(t1[:], u_f[g * 128:(g + 1) * 128, :])
        S.dma(zt[:], z_bf[g * 128:(g + 1) * 128, :])
        S.op("vector", (lambda g=g: lambda e: e.scalar_tensor_tensor(
            out=t0[:], in0=t1[:], scalar=dcol[:, g:g + 1],
            in1=t0[:], op0=ALU.mult, op1=ALU.add))())
        S.op("scalar", lambda e: e.activation(out=t1[:], in_=zt[:], func=AF.Silu))
        S.op("vector", (lambda g=g: lambda e: e.tensor_mul(
            out=gbuf[:, g, :], in0=t0[:], in1=t1[:]))())
    for k in range(KD):
        S.dma(xnb[:, k, :], xn_bf[k * 128:(k + 1) * 128, :])
    q1 = ctx.val("q")

    BL, BX, BP, BE, BS = tg + "bl", tg + "bx", tg + "bp", tg + "be", tg + "bs"
    womr = wout.rearrange("(k p) m -> p k m", p=128)
    for m in range(KD):
        pw = m % 2
        ctx.emit("sync", (lambda m=m, pw=pw: lambda e: e.dma_start(
            out=wom[:, pw, :, :], in_=womr[:, :, m * 128:(m + 1) * 128]))(),
            waits=tuple([("q", q1)] + ([(BP, 2 * (m - 1))] if m >= 2 else [])),
            fire=ctx.fire(BL + str(pw), 16))
        for th in range(2):
            it = m * 2 + th
            pb = it % 2
            sl = pb * 2
            ctx.emit("sync", (lambda m=m, th=th, pb=pb: lambda e: e.dma_start(
                out=xnt[:, pb, :],
                in_=xn_f[m * 128:(m + 1) * 128, th * H:(th + 1) * H]))(),
                waits=((BE, 2 * (it - 1)),) if it >= 2 else (),
                fire=ctx.fire(BX + str(pb), 16))
            for k in range(16):
                for j in range(2):
                    wts = []
                    if k == 0 and j == 0:
                        wts = [(BL + str(pw), 16 * (m // 2 + 1)), ("q", q1)]
                        if it >= 2:
                            wts.append((BE, 2 * (it - 1)))
                    ctx.emit("tensor", (lambda k=k, th=th, j=j, sl=sl, pw=pw: lambda e: e.matmul(
                        out=ps[:, sl + j, :], lhsT=wom[:, pw, k, :],
                        rhs=gbuf[:, k, th * H + j * 512:th * H + (j + 1) * 512],
                        start=(k == 0), stop=(k == 15)))(),
                        waits=tuple(wts),
                        fire=ctx.fire(BP, 1) if (k, j) == (15, 1) else None)
            ctx.emit("vector", (lambda it=it, pb=pb, sl=sl: lambda e: e.tensor_add(
                out=abf[:, pb, :], in0=xnt[:, pb, :],
                in1=ps[:, sl:sl + 2, :].rearrange("p a f -> p (a f)")))(),
                waits=tuple([(BP, it + 1), (BX + str(pb), 16 * (it // 2 + 1))]
                            + ([(BS + str(pb), 16 * (it // 2))] if it >= 2 else [])),
                fire=ctx.fire(BE, 2))
            ctx.emit("gpsimd", (lambda m=m, th=th, pb=pb: lambda e: e.dma_start(
                out=attn0[m * 128:(m + 1) * 128, th * H:(th + 1) * H],
                in_=abf[:, pb, :]))(),
                waits=((BE, 2 * it + 2),), fire=ctx.fire(BS + str(pb), 16))

    CL, CP, CA, CV, CS = tg + "cl", tg + "cp", tg + "ca", tg + "cv", tg + "cs"
    w1r = w1.rearrange("(k p) m -> p k m", p=128)
    NA, NV = 18, 24
    for m in range(32):
        pw = m % 2
        ctx.emit("sync", (lambda m=m, pw=pw: lambda e: e.dma_start(
            out=w1m[:, pw, :, :], in_=w1r[:, :, m * 128:(m + 1) * 128]))(),
            waits=tuple([("q", q1)] + ([(CP, 2 * (m - 1))] if m >= 2 else [])),
            fire=ctx.fire(CL + str(pw), 16))
        for th in range(2):
            it = m * 2 + th
            p = it % 2
            va = NV * it
            aa = NA * it
            for k in range(KD):
                for j in range(2):
                    wts = []
                    if k == 0 and j == 0:
                        wts = [(CL + str(pw), 16 * (m // 2 + 1)), ("q", q1)]
                        if it >= 1:
                            wts.append((CV, NV * (it - 1) + 1))
                        else:
                            wts.append((BE, 32))
                    ctx.emit("tensor", (lambda k=k, th=th, j=j, p=p, pw=pw: lambda e: e.matmul(
                        out=ps[:, p * 2 + j, :], lhsT=w1m[:, pw, k, :],
                        rhs=xnb[:, k, th * H + j * 512:th * H + (j + 1) * 512],
                        start=(k == 0), stop=(k == KD - 1)))(),
                        waits=tuple(wts),
                        fire=ctx.fire(CP, 1) if (k, j) == (KD - 1, 1) else None)
            psv = ps[:, p * 2:p * 2 + 2, :].rearrange("p a f -> p (a f)")
            ctx.emit("vector", (lambda m=m, p=p, psv=psv: lambda e: e.tensor_scalar(
                out=s0b[:, p, :], in0=psv, scalar1=b1c[:, m:m + 1],
                scalar2=256.0, op0=ALU.add, op1=ALU.mult))(),
                waits=tuple([(CP, it + 1)] + ([(CV, NV * (it - 1) + 2)] if it >= 1 else [])),
                fire=ctx.fire(CV, 1))
            ctx.emit("vector", (lambda p=p: lambda e: e.tensor_scalar(
                out=qb[:, p, :], in0=s0b[:, p, :], scalar1=RNE_C,
                scalar2=RNE_C, op0=ALU.add, op1=ALU.subtract))(),
                waits=((CV, va + 1),) + (((CV, NV * (it - 1) + NV - 1),) if it >= 1 else ()),
                fire=ctx.fire(CV, 1))
            ctx.emit("scalar", (lambda p=p: lambda e: e.activation(
                out=vb[:, p, :], in_=qb[:, p, :], func=AF.Abs))(),
                waits=((CV, va + 2),) + (((CA, NA * (it - 1) + NA),) if it >= 1 else ()),
                fire=ctx.fire(CA, 1))
            ctx.emit("vector", (lambda p=p: lambda e: e.tensor_single_scalar(
                out=acc[:, p, :], in_=vb[:, p, :], scalar=PWL_S0, op=ALU.mult))(),
                waits=((CA, aa + 1),), fire=ctx.fire(CV, 1))
            for kk, (vk, ck) in enumerate(PWL_KNOTS):
                rp = kk % 2
                w = [(CA, aa + 1)]
                if kk >= 2:
                    w = [(CV, va + 3 + kk - 1)]
                ctx.emit("scalar", (lambda kk=kk, ck=ck, p=p, rp=rp: lambda e: e.activation(
                    out=rr[:, rp, :], in_=vb[:, p, :], func=AF.Relu,
                    scale=abs(ck), bias=knot[:, kk:kk + 1]))(),
                    waits=tuple(w), fire=ctx.fire(CA, 1))
                ctx.emit("vector", (lambda kk=kk, p=p, rp=rp: lambda e: e.scalar_tensor_tensor(
                    out=acc[:, p, :], in0=rr[:, rp, :], scalar=sgn[kk],
                    in1=acc[:, p, :], op0=ALU.mult, op1=ALU.add))(),
                    waits=((CA, aa + 2 + kk), (CV, va + 3 + kk)),
                    fire=ctx.fire(CV, 1))
            ctx.emit("scalar", (lambda p=p: lambda e: e.activation(
                out=rr[:, 0, :], in_=vb[:, p, :], func=AF.Relu,
                bias=knot[:, 16:17]))(),
                waits=((CV, va + 3 + 15),), fire=ctx.fire(CA, 1))
            ctx.emit("vector", (lambda p=p: lambda e: e.tensor_scalar(
                out=rr[:, 0, :], in0=rr[:, 0, :], scalar1=1.0,
                scalar2=PWL_J256, op0=ALU.min, op1=ALU.mult))(),
                waits=((CA, aa + NA),), fire=ctx.fire(CV, 1))
            ctx.emit("vector", (lambda p=p: lambda e: e.tensor_add(
                out=acc[:, p, :], in0=acc[:, p, :], in1=rr[:, 0, :]))(),
                waits=((CV, va + 20),), fire=ctx.fire(CV, 1))
            ctx.emit("vector", (lambda p=p: lambda e: e.scalar_tensor_tensor(
                out=s0b[:, p, :], in0=qb[:, p, :], scalar=0.5,
                in1=acc[:, p, :], op0=ALU.mult, op1=ALU.add))(),
                waits=((CV, va + 21),), fire=ctx.fire(CV, 1))
            ctx.emit("vector", (lambda p=p: lambda e: e.tensor_scalar(
                out=qb[:, p, :], in0=s0b[:, p, :], scalar1=RNE_C,
                scalar2=RNE_C, op0=ALU.add, op1=ALU.subtract))(),
                waits=((CV, va + 22),), fire=ctx.fire(CV, 1))
            ctx.emit("vector", (lambda p=p: lambda e: e.tensor_copy(
                out=hb[:, p, :], in_=qb[:, p, :]))(),
                waits=tuple([(CV, va + 23)]
                            + ([(CS + str(p), 16 * (it // 2))] if it >= 2 else [])),
                fire=ctx.fire(CV, 1))
            ctx.emit("gpsimd", (lambda m=m, th=th, p=p: lambda e: e.dma_start(
                out=h_bf[m * 128:(m + 1) * 128, th * H:(th + 1) * H],
                in_=hb[:, p, :]))(),
                waits=((CV, va + 24),), fire=ctx.fire(CS + str(p), 16))

    DL, DR, DP, DE, DS = tg + "dl", tg + "dr", tg + "dp", tg + "de", tg + "ds"
    w2r = w2.rearrange("(k p) m -> p k m", p=128)

    def rqv(pf, k):
        fl = pf * 32 + k
        return gbuf[:, fl // 4, (fl % 4) * 512:(fl % 4) * 512 + 512]

    def w2v(pw):
        return xnb[:, pw * 2:pw * 2 + 2, :].rearrange(
            "p a t -> p (a t)").rearrange("p (k m) -> p k m", m=128)

    hbr = h_bf.rearrange("(k p) m -> p k m", p=128)
    for f4 in range(4):
        pf = f4 % 2
        w = ([(CS + "0", 16 * 32), (CS + "1", 16 * 32),
              (BS + "0", 16 * 8), (BS + "1", 16 * 8)] if f4 == 0 else [])
        if f4 >= 2:
            w.append((DP, 8 * (f4 - 1)))
        ctx.emit("sync", (lambda f4=f4, pf=pf: lambda e: e.dma_start(
            out=gbuf[:, pf * 8:pf * 8 + 8, :].rearrange("p a t -> p (a t)"),
            in_=hbr[:, :, f4 * 512:(f4 + 1) * 512]))(),
            waits=tuple(w), fire=ctx.fire(DR + str(pf), 16))
        for m in range(KD):
            it = f4 * KD + m
            pw = it % 2
            w = ([(CS + "0", 16 * 32), (CS + "1", 16 * 32)] if it < 2 else
                 [(DP, it - 1)])
            ctx.emit("scalar", (lambda it=it, m=m, pw=pw: lambda e: e.dma_start(
                out=w2v(pw), in_=w2r[:, :, m * 128:(m + 1) * 128]))(),
                waits=tuple(w), fire=ctx.fire(DL + "w" + str(pw), 16))
            ctx.emit("sync", (lambda it=it, m=m, f4=f4, pw=pw: lambda e: e.dma_start(
                out=fx[:, pw, :],
                in_=x[m * 128:(m + 1) * 128, f4 * 512:(f4 + 1) * 512]))(),
                waits=((DE, 4 * (it - 1)),) if it >= 2 else (),
                fire=ctx.fire(DL + "x" + str(pw), 16))
            ctx.emit("sync", (lambda it=it, m=m, f4=f4, pw=pw: lambda e: e.dma_start(
                out=fa[:, pw, :],
                in_=attn0[m * 128:(m + 1) * 128, f4 * 512:(f4 + 1) * 512]))(),
                waits=((DE, 4 * (it - 1)),) if it >= 2 else (),
                fire=ctx.fire(DL + "a" + str(pw), 16))
            for k in range(32):
                wts = []
                if k == 0:
                    wts = [(DL + "w" + str(pw), 16 * (it // 2 + 1)),
                           (DR + str(pf), 16 * (f4 // 2 + 1))]
                    if it >= 2:
                        wts.append((DE, 4 * (it - 1) + 1))
                ctx.emit("tensor", (lambda k=k, pw=pw, pf=pf, it=it: lambda e: e.matmul(
                    out=ps[:, it % 4, :], lhsT=w2v(pw)[:, k, :],
                    rhs=rqv(pf, k), start=(k == 0), stop=(k == 31)))(),
                    waits=tuple(wts),
                    fire=ctx.fire(DP, 1) if k == 31 else None)
            ctx.emit("vector", (lambda it=it, m=m, pw=pw: lambda e: e.tensor_scalar(
                out=fo[:, pw, :], in0=ps[:, it % 4, :], scalar1=b2c[:, m:m + 1],
                scalar2=RNE_C, op0=ALU.add, op1=ALU.add))(),
                waits=tuple([(DP, it + 1),
                             (DL + "x" + str(pw), 16 * (it // 2 + 1)),
                             (DL + "a" + str(pw), 16 * (it // 2 + 1))]
                            + ([(DS + str(pw), 16 * (it // 2))] if it >= 2 else [])),
                fire=ctx.fire(DE, 1))
            ctx.emit("vector", (lambda pw=pw: lambda e: e.tensor_scalar(
                out=fo[:, pw, :], in0=fo[:, pw, :], scalar1=RNE_C,
                scalar2=1.0 / 256.0, op0=ALU.subtract, op1=ALU.mult))(),
                waits=((DE, 4 * it + 1),), fire=ctx.fire(DE, 1))
            ctx.emit("vector", (lambda pw=pw: lambda e: e.tensor_add(
                out=fx[:, pw, :], in0=fx[:, pw, :], in1=fa[:, pw, :]))(),
                waits=((DE, 4 * it + 1),), fire=ctx.fire(DE, 1))
            ctx.emit("vector", (lambda pw=pw: lambda e: e.tensor_add(
                out=fo[:, pw, :], in0=fo[:, pw, :], in1=fx[:, pw, :]))(),
                waits=((DE, 4 * it + 3),),
                fire=ctx.fire(DE, 1))
            ctx.emit("gpsimd", (lambda it=it, m=m, f4=f4, pw=pw: lambda e: e.dma_start(
                out=x1[m * 128:(m + 1) * 128, f4 * 512:(f4 + 1) * 512],
                in_=fo[:, pw, :]))(),
                waits=((DE, 4 * it + 4),), fire=ctx.fire(DS + str(pw), 16))
    ctx.emit("vector", lambda e: e.engine_nop(),
             waits=((DS + "0", 16 * 16), (DS + "1", 16 * 16)),
             fire=ctx.fire("q", 1))


def emit_lxa2(nc, ctx, S, io, layer, stop=None):
    KD = D // 128
    H = T // 2
    NST = 64 if layer == 0 else 16
    MX = 192 if layer == 0 else 96
    tg = "ax_"

    x = io["x"]
    win = io["win_bf"]
    convw, convb = io["convw"], io["convb"]
    wxp, wdt, bdt = io["wxp_bf"], io["wdt"], io["bdt"]
    if layer == 0:
        lnw, lnb, xn_f = io["lnw"], io["lnb"], io["xn_f"]
    xn_bf, xi_bf, z_bf = io["xn_bf"], io["xi_bf"], io["z_bf"]
    u_f, u_bf = io["u_f"], io["u_bf"]
    B0, C0, delta, du, du_f16 = io["B0"], io["C0"], io["delta"], io["du"], io["du_f16"]

    xt = ctx.sbuf("xt", [128, 2, H], F32)
    sq = ctx.sbuf("sq", [128, 2, H], F32)
    ones_col = ctx.sbuf("ones_col", [128, 1], F32)
    ones_row = ctx.sbuf("ones_row", [1, 128], F32)
    rows = ctx.sbuf("rows", [1, 4, T], F32)
    rep_a = ctx.sbuf("rep_a", [128, T], F32)
    rep_b = ctx.sbuf("rep_b", [128, T], F32)
    xbf = ctx.sbuf("xbf", [128, 2, H], BF16)
    xn2 = ctx.sbuf("xn2", [128, KD, T], BF16)
    wm = ctx.sbuf("wm", [128, 2, KD, 128], BF16)
    ebf = ctx.sbuf("ebf", [128, 2, H], BF16)
    ct = ctx.sbuf("ct", [128, 2, T + 3], BF16)
    cwr = ctx.sbuf("cwr", [128, 16, 4], F32 if layer == 0 else BF16)
    cbr = ctx.sbuf("cbr", [128, 16], F32 if layer == 0 else BF16)
    cw = ctx.sbuf("cw", [128, 16, 4], F32)
    cb = ctx.sbuf("cb", [128, 16], F32)
    ua = ctx.sbuf("ua", [128, 2, T], F32)
    ub = ctx.sbuf("ub", [128, 2, T], BF16)
    wxpt = ctx.sbuf("wxpt", [128, 16, MX], BF16)
    uh = ctx.sbuf("uh", [128, 2, H], BF16)
    dtr = ctx.sbuf("dtr", [64, T], BF16)
    wdtf = ctx.sbuf("wdtf", [64, DI], F32 if layer == 0 else BF16)
    wdtb = ctx.sbuf("wdtb", [64, DI], BF16)
    bdtr = ctx.sbuf("bdtr", [128, 16], F32 if layer == 0 else BF16)
    bdtc = ctx.sbuf("bdtc", [128, 16], F32)
    tb = ctx.sbuf("tb", [128, 2, 4, H], F32)
    duh = ctx.sbuf("duh", [128, 2, H], F16)
    lncol = ctx.sbuf("lncol", [128, 2, KD], F32)
    ps_big, ps_sm = ctx.psA, ctx.psB[0:64]
    MU, SQS, RSTD, SCR = range(4)

    S.op("vector", lambda e: e.memset(ones_col[:], 1.0))
    S.op("vector", lambda e: e.memset(ones_row[:], 1.0))
    cst = ConstCols(S, [1e-5])
    if layer == 0:
        S.dma(lncol[:, 0, :], lnw.rearrange("(k p) -> p k", p=128), slow=True)
        S.dma(lncol[:, 1, :], lnb.rearrange("(k p) -> p k", p=128), slow=True)
    S.dma(cwr[:], convw.rearrange("(g p) k -> p g k", p=128), slow=True)
    S.dma(cbr[:], convb.rearrange("(g p) -> p g", p=128), slow=True)
    S.op("vector", lambda e: e.tensor_copy(out=cw[:], in_=cwr[:]))
    S.op("vector", lambda e: e.tensor_copy(out=cb[:], in_=cbr[:]))
    S.op("vector", lambda e: e.memset(ct[:, 0, 0:3], 0.0))
    S.op("vector", lambda e: e.memset(ct[:, 1, 0:3], 0.0))
    wxpr = wxp.rearrange("(k p) m -> p k m", p=128)
    if MX == 192:
        S.dma(wxpt[:], wxpr)
    else:
        S.dma(wxpt[:, :, 0:64], wxpr[:, :, 0:64])
        S.dma(wxpt[:, :, 64:80], wxpr[:, :, 80:96])
        S.dma(wxpt[:, :, 80:96], wxpr[:, :, 64:80])
    S.dma(wdtf[:], wdt)
    S.op("vector", lambda e: e.tensor_copy(out=wdtb[:], in_=wdtf[:]))
    S.dma(bdtr[:], bdt.rearrange("(g p) -> p g", p=128), slow=True)
    S.op("vector", lambda e: e.tensor_copy(out=bdtc[:], in_=bdtr[:]))
    q0 = ctx.val("q")

    tok_xnf = {}
    tok_xnbf = {}
    tok_s6 = {}
    tok_uf = {}
    tok_ub = {}

    def stats_stage(st, src_ap, xn_mode):
        A, P, E, V = st + "a", st + "p", st + "e", st + "v"
        dsl = DStream(ctx, st + "L")
        dsf = DStream(ctx, st + "F")
        dsb = DStream(ctx, st + "B")
        nv = 3 if xn_mode else 0
        do_stats = (not xn_mode) or layer == 0
        qs = ctx.val("q")
        for th in range(2):
            for k in range(KD):
                i = th * KD + k
                p = i % 2
                w = [("q", qs)]
                if i >= 2:
                    w = [(V, nv * (i - 1))] if xn_mode else [(P, i - 1), (A, i - 1)]
                    if do_stats and xn_mode:
                        w += [(P, i - 1), (A, i - 1)]
                    if xn_mode:
                        w.append(dsb.tok(p))
                        if layer == 0:
                            w.append(dsf.tok(p))
                lt = dsl.emit("sync", p, (lambda k=k, th=th, p=p: lambda e: e.dma_start(
                    out=xt[:, p, :],
                    in_=src_ap[k * 128:(k + 1) * 128, th * H:(th + 1) * H]))(),
                    waits=tuple(w))
                if xn_mode:
                    ctx.emit("vector", (lambda th=th, p=p: lambda e: e.tensor_sub(
                        out=xt[:, p, :], in0=xt[:, p, :],
                        in1=rep_a[:, th * H:(th + 1) * H]))(),
                        waits=(lt, ("q", qs)), fire=ctx.fire(V, 1))
                    ctx.emit("vector", (lambda th=th, p=p: lambda e: e.tensor_mul(
                        out=xt[:, p, :], in0=xt[:, p, :],
                        in1=rep_b[:, th * H:(th + 1) * H]))(),
                        waits=((V, nv * i + 1),), fire=ctx.fire(V, 1))
                    if layer == 0:
                        tok_xnf[i] = dsf.emit("gpsimd", p, (lambda k=k, th=th, p=p: lambda e: e.dma_start(
                            out=xn_f[k * 128:(k + 1) * 128, th * H:(th + 1) * H],
                            in_=xt[:, p, :]))(),
                            waits=((V, nv * i + 2),))
                    ctx.emit("vector", (lambda p=p: lambda e: e.tensor_copy(
                        out=xbf[:, p, :], in_=xt[:, p, :]))(),
                        waits=(dsb.tok(p), (V, nv * i + 2)), fire=ctx.fire(V, 1))
                    tok_xnbf[i] = dsb.emit("gpsimd", p, (lambda k=k, th=th, p=p: lambda e: e.dma_start(
                        out=xn_bf[k * 128:(k + 1) * 128, th * H:(th + 1) * H],
                        in_=xbf[:, p, :]))(),
                        waits=((V, nv * i + 3),))
                if do_stats:
                    ctx.emit("scalar", (lambda p=p: lambda e: e.activation(
                        out=sq[:, p, :], in_=xt[:, p, :], func=AF.Square))(),
                        waits=(lt, (P, i - 1)) if not xn_mode
                        else ((V, nv * i + 2), (P, i - 1)),
                        fire=ctx.fire(A, 1))
                    for which in range(2):
                        stt = xt if which == 0 else sq
                        for fc in range(2):
                            wts = []
                            if which == 0 and fc == 0:
                                wts = [(A, i + 1), ("q", qs), lt]
                                if th == 1 and k == 0:
                                    wts.append((E, 4))
                            ctx.emit("tensor", (lambda w2=which, fc=fc, k=k, p=p, stt=stt: lambda e: e.matmul(
                                out=ps_sm[0:1, w2 * 2 + fc, :],
                                lhsT=ones_col[:],
                                rhs=stt[:, p, fc * 512:(fc + 1) * 512],
                                start=(k == 0), stop=(k == KD - 1)))(),
                                waits=tuple(wts),
                                fire=ctx.fire(P, 1) if (which, fc) == (1, 1) else None)
            if do_stats:
                for which, slot in ((0, MU), (1, SQS)):
                    for fc in range(2):
                        ctx.emit("scalar", (lambda w2=which, s=slot, fc=fc, th=th: lambda e: e.activation(
                            out=rows[:, s, th * H + fc * 512:th * H + (fc + 1) * 512],
                            in_=ps_sm[0:1, w2 * 2 + fc, :],
                            func=AF.Identity, scale=1.0 / D))(),
                            waits=((P, 8 * (th + 1)),), fire=ctx.fire(E, 1))
        joins = [(E, 8)] if do_stats else []
        if xn_mode:
            joins += [dsb.tok(0), dsb.tok(1)]
            if layer == 0:
                joins += [dsf.tok(0), dsf.tok(1)]
        return joins

    def row_math(pre):
        S.opw("vector", lambda e: e.tensor_mul(
            out=rows[:, SCR, :], in0=rows[:, MU, :], in1=rows[:, MU, :]), pre)
        S.op("vector", lambda e: e.tensor_sub(
            out=rows[:, SCR, :], in0=rows[:, SQS, :], in1=rows[:, SCR, :]))
        S.op("scalar", lambda e: e.activation(
            out=rows[:, SCR, :], in_=rows[:, SCR, :], func=AF.Sqrt,
            bias=cst.ap(1e-5)[0:1, :]))
        S.op("vector", lambda e: e.reciprocal(
            out=rows[:, RSTD, :], in_=rows[:, SCR, :]))

    def rep_pass():
        for slot, dst in ((MU, rep_a), (RSTD, rep_b)):
            for fc in range(4):
                S.op("tensor", (lambda s=slot, fc=fc: lambda e: e.matmul(
                    out=ps_big[:, fc, :], lhsT=ones_row[:],
                    rhs=rows[:, s, fc * 512:(fc + 1) * 512],
                    start=True, stop=True))())
            S.op("vector", (lambda d=dst: lambda e: e.tensor_copy(
                out=d[:], in_=ps_big[:].rearrange("p a f -> p (a f)")))())

    j1 = stats_stage(tg + "s1", x, False)
    if stop == "s1":
        ctx.emit("vector", lambda e: e.engine_nop(), waits=tuple(j1),
                 fire=ctx.fire("q", 1))
        return
    row_math(j1)
    rep_pass()
    j2 = stats_stage(tg + "s2", x, True)
    if stop == "s2":
        ctx.emit("vector", lambda e: e.engine_nop(), waits=tuple(j2),
                 fire=ctx.fire("q", 1))
        return
    if layer == 0:
        row_math(j2)
        rep_pass()
        X5V, X5A = tg + "x5v", tg + "x5a"
        ds5 = DStream(ctx, tg + "L5")
        q5 = ctx.val("q")
        for th in range(2):
            for k in range(KD):
                i = th * KD + k
                p = i % 2
                w = [("q", q5)] if i < 2 else [(X5A, i - 1)]
                lt = ds5.emit("sync", p, (lambda k=k, th=th, p=p: lambda e: e.dma_start(
                    out=xt[:, p, :],
                    in_=xn_f[k * 128:(k + 1) * 128, th * H:(th + 1) * H]))(),
                    waits=tuple(w))
                ctx.emit("vector", (lambda th=th, p=p: lambda e: e.tensor_sub(
                    out=xt[:, p, :], in0=xt[:, p, :],
                    in1=rep_a[:, th * H:(th + 1) * H]))(),
                    waits=(lt, ("q", q5)), fire=ctx.fire(X5V, 1))
                ctx.emit("vector", (lambda th=th, p=p: lambda e: e.tensor_mul(
                    out=xt[:, p, :], in0=xt[:, p, :],
                    in1=rep_b[:, th * H:(th + 1) * H]))(),
                    waits=((X5V, 2 * i + 1),), fire=ctx.fire(X5V, 1))
                ctx.emit("scalar", (lambda k=k, th=th, p=p: lambda e: e.activation(
                    out=xn2[:, k, th * H:(th + 1) * H], in_=xt[:, p, :],
                    func=AF.Identity, scale=lncol[:, 0, k:k + 1],
                    bias=lncol[:, 1, k:k + 1]))(),
                    waits=((X5V, 2 * i + 2),), fire=ctx.fire(X5A, 1))
        xn2_done = [(X5A, 16)]
    else:
        ds5 = DStream(ctx, tg + "L5", npar=8)
        t5 = []
        for k in range(KD):
            t5.append(ds5.emit("sync", k, (lambda k=k: lambda e: e.dma_start(
                out=xn2[:, k, :], in_=xn_bf[k * 128:(k + 1) * 128, :]))(),
                waits=tuple([tok_xnbf[k], tok_xnbf[k + 8]])))
        xn2_done = t5

    if stop == "x5":
        ctx.emit("vector", lambda e: e.engine_nop(), waits=tuple(xn2_done),
                 fire=ctx.fire("q", 1))
        return
    P6, E6 = tg + "p6", tg + "e6"
    dsw6 = DStream(ctx, tg + "W6")
    ds6 = DStream(ctx, tg + "S6")
    winr = win.rearrange("(k p) m -> p k m", p=128)
    for m in range(32):
        pw = m % 2
        w = [(P6, 2 * (m - 1))] if m >= 2 else []
        wt = dsw6.emit("sync", pw, (lambda m=m, pw=pw: lambda e: e.dma_start(
            out=wm[:, pw, :, :], in_=winr[:, :, m * 128:(m + 1) * 128]))(),
            waits=tuple(w))
        for th in range(2):
            it = m * 2 + th
            pb = it % 2
            for k in range(KD):
                for j in range(2):
                    wts = []
                    if k == 0 and j == 0:
                        wts = [wt] + list(xn2_done)
                        if it >= 2:
                            wts.append((E6, it - 1))
                    ctx.emit("tensor", (lambda k=k, th=th, j=j, pb=pb, pw=pw: lambda e: e.matmul(
                        out=ps_big[:, pb * 2 + j, :], lhsT=wm[:, pw, k, :],
                        rhs=xn2[:, k, th * H + j * 512:th * H + (j + 1) * 512],
                        start=(k == 0), stop=(k == KD - 1)))(),
                        waits=tuple(wts),
                        fire=ctx.fire(P6, 1) if (k, j) == (KD - 1, 1) else None)
            ctx.emit("vector", (lambda it=it, pb=pb: lambda e: e.tensor_copy(
                out=ebf[:, pb, :],
                in_=ps_big[:, pb * 2:pb * 2 + 2, :].rearrange("p a f -> p (a f)")))(),
                waits=tuple([(P6, it + 1), ds6.tok(pb)]),
                fire=ctx.fire(E6, 1))
            dst = xi_bf if m < 16 else z_bf
            mm2 = m if m < 16 else m - 16
            tok_s6[(mm2, th, m < 16)] = ds6.emit(
                "gpsimd", pb, (lambda mm2=mm2, th=th, pb=pb, dst=dst: lambda e: e.dma_start(
                    out=dst[mm2 * 128:(mm2 + 1) * 128, th * H:(th + 1) * H],
                    in_=ebf[:, pb, :]))(),
                waits=((E6, it + 1),))

    if stop == "inproj":
        ctx.emit("vector", lambda e: e.engine_nop(),
                 waits=(ds6.tok(0), ds6.tok(1)), fire=ctx.fire("q", 1))
        return
    V7, A7 = tg + "v7", tg + "a7"
    NV7 = 5 if layer == 0 else 6
    dsct = DStream(ctx, tg + "CT")
    dsuf = DStream(ctx, tg + "UF")
    dsub = DStream(ctx, tg + "UB")
    for g in range(16):
        p = g % 2
        w = [tok_s6[(g, 0, True)], tok_s6[(g, 1, True)], ("q", q0)]
        if g >= 2:
            w.append((V7, NV7 * (g - 1)))
        lt = dsct.emit("sync", p, (lambda g=g, p=p: lambda e: e.dma_start(
            out=ct[:, p, 3:T + 3], in_=xi_bf[g * 128:(g + 1) * 128, :]))(),
            waits=tuple(w))
        ctx.emit("vector", (lambda g=g, p=p: lambda e: e.tensor_scalar(
            out=ua[:, p, :], in0=ct[:, p, 0:T], scalar1=cw[:, g, 0:1],
            scalar2=None, op0=ALU.mult))(),
            waits=tuple([lt, ("q", q0), dsuf.tok(p)]),
            fire=ctx.fire(V7, 1))
        for kk in range(1, 4):
            ctx.emit("vector", (lambda g=g, kk=kk, p=p: lambda e: e.scalar_tensor_tensor(
                out=ua[:, p, :], in0=ct[:, p, kk:T + kk], scalar=cw[:, g, kk:kk + 1],
                in1=ua[:, p, :], op0=ALU.mult, op1=ALU.add))(),
                waits=((V7, NV7 * g + kk),), fire=ctx.fire(V7, 1))
        ctx.emit("scalar", (lambda g=g, p=p: lambda e: e.activation(
            out=ua[:, p, :], in_=ua[:, p, :], func=AF.Silu, bias=cb[:, g:g + 1]))(),
            waits=((V7, NV7 * g + 4),), fire=ctx.fire(A7, 1))
        ctx.emit("vector", (lambda p=p: lambda e: e.tensor_copy(
            out=ub[:, p, :], in_=ua[:, p, :]))(),
            waits=((A7, g + 1), dsub.tok(p)), fire=ctx.fire(V7, 1))
        if layer == 1:
            ctx.emit("vector", (lambda p=p: lambda e: e.tensor_copy(
                out=ua[:, p, :], in_=ub[:, p, :]))(),
                waits=((V7, NV7 * g + 5),), fire=ctx.fire(V7, 1))
        tok_uf[g] = dsuf.emit("gpsimd", p, (lambda g=g, p=p: lambda e: e.dma_start(
            out=u_f[g * 128:(g + 1) * 128, :], in_=ua[:, p, :]))(),
            waits=((V7, NV7 * (g + 1)),))
        tok_ub[g] = dsub.emit("gpsimd", p, (lambda g=g, p=p: lambda e: e.dma_start(
            out=u_bf[g * 128:(g + 1) * 128, :], in_=ub[:, p, :]))(),
            waits=((V7, NV7 * (g + 1)),))

    if stop == "conv":
        ctx.emit("vector", lambda e: e.engine_nop(),
                 waits=(dsuf.tok(0), dsuf.tok(1), dsub.tok(0), dsub.tok(1)),
                 fire=ctx.fire("q", 1))
        return
    P8, E8 = tg + "p8", tg + "e8"
    dsu8 = DStream(ctx, tg + "U8")
    ds8 = DStream(ctx, tg + "S8")
    psc2 = ctx.psB[0:64]
    for th in range(2):
        for k in range(16):
            i8 = th * 16 + k
            p = i8 % 2
            w = [tok_ub[k]]
            if i8 >= 2:
                w.append((P8, i8 - 1))
            if th == 1 and k == 0:
                w.append((E8, 3))
            lt = dsu8.emit("sync", p, (lambda k=k, th=th, p=p: lambda e: e.dma_start(
                out=uh[:, p, :],
                in_=u_bf[k * 128:(k + 1) * 128, th * H:(th + 1) * H]))(),
                waits=tuple(w))
            for j in range(2):
                wts = []
                if j == 0:
                    wts = [lt, ("q", q0)]
                    if k == 0 and th == 1:
                        wts.append((E8, 3))
                if MX == 192:
                    mms = [(ps_big[:, 2 * th + j, :], wxpt[:, k, 0:128]),
                           (ps_sm[:, 2 * th + j, :], wxpt[:, k, 128:192])]
                else:
                    mms = [(ps_big[0:80, 2 * th + j, :], wxpt[:, k, 0:80]),
                           (ps_sm[0:16, 2 * th + j, :], wxpt[:, k, 80:96])]
                for mi, (pso, lh) in enumerate(mms):
                    ctx.emit("tensor", (lambda pso=pso, lh=lh, k=k, p=p, j=j: lambda e: e.matmul(
                        out=pso, lhsT=lh,
                        rhs=uh[:, p, j * 512:(j + 1) * 512],
                        start=(k == 0), stop=(k == 15)))(),
                        waits=tuple(wts if mi == 0 else []),
                        fire=ctx.fire(P8, 1) if (j == 1 and mi == len(mms) - 1) else None)
        ev = [(P8, 16 * (th + 1))]
        ctx.emit("vector", (lambda th=th: lambda e: e.tensor_copy(
            out=dtr[:, th * H:(th + 1) * H],
            in_=ps_big[0:64, 2 * th:2 * th + 2, :].rearrange("p a f -> p (a f)")))(),
            waits=tuple(ev), fire=ctx.fire(E8, 1))
        if MX == 192:
            ctx.emit("vector", (lambda th=th: lambda e: e.tensor_copy(
                out=tb[64:128, 0, 0, :], in_=ps_big[64:128, 2 * th:2 * th + 2, :]
                .rearrange("p a f -> p (a f)")))(),
                waits=(ds8.tok(0),), fire=ctx.fire(E8, 1))
            tk = ds8.emit("gpsimd", 0, (lambda th=th: lambda e: e.dma_start(
                out=B0[:, th * H:(th + 1) * H], in_=tb[64:128, 0, 0, :]))(),
                waits=((E8, 3 * th + 2),))
            ctx.emit("vector", (lambda th=th: lambda e: e.tensor_copy(
                out=tb[0:64, 0, 1, :], in_=psc2[:, 2 * th:2 * th + 2, :]
                .rearrange("p a f -> p (a f)")))(),
                waits=(ds8.tok(1),), fire=ctx.fire(E8, 1))
            tk = ds8.emit("gpsimd", 1, (lambda th=th: lambda e: e.dma_start(
                out=C0[:, th * H:(th + 1) * H], in_=tb[0:64, 0, 1, :]))(),
                waits=((E8, 3 * th + 3),))
        else:
            ctx.emit("vector", (lambda th=th: lambda e: e.tensor_copy(
                out=tb[0:16, 0, 0, :], in_=psc2[0:16, 2 * th:2 * th + 2, :]
                .rearrange("p a f -> p (a f)")))(),
                waits=(ds8.tok(0),), fire=ctx.fire(E8, 1))
            tk = ds8.emit("gpsimd", 0, (lambda th=th: lambda e: e.dma_start(
                out=B0[:, th * H:(th + 1) * H], in_=tb[0:16, 0, 0, :]))(),
                waits=((E8, 3 * th + 2),))
            ctx.emit("vector", (lambda th=th: lambda e: e.tensor_copy(
                out=tb[64:80, 0, 1, :], in_=ps_big[64:80, 2 * th:2 * th + 2, :]
                .rearrange("p a f -> p (a f)")))(),
                waits=(ds8.tok(1),), fire=ctx.fire(E8, 1))
            tk = ds8.emit("gpsimd", 1, (lambda th=th: lambda e: e.dma_start(
                out=C0[:, th * H:(th + 1) * H], in_=tb[64:80, 0, 1, :]))(),
                waits=((E8, 3 * th + 3),))
    bc_done = [ds8.tok(0), ds8.tok(1)]

    if stop == "xdbl":
        ctx.emit("vector", lambda e: e.engine_nop(),
                 waits=tuple(bc_done), fire=ctx.fire("q", 1))
        return
    P9, A9, V9 = tg + "p9", tg + "a9", tg + "v9"
    NV9 = 5
    dsu9 = DStream(ctx, tg + "U9")
    ds9 = DStream(ctx, tg + "S9")
    for m in range(16):
        for th in range(2):
            it = m * 2 + th
            p = it % 2
            for j in range(2):
                wts = []
                if j == 0:
                    wts = [(E8, 6), ("q", q0)] + (bc_done if it < 2 else [])
                    if it >= 2:
                        wts.append((V9, NV9 * (it - 1) + 1))
                        wts.append((A9, 3 * (it - 1)))
                ctx.emit("tensor", (lambda m=m, th=th, j=j, p=p: lambda e: e.matmul(
                    out=ps_big[:, p * 2 + j, :], lhsT=wdtb[:, m * 128:(m + 1) * 128],
                    rhs=dtr[:, th * H + j * 512:th * H + (j + 1) * 512],
                    start=True, stop=True))(),
                    waits=tuple(wts),
                    fire=ctx.fire(P9, 1) if j == 1 else None)
            psv = ps_big[:, p * 2:p * 2 + 2, :].rearrange("p a f -> p (a f)")
            va = NV9 * it
            aa = 3 * it
            ctx.emit("vector", (lambda m=m, p=p, psv=psv: lambda e: e.tensor_scalar(
                out=tb[:, p, 1, :], in0=psv, scalar1=bdtc[:, m:m + 1],
                scalar2=None, op0=ALU.add))(),
                waits=tuple([(P9, it + 1)]
                            + (bc_done if it < 2 else [])),
                fire=ctx.fire(V9, 1))
            ctx.emit("vector", (lambda m=m, p=p, psv=psv: lambda e: e.tensor_scalar(
                out=tb[:, p, 0, :], in0=psv, scalar1=bdtc[:, m:m + 1],
                scalar2=0.0, op0=ALU.add, op1=ALU.max))(),
                waits=tuple((bc_done if it < 2 else [ds9.tok(p)])),
                fire=ctx.fire(V9, 1))
            ctx.emit("scalar", (lambda p=p: lambda e: e.activation(
                out=tb[:, p, 1, :], in_=tb[:, p, 1, :], func=AF.Abs))(),
                waits=tuple([(V9, va + 1)] + ([ds9.tok(p)] if it >= 2 else [])),
                fire=ctx.fire(A9, 1))
            ctx.emit("scalar", (lambda p=p: lambda e: e.activation(
                out=tb[:, p, 2, :], in_=tb[:, p, 1, :], func=AF.Exp, scale=-1.0))(),
                waits=((A9, aa + 1),), fire=ctx.fire(A9, 1))
            ctx.emit("scalar", (lambda p=p: lambda e: e.activation(
                out=tb[:, p, 1, :], in_=tb[:, p, 2, :], func=AF.Ln, bias=1.0))(),
                waits=((A9, aa + 2),), fire=ctx.fire(A9, 1))
            ut = dsu9.emit("sync", p, (lambda m=m, th=th, p=p: lambda e: e.dma_start(
                out=tb[:, p, 3, :],
                in_=u_f[m * 128:(m + 1) * 128, th * H:(th + 1) * H]))(),
                waits=tuple([tok_uf[m]]
                            + ([(V9, NV9 * (it - 1) + 4)] if it >= 2 else [])))
            ctx.emit("vector", (lambda p=p: lambda e: e.tensor_add(
                out=tb[:, p, 2, :], in0=tb[:, p, 0, :], in1=tb[:, p, 1, :]))(),
                waits=((A9, aa + 3), (V9, va + 2)), fire=ctx.fire(V9, 1))
            ctx.emit("vector", (lambda p=p: lambda e: e.tensor_mul(
                out=tb[:, p, 0, :], in0=tb[:, p, 2, :], in1=tb[:, p, 3, :]))(),
                waits=tuple([(V9, va + 3), ut] + ([ds9.tok(p)] if it >= 2 else [])),
                fire=ctx.fire(V9, 1))
            ctx.emit("vector", (lambda p=p: lambda e: e.tensor_copy(
                out=duh[:, p, :], in_=tb[:, p, 0, :]))(),
                waits=((V9, va + 4),), fire=ctx.fire(V9, 1))
            ds9.emit("gpsimd", p, (lambda m=m, th=th, p=p: lambda e: e.dma_start(
                out=delta[m * 128:(m + 1) * 128, th * H:(th + 1) * H],
                in_=tb[:, p, 2, :]))(),
                waits=((V9, va + 4),))
            ds9.emit("gpsimd", p, (lambda m=m, th=th, p=p: lambda e: e.dma_start(
                out=du[m * 128:(m + 1) * 128, th * H:(th + 1) * H],
                in_=tb[:, p, 0, :]))(),
                waits=((V9, va + 5),))
            ds9.emit("gpsimd", p, (lambda m=m, th=th, p=p: lambda e: e.dma_start(
                out=du_f16[m * 128:(m + 1) * 128, th * H:(th + 1) * H],
                in_=duh[:, p, :]))(),
                waits=((V9, va + 5),))
    joins = [ds9.tok(0), ds9.tok(1)] + bc_done
    joins += [dsuf.tok(0), dsuf.tok(1), dsub.tok(0), dsub.tok(1)]
    joins += [ds6.tok(0), ds6.tok(1)]
    joins += j2[-4:] if layer == 0 else j2
    ctx.emit("vector", lambda e: e.engine_nop(),
             waits=tuple(joins), fire=ctx.fire("q", 1))


